# version 01 (final)
# baseline (speedup 1.0000x reference)
import contextlib
import numpy as np
import ml_dtypes
import concourse.bass as bass
import concourse.mybir as mybir
from concourse.bass_utils import run_bass_kernel_spmd

F32 = mybir.dt.float32
BF16 = mybir.dt.bfloat16
FP16 = mybir.dt.float16
AF = mybir.ActivationFunctionType
ALU = mybir.AluOpType
AX = mybir.AxisListType.X
MAGIC = float(np.float32(1.5 * 2 ** 23))

D = 2048
TC = 1024
NH = 16
NCH = 8
NDC = 16
ISL = 1024
NB = 8

WG_SZ = D * ISL
WQIN_SZ = 3 * WG_SZ
ST_SZ = 128 * D + 32
MOD_SZ = 2 * 12 * 128 + 16

DEBUG = False
STOP_AFTER = None
SAFE = False


def build(dbg=False):
    nc = bass.Bass("TRN2", target_bir_lowering=False)

    def IN(n, s):
        return nc.dram_tensor(n, s, F32, kind="ExternalInput").ap()

    xT = IN("xT", [D, TC])
    Wq = IN("Wq", [D, D]); Wk = IN("Wk", [D, D]); Wv = IN("Wv", [D, D])
    bq = IN("bq", [1, D]); bk = IN("bk", [1, D]); bv = IN("bv", [1, D])
    Wab = IN("Wab", [D, 32]); bab = IN("bab", [1, 32])
    wmod_sl = IN("wmod_sl", [D, 1536])
    cT32 = IN("cT32", [128, 32])
    wgT_sl = IN("wgT_sl", [D, ISL])
    wuT_sl = IN("wuT_sl", [D, ISL])
    wdT_sl = IN("wdT_sl", [ISL, D])
    ident = IN("ident", [128, 128])
    ones_full = IN("ones_full", [128, 128])
    triu = IN("triu", [128, 128])
    m0c = IN("m0c", [128, 512])
    e127 = IN("e127", [128, 1])
    bsel = IN("bsel", [128, 2])
    selT = IN("selT", [8, 8])
    amask = IN("amask", [8, 1])
    posisel = IN("posisel", [16, 2048])
    pos8 = IN("pos8", [8, 1024])
    bm16 = IN("bm16", [8, 128])

    out = nc.dram_tensor("out", [D, TC], F32, kind="ExternalOutput").ap()
    dbg_out = {}
    if dbg:
        for n, s in [("d_mbg", [128, 192]), ("d_ga", [128, 32]), ("d_ysb", [128, 512]),
                     ("d_dm", [128, 512]), ("d_eD", [128, 512]),
                     ("d_mymod", [128, 96]), ("d_la", [128, 128]),
                     ("d_lnb", [128, 128]), ("d_cla", [128, 128]),
                     ("d_wfac", [128, 128]), ("d_mrow", [1, 1024]),
                     ("d_istd", [1, 1024]), ("d_scal", [128, 16]),
                     ("d_coefb", [128, 128]), ("d_mx", [128, 8]),
                     ("d_x1", [D, TC]), ("d_xq", [D, TC]),
                     ("d_h", [D, TC]), ("d_state", [128, D]),
                     ("d_td", [128, 128]), ("d_cd", [128, 128]),
                     ("d_f4", [8, 128]), ("d_s2", [8, 128])]:
            dbg_out[n] = nc.dram_tensor(n, s, F32, kind="ExternalOutput").ap()
        for n, s in [("d_q", [NH * 128, TC]), ("d_k", [NCH * 128, D]),
                     ("d_v", [NCH * 128, D]), ("d_wbig", [128, 2048]),
                     ("d_hbf", [128, 2048]), ("d_at", [128, 512])]:
            dbg_out[n] = nc.dram_tensor(n, s, BF16, kind="ExternalOutput").ap()

    cc_mod_in = nc.dram_tensor("cc_mod_in", [MOD_SZ], F32, kind="Internal")
    cc_mod_out = nc.dram_tensor("cc_mod_out", [NB * MOD_SZ], F32, kind="Internal")
    cc_wq_in = nc.dram_tensor("cc_wq_in", [WQIN_SZ], BF16, kind="Internal")
    cc_wq_out = nc.dram_tensor("cc_wq_out", [NB * WQIN_SZ], BF16, kind="Internal")
    cc_st_in = nc.dram_tensor("cc_st_in", [ST_SZ], F32, kind="Internal")
    cc_st_out = nc.dram_tensor("cc_st_out", [NB * ST_SZ], F32, kind="Internal")
    q_sp = nc.dram_tensor("q_sp", [NH, 128, TC], BF16, kind="Internal")
    k_sp = nc.dram_tensor("k_sp", [NCH, 128, D], BF16, kind="Internal")
    v_sp = nc.dram_tensor("v_sp", [NCH, 128, D], BF16, kind="Internal")
    x_sp = nc.dram_tensor("x_sp", [D, TC], F32, kind="Internal")

    es = contextlib.ExitStack()
    SB = lambda n, s, d: es.enter_context(nc.sbuf_tensor(n, s, d))
    PS = lambda n, s, d: es.enter_context(nc.psum_tensor(n, s, d))
    SEM = lambda n: es.enter_context(nc.semaphore(n))

    id_f = SB("id_f", [128, 128], F32)
    id_bf = SB("id_bf", [128, 128], BF16)
    onef = SB("onef", [128, 128], F32)
    onebf_c = SB("onebf_c", [128, 1], BF16)
    onebf_r = SB("onebf_r", [1, 512], BF16)
    cEPS5 = SB("cEPS5", [128, 1], F32)
    cEPS6 = SB("cEPS6", [128, 1], F32)
    cONE = SB("cONE", [128, 1], F32)
    cEPS24 = SB("cEPS24", [128, 1], F32)
    SCAL = SB("SCAL", [128, 16], F32)
    mymod = SB("mymod", [128, 96], F32)
    camod = SB("camod", [128, 16], F32)
    cbmod = SB("cbmod", [128, 16], F32)
    ga_t = SB("ga_t", [128, 16], F32)
    gb_t = SB("gb_t", [128, 16], F32)
    gtmp = SB("gtmp", [128, 32], F32)
    rowA = SB("rowA", [1, 1024], F32)
    rowB = SB("rowB", [1, 1024], F32)
    stage0 = SB("stage0", [128, 1024], F32)
    stage1 = SB("stage1", [128, 1024], F32)
    stage2 = SB("stage2", [128, 1024], F32)
    bst = [SB("bst0", [128, 512], BF16), SB("bst1", [128, 512], BF16)]
    wsum = SB("wsum", [128, 52], F32)
    wsmall = SB("wsmall", [128, 8], F32)
    scalrow = SB("scalrow", [1, 16], F32)
    pos8_f = SB("pos8_f", [8, 1024], F32)
    substacks = []
    es_a = contextlib.ExitStack()
    substacks.append(es_a)
    SBA = lambda n, s, d: es_a.enter_context(nc.sbuf_tensor(n, s, d))
    triu_f = SBA("triu_f", [128, 128], F32)
    m0_f = SBA("m0_f", [128, 512], F32)
    e127_f = SBA("e127_f", [128, 1], F32)
    bsel_f = SBA("bsel_f", [128, 2], F32)
    selT_f = SBA("selT_f", [8, 8], F32)
    amask_f = SBA("amask_f", [8, 1], F32)
    zab = SBA("zab", [128, 256], F32)
    la_all = SBA("la_all", [128, 128], F32)
    lnb_all = SBA("lnb_all", [128, 128], F32)
    albe = SBA("albe", [128, 256], F32)
    cla_all = SBA("cla_all", [128, 128], F32)
    cd_all = SBA("cd_all", [128, 128], F32)
    nclb = SBA("nclb", [128, 128], F32)
    claT_s = SBA("claT_s", [16, 1024], F32)
    nclbT_s = SBA("nclbT_s", [16, 1024], F32)
    wfac_all = SBA("wfac_all", [128, 128], F32)
    td_b = SBA("td_b", [128, 128], F32)
    cend_row = SBA("cend_row", [1, 128], F32)
    g_pub = SBA("g_pub", [1, 16], F32)
    gsb = SBA("gsb", [8, 16], F32)
    coef_sb = SBA("coef_sb", [8, 16], F32)
    coefB = SBA("coefB", [128, 128], F32)
    mb_g = SBA("mb_g", [128, 192], F32)
    psel_f = SBA("psel_f", [16, 2048], F32)
    bm16_f = SBA("bm16_f", [8, 128], F32)
    cmsk = SBA("cmsk", [8, 128], F32)
    ctrep = SBA("ctrep", [16, 512], F32)
    dr1 = [SBA("dr10", [16, 512], F32), SBA("dr11", [16, 512], F32)]

    pb = [PS("pb%d" % i, [128, 512], F32) for i in range(8)]

    qS = SEM("qS"); qG = SEM("qG"); qC = SEM("qC")
    cS = SEM("cS"); tS = SEM("tS"); vS = SEM("vS"); sS = SEM("sS")
    C = {"qS": 0, "qG": 0, "qC": 0, "cS": 0, "tS": 0, "vS": 0, "sS": 0}
    SMS = {"qS": qS, "qG": qG, "qC": qC, "cS": cS, "tS": tS, "vS": vS, "sS": sS}

    def inc(name, n=1):
        C[name] += n
        return C[name]

    bankfree = {}

    def _bail():
        for _st in reversed(substacks):
            _st.close()
        @block.sync
        def _(y):
            y.wait_ge(qC, C["qC"])
            y.wait_ge(qS, C["qS"])
            y.wait_ge(qG, C["qG"])
            y.wait_ge(vS, C["vS"])
            y.wait_ge(sS, C["sS"])
            y.wait_ge(tS, C["tS"])

    def bwait(eng, *banks):
        for i in banks:
            if i in bankfree:
                sn, cnt = bankfree[i]
                eng.wait_ge(SMS[sn], cnt)

    with es, nc.Block() as _block:
        class _SafeBlock:
            def __getattr__(self, eng):
                dec = getattr(_block, eng)
                def wrap(fn):
                    def body(e, fn=fn):
                        if SAFE:
                            for nm in ("qS", "qG", "qC", "cS", "tS", "vS", "sS"):
                                if C[nm]:
                                    e.wait_ge(SMS[nm], C[nm])
                        return fn(e)
                    return dec(body)
                return wrap
        block = _SafeBlock()
        @block.sync
        def _(y):
            y.dma_start(out=id_f[:], in_=ident).then_inc(qS, 16); inc("qS", 16)
            y.dma_start(out=onef[:], in_=ones_full).then_inc(qS, 16); inc("qS", 16)
            y.dma_start(out=triu_f[:], in_=triu).then_inc(qS, 16); inc("qS", 16)
            y.dma_start(out=m0_f[:], in_=m0c).then_inc(qS, 16); inc("qS", 16)
            y.dma_start(out=e127_f[:], in_=e127).then_inc(qS, 16); inc("qS", 16)
            y.dma_start(out=bsel_f[:], in_=bsel).then_inc(qS, 16); inc("qS", 16)
            y.dma_start(out=selT_f[:], in_=selT).then_inc(qS, 16); inc("qS", 16)
            y.dma_start(out=amask_f[:], in_=amask).then_inc(qS, 16); inc("qS", 16)
            y.dma_start(out=psel_f[:], in_=posisel).then_inc(qS, 16); inc("qS", 16)
            y.dma_start(out=bm16_f[:], in_=bm16).then_inc(qS, 16); inc("qS", 16)
            y.dma_start(out=pos8_f[:], in_=pos8).then_inc(qS, 16); inc("qS", 16)
            y.dma_start(out=stage0[:, 0:32], in_=cT32).then_inc(qS, 16); inc("qS", 16)
        CONSTS_S = C["qS"]

        @block.gpsimd
        def _(g):
            g.dma_start(out=id_bf[:], in_=ident).then_inc(qG, 16); inc("qG", 16)
            g.memset(onebf_c[:], 1.0)
            g.memset(onebf_r[:], 1.0)
            g.memset(cEPS5[:], 1e-5)
            g.memset(cEPS6[:], 1e-6)
            g.memset(cONE[:], 1.0)
            g.memset(cEPS24[:], 1e-24)
            g.drain().then_inc(qG, 1); inc("qG", 1)
        CONSTS_G = C["qG"]

        scT = stage1[:, 0:64].bitcast(BF16)
        @block.scalar
        def _(s):
            s.wait_ge(qS, CONSTS_S)
            s.wait_ge(qG, CONSTS_G)
            s.activation(out=scT[:, 0:32], in_=stage0[:, 0:32], func=AF.Silu)
            s.drain().then_inc(sS, 1); inc("sS", 1)
        SILU_C = C["sS"]

        if STOP_AFTER == "consts":
            _bail()
            return nc, dbg_out

        es_p0 = contextlib.ExitStack()
        substacks.append(es_p0)
        SBP = lambda n, s, d: es_p0.enter_context(nc.sbuf_tensor(n, s, d))
        wm = [SBP("wm0", [128, 1536], BF16), SBP("wm1", [128, 1536], BF16)]
        modst = SBP("modst", [128, 24], F32)
        wm_ld = []
        wm_used = []
        mod_ev = []
        for j in range(NDC):
            @block.gpsimd
            def _(g, j=j):
                if j >= 2:
                    g.wait_ge(tS, wm_used[j - 2])
                g.dma_start(out=wm[j % 2][:],
                            in_=wmod_sl[j * 128:(j + 1) * 128, :]).then_inc(qG, 16)
            inc("qG", 16)
            wm_ld.append(C["qG"])

            bk_i = j % 2
            @block.tensor
            def _(t, j=j, bk_i=bk_i):
                if j == 0:
                    t.wait_ge(sS, SILU_C)
                bwait(t, bk_i)
                t.wait_ge(qG, wm_ld[j])
                for ql in range(12):
                    t.matmul(pb[bk_i][:, ql * 2:ql * 2 + 2],
                             wm[j % 2][:, ql * 128:(ql + 1) * 128],
                             scT[:, 2 * j:2 * j + 2], start=True, stop=True)
                t.drain().then_inc(tS, 1)
            inc("tS", 1)
            wm_used.append(C["tS"])

            @block.vector
            def _(v, j=j, bk_i=bk_i):
                v.wait_ge(tS, wm_used[j])
                if j == 0:
                    v.tensor_copy(modst[:], pb[bk_i][:, 0:24])
                else:
                    v.drain()
                    v.tensor_tensor(out=modst[:], in0=modst[:], in1=pb[bk_i][:, 0:24],
                                    op=ALU.add)
                v.drain().then_inc(vS, 1)
            inc("vS", 1)
            mod_ev.append(C["vS"])
            bankfree[bk_i] = ("vS", C["vS"])
        MOD_EVAC = C["vS"]

        if STOP_AFTER == "mod":
            _bail()
            return nc, dbg_out

        wsl = [SBP("wsl0", [128, 1024], F32), SBP("wsl1", [128, 1024], F32)]
        absdump = SBP("absdump", [128, 1024], F32)
        wtiles = []
        for j in range(NDC):
            wtiles.append(wgT_sl[j * 128:(j + 1) * 128, :])
        for j in range(NDC):
            wtiles.append(wuT_sl[j * 128:(j + 1) * 128, :])
        for it in range(8):
            wtiles.append(wdT_sl[it * 128:(it + 1) * 128, 0:1024])
            wtiles.append(wdT_sl[it * 128:(it + 1) * 128, 1024:2048])
        wt_ld = []
        wt_used = []
        for i in range(48):
            @block.sync
            def _(y, i=i):
                if i >= 2:
                    y.wait_ge(sS, wt_used[i - 2])
                y.dma_start(out=wsl[i % 2][:], in_=wtiles[i]).then_inc(qS, 16)
            inc("qS", 16)
            wt_ld.append(C["qS"])

            @block.scalar
            def _(s, i=i):
                s.wait_ge(qS, wt_ld[i])
                s.activation(out=absdump[:], in_=wsl[i % 2][:], func=AF.Abs,
                             accum_out=wsum[:, i:i + 1]).then_inc(sS, 1)
            inc("sS", 1)
            wt_used.append(C["sS"])
        ABS_DONE = C["sS"]

        @block.vector
        def _(v):
            v.wait_ge(sS, ABS_DONE)
            v.tensor_reduce(out=wsmall[:, 0:1], in_=wsum[:, 0:16], axis=AX, op=ALU.add)
            v.tensor_reduce(out=wsmall[:, 1:2], in_=wsum[:, 16:32], axis=AX, op=ALU.add)
            v.tensor_reduce(out=wsmall[:, 2:3], in_=wsum[:, 32:48], axis=AX, op=ALU.add)
            v.drain().then_inc(vS, 1)
        inc("vS", 1)
        WRED = C["vS"]

        @block.tensor
        def _(t):
            t.wait_ge(vS, WRED)
            bwait(t, 1)
            t.matmul(pb[1][0:1, 0:3], onef[:, 0:1], wsmall[:, 0:3],
                     start=True, stop=True).then_inc(tS, 1)
        inc("tS", 1)
        WRED_MM = C["tS"]

        @block.vector
        def _(v):
            v.wait_ge(tS, WRED_MM)
            v.tensor_copy(scalrow[:, 0:3], pb[1][0:1, 0:3]).then_inc(vS, 1)
        inc("vS", 1)
        WROW = C["vS"]
        bankfree[1] = ("vS", WROW)

        if STOP_AFTER == "absw":
            _bail()
            return nc, dbg_out

        @block.gpsimd
        def _(g):
            g.wait_ge(vS, max(MOD_EVAC, WROW))
            g.dma_start(out=cc_mod_in.ap()[0:3072].rearrange("(p c) -> p c", p=128),
                        in_=modst[:]).then_inc(qG, 16)
            inc("qG", 16)
            g.dma_start(out=cc_mod_in.ap()[3072:3088].rearrange("(o c) -> o c", o=1),
                        in_=scalrow[:, 0:16]).then_inc(qG, 16)
            inc("qG", 16)
            g.wait_ge(qG, C["qG"])
            g.collective_compute(
                "AllGather", ALU.bypass, replica_groups=[list(range(NB))],
                ins=[cc_mod_in.ap().opt()], outs=[cc_mod_out.ap().opt()],
            ).then_inc(cS, 1)
        inc("cS", 1)
        AG1 = C["cS"]

        @block.sync
        def _(y):
            y.wait_ge(cS, AG1)
            y.dma_start(out=gsb[:],
                        in_=cc_mod_out.ap().rearrange("(r x) -> r x", r=NB)[:, 3072:3088]
                        ).then_inc(qS, 16)
            inc("qS", 16)
            y.dma_start(out=mb_g[:].rearrange("p (r c) -> p r c", r=NB),
                        in_=cc_mod_out.ap().rearrange("(r x) -> r x", r=NB)[:, 0:3072]
                        .rearrange("r (p c) -> p r c", p=128)).then_inc(qS, 16)
            inc("qS", 16)
        AG1_LD = C["qS"]

        @block.tensor
        def _(t):
            t.wait_ge(qS, AG1_LD)
            bwait(t, 2)
            t.matmul(pb[2][0:1, 0:16], onef[0:8, 0:1], gsb[:, 0:16],
                     start=True, stop=True).then_inc(tS, 1)
        inc("tS", 1)
        GS_MM = C["tS"]

        @block.vector
        def _(v):
            v.wait_ge(tS, GS_MM)
            v.tensor_scalar(out=scalrow[:, 0:3], in0=pb[2][0:1, 0:3],
                            scalar1=1.0 / (8192.0 * 2048.0), scalar2=1e-5,
                            op0=ALU.mult, op1=ALU.add)
            v.drain()
            v.reciprocal(scalrow[:, 3:6], scalrow[:, 0:3])
            v.drain().then_inc(vS, 1)
        inc("vS", 1)
        SCROW = C["vS"]
        bankfree[2] = ("vS", SCROW)

        @block.tensor
        def _(t):
            t.wait_ge(vS, SCROW)
            bwait(t, 3)
            t.matmul(pb[3][:, 0:16], onef[0:1, 0:128], scalrow[:, 0:16],
                     start=True, stop=True).then_inc(tS, 1)
        inc("tS", 1)
        SC_MM = C["tS"]

        @block.vector
        def _(v):
            v.wait_ge(tS, SC_MM)
            v.tensor_copy(SCAL[:], pb[3][:, 0:16]).then_inc(vS, 1)
            inc("vS", 1)
            v.wait_ge(qS, AG1_LD)
            mbv = mb_g[:].rearrange("p (r q b) -> p r q b", r=NB, b=2)
            v.tensor_scalar(out=mymod[:].rearrange("p (r q) -> p r q", r=NB),
                            in0=mbv[:, :, :, 0], scalar1=bsel_f[:, 0:1],
                            scalar2=0.0, op0=ALU.mult, op1=ALU.add)
            v.drain()
            v.scalar_tensor_tensor(out=mymod[:].rearrange("p (r q) -> p r q", r=NB),
                                   in0=mbv[:, :, :, 1], scalar=bsel_f[:, 1:2],
                                   in1=mymod[:].rearrange("p (r q) -> p r q", r=NB),
                                   op0=ALU.mult, op1=ALU.add)
            v.drain()
            v.tensor_scalar(out=camod[:], in0=mymod[:, 16:32], scalar1=1.0, scalar2=0.0,
                            op0=ALU.add, op1=ALU.add)
            v.tensor_scalar(out=cbmod[:], in0=mymod[:, 64:80], scalar1=1.0, scalar2=0.0,
                            op0=ALU.add, op1=ALU.add)
            v.drain().then_inc(vS, 1)
        inc("vS", 1)
        MYMOD = C["vS"]
        bankfree[3] = ("vS", MYMOD)

        @block.vector
        def _(v):
            v.wait_ge(vS, MYMOD)
            for (gt, col) in ((ga_t, 32), (gb_t, 80)):
                off = 0 if col == 32 else 16
                v.tensor_scalar_mul(gtmp[:, off:off + 16], mymod[:, col:col + 16], 1e30)
                v.drain()
                v.tensor_scalar(out=gtmp[:, off:off + 16], in0=gtmp[:, off:off + 16],
                                scalar1=1.0, scalar2=-1.0, op0=ALU.min, op1=ALU.max)
                v.drain()
                v.tensor_tensor(out=gt[:], in0=mymod[:, col:col + 16],
                                in1=gtmp[:, off:off + 16], op=ALU.mult)
                v.drain()
                v.tensor_scalar(out=gt[:], in0=gt[:], scalar1=0.25, scalar2=0.0,
                                op0=ALU.max, op1=ALU.add)
                v.drain()
                v.tensor_tensor(out=gt[:], in0=gt[:], in1=gtmp[:, off:off + 16],
                                op=ALU.mult)
            v.drain().then_inc(vS, 1)
        inc("vS", 1)
        GAGB = C["vS"]

        if STOP_AFTER == "ag1c":
            _bail()
            return nc, dbg_out

        wqstg = [SBP("wqstg0", [128, 1024], BF16), SBP("wqstg1", [128, 1024], BF16)]
        wq_dst = []
        for j in range(NDC):
            wq_dst.append(("gu", j * 131072))
        for j in range(NDC):
            wq_dst.append(("gu", WG_SZ + j * 131072))
        for it in range(8):
            wq_dst.append(("wd", (it, 0)))
            wq_dst.append(("wd", (it, 1)))
        q_ld = []
        q_sc = []
        q_vc = []
        q_st = []
        for i in range(48):
            invcol = 3 + (0 if i < 16 else (1 if i < 32 else 2))

            @block.sync
            def _(y, i=i):
                if i >= 2:
                    y.wait_ge(sS, q_sc[i - 2])
                if i == 0:
                    y.wait_ge(vS, SCROW)
                y.dma_start(out=wsl[i % 2][:], in_=wtiles[i]).then_inc(qS, 16)
            inc("qS", 16)
            q_ld.append(C["qS"])

            @block.scalar
            def _(s, i=i, invcol=invcol):
                s.wait_ge(qS, q_ld[i])
                if i == 0:
                    s.wait_ge(vS, SCROW + 1)
                if i >= 2:
                    s.wait_ge(vS, q_vc[i - 2])
                s.activation(out=stage2[:], in_=wsl[i % 2][:], func=AF.Copy,
                             scale=SCAL[:, invcol:invcol + 1], bias=MAGIC
                             ).then_inc(sS, 1)
            inc("sS", 1)
            q_sc.append(C["sS"])

            @block.vector
            def _(v, i=i):
                v.wait_ge(sS, q_sc[i])
                if i >= 2:
                    v.wait_ge(qG, q_st[i - 2])
                v.tensor_scalar(out=stage2[:], in0=stage2[:], scalar1=MAGIC + 1.0,
                                scalar2=MAGIC - 1.0, op0=ALU.min, op1=ALU.max)
                v.tensor_scalar(out=wqstg[i % 2][:], in0=stage2[:], scalar1=MAGIC,
                                scalar2=0.0, op0=ALU.subtract, op1=ALU.add
                                ).then_inc(vS, 1)
            inc("vS", 1)
            q_vc.append(C["vS"])

            kind, off = wq_dst[i]

            @block.gpsimd
            def _(g, i=i, kind=kind, off=off):
                g.wait_ge(vS, q_vc[i])
                if kind == "gu":
                    g.dma_start(out=cc_wq_in.ap()[off:off + 131072]
                                .rearrange("(p c) -> p c", p=128),
                                in_=wqstg[i % 2][:]).then_inc(qG, 16)
                else:
                    it, half = off
                    w0 = 2 * WG_SZ + half * 8 * 131072
                    dst = (cc_wq_in.ap()[w0:w0 + 8 * 131072]
                           .rearrange("(hc i h) -> hc i h", hc=8, i=1024)
                           [:, it * 128:(it + 1) * 128, :]
                           .rearrange("hc p h -> p hc h"))
                    g.dma_start(out=dst,
                                in_=wqstg[i % 2][:].rearrange("p (hc h) -> p hc h", hc=8)
                                ).then_inc(qG, 16)
            inc("qG", 16)
            q_st.append(C["qG"])
        QPUB = C["qG"]

        @block.gpsimd
        def _(g):
            g.wait_ge(qG, QPUB)
            g.collective_compute(
                "AllGather", ALU.bypass, replica_groups=[list(range(NB))],
                ins=[cc_wq_in.ap().opt()], outs=[cc_wq_out.ap().opt()],
            ).then_inc(cS, 1)
        inc("cS", 1)
        AG2 = C["cS"]

        if STOP_AFTER == "quant":
            _bail()
            return nc, dbg_out

        es_p0.close()
        es_h = contextlib.ExitStack()
        substacks.append(es_h)
        h_bf = es_h.enter_context(nc.sbuf_tensor("h_bf", [128, NDC * 1024], BF16))
        es_x = contextlib.ExitStack()
        substacks.append(es_x)
        x_f = es_x.enter_context(nc.sbuf_tensor("x_f", [128, NDC * 1024], F32))
        x_bf = es_x.enter_context(nc.sbuf_tensor("x_bf", [128, NDC * 1024], BF16))
        x2b = [es_x.enter_context(nc.sbuf_tensor("x2b%d" % i, [128, 1024], BF16))
               for i in range(2)]
        rowC = es_x.enter_context(nc.sbuf_tensor("rowC", [1, 1024], F32))

        @block.sync
        def _(y):
            y.wait_ge(qG, QPUB)
            y.wait_ge(vS, C["vS"])
            y.wait_ge(sS, C["sS"])
            y.dma_start(out=x_f[:].rearrange("p (j t) -> p j t", j=NDC),
                        in_=xT.rearrange("(j p) t -> p j t", p=128)).then_inc(qS, 16)
        inc("qS", 16)
        XLD = C["qS"]

        @block.gpsimd
        def _(g):
            g.wait_ge(qG, QPUB)
            g.dma_start(out=x_bf[:].rearrange("p (j t) -> p j t", j=NDC),
                        in_=xT.rearrange("(j p) t -> p j t", p=128)).then_inc(qG, 16)
        inc("qG", 16)
        XBLD = C["qG"]

        sq_done = []
        st_done = []
        for j in range(NDC):
            @block.scalar
            def _(s, j=j):
                if j == 0:
                    s.wait_ge(qS, XLD)
                if j >= 2:
                    s.wait_ge(tS, st_done[j - 2])
                s.activation(out=x2b[j % 2][:], in_=x_f[:, j * 1024:(j + 1) * 1024],
                             func=AF.Square).then_inc(sS, 1)
            inc("sS", 1)
            sq_done.append(C["sS"])

            @block.tensor
            def _(t, j=j):
                if j == 0:
                    t.wait_ge(qG, XBLD)
                    bwait(t, 4, 5, 6, 7)
                t.wait_ge(sS, sq_done[j])
                for tt in range(2):
                    t.matmul(pb[4 + tt][0:1, :], onebf_c[:, 0:1],
                             x_bf[:, j * 1024 + tt * 512: j * 1024 + (tt + 1) * 512],
                             start=(j == 0), stop=(j == NDC - 1))
                    t.matmul(pb[6 + tt][0:1, :], onebf_c[:, 0:1],
                             x2b[j % 2][:, tt * 512:(tt + 1) * 512],
                             start=(j == 0), stop=(j == NDC - 1))
                t.drain().then_inc(tS, 1)
            inc("tS", 1)
            st_done.append(C["tS"])
        STATS_MM = C["tS"]

        @block.vector
        def _(v):
            v.wait_ge(tS, STATS_MM)
            v.tensor_scalar(out=rowA[0:1, 0:512], in0=pb[4][0:1, :], scalar1=1.0 / 2048.0,
                            scalar2=0.0, op0=ALU.mult, op1=ALU.add)
            v.tensor_scalar(out=rowA[0:1, 512:1024], in0=pb[5][0:1, :], scalar1=1.0 / 2048.0,
                            scalar2=0.0, op0=ALU.mult, op1=ALU.add)
            v.tensor_scalar(out=rowB[0:1, 0:512], in0=pb[6][0:1, :], scalar1=1.0 / 2048.0,
                            scalar2=0.0, op0=ALU.mult, op1=ALU.add)
            v.tensor_scalar(out=rowB[0:1, 512:1024], in0=pb[7][0:1, :], scalar1=1.0 / 2048.0,
                            scalar2=0.0, op0=ALU.mult, op1=ALU.add)
            v.drain()
            v.tensor_tensor(out=rowC[:], in0=rowA[:], in1=rowA[:], op=ALU.mult)
            v.drain()
            v.tensor_tensor(out=rowB[:], in0=rowB[:], in1=rowC[:], op=ALU.subtract)
            v.drain().then_inc(vS, 1)
        inc("vS", 1)
        VAR1 = C["vS"]
        for b_ in (4, 5, 6, 7):
            bankfree[b_] = ("vS", VAR1)

        @block.scalar
        def _(s):
            s.wait_ge(vS, VAR1)
            s.activation(out=rowC[:], in_=rowB[:], func=AF.Ln, bias=cEPS5[0:1, 0:1])
            s.activation(out=rowB[:], in_=rowC[:], func=AF.Exp, scale=-0.5)
            s.drain().then_inc(sS, 1)
        inc("sS", 1)
        ISTD1 = C["sS"]

        if dbg:
            @block.scalar
            def _(s):
                s.wait_ge(vS, MYMOD)
                s.wait_ge(sS, ISTD1)
                s.drain()
                s.dma_start(out=dbg_out["d_mbg"], in_=mb_g[:]).then_inc(qC, 16)
                s.dma_start(out=dbg_out["d_mrow"], in_=rowA[:]).then_inc(qC, 16)
                s.dma_start(out=dbg_out["d_istd"], in_=rowB[:]).then_inc(qC, 16)
                s.dma_start(out=dbg_out["d_mymod"], in_=mymod[:]).then_inc(qC, 16)
                s.dma_start(out=dbg_out["d_scal"], in_=SCAL[:]).then_inc(qC, 16)
            inc("qC", 80)

        @block.tensor
        def _(t):
            t.wait_ge(sS, ISTD1)
            bwait(t, 0, 1, 2, 3)
            for tt in range(2):
                t.matmul(pb[0 + tt][:, :], onef[0:1, 0:128],
                         rowA[0:1, tt * 512:(tt + 1) * 512], start=True, stop=True)
                t.matmul(pb[2 + tt][:, :], onef[0:1, 0:128],
                         rowB[0:1, tt * 512:(tt + 1) * 512], start=True, stop=True)
            t.drain().then_inc(tS, 1)
        inc("tS", 1)
        BC1 = C["tS"]

        @block.vector
        def _(v):
            v.wait_ge(tS, BC1)
            v.tensor_copy(stage0[:, 0:512], pb[0][:, :])
            v.tensor_copy(stage0[:, 512:1024], pb[1][:, :])
            v.tensor_copy(stage1[:, 0:512], pb[2][:, :])
            v.tensor_copy(stage1[:, 512:1024], pb[3][:, :])
            v.drain().then_inc(vS, 1)
        inc("vS", 1)
        MB1 = C["vS"]
        for b_ in (0, 1, 2, 3):
            bankfree[b_] = ("vS", MB1)

        for j in range(NDC):
            @block.vector
            def _(v, j=j):
                if j == 0:
                    v.wait_ge(vS, max(MB1, GAGB))
                cs = slice(j * 1024, (j + 1) * 1024)
                v.tensor_tensor(out=stage2[:], in0=x_f[:, cs], in1=stage0[:], op=ALU.subtract)
                v.tensor_tensor(out=stage2[:], in0=stage2[:], in1=stage1[:], op=ALU.mult)
                v.tensor_scalar(out=h_bf[:, cs], in0=stage2[:],
                                scalar1=camod[:, j:j + 1], scalar2=mymod[:, j:j + 1],
                                op0=ALU.mult, op1=ALU.add)
                v.drain().then_inc(vS, 1)
            inc("vS", 1)
        H_DONE = C["vS"]

        if dbg:
            @block.gpsimd
            def _(g):
                g.wait_ge(vS, H_DONE)
                g.dma_start(out=dbg_out["d_h"].rearrange("(j p) t -> p j t", p=128),
                            in_=h_bf[:].rearrange("p (j t) -> p j t", j=NDC)).then_inc(qG, 16)
            inc("qG", 16)

        @block.scalar
        def _(s):
            s.wait_ge(vS, H_DONE)
            s.dma_start(out=x_sp.ap().rearrange("(j p) t -> p j t", p=128),
                        in_=x_f[:].rearrange("p (j t) -> p j t", j=NDC)).then_inc(qC, 16)
        inc("qC", 16)
        XSPILL = C["qC"]
        es_x.close()

        if STOP_AFTER == "ln1":
            _bail()
            return nc, dbg_out

        es_w = contextlib.ExitStack()
        substacks.append(es_w)
        wbig = [es_w.enter_context(nc.sbuf_tensor("wbig%d" % i, [128, 8192], BF16))
                for i in range(2)]
        wab_t = es_w.enter_context(nc.sbuf_tensor("wab_t", [128, 512], BF16))
        bq_bf = es_w.enter_context(nc.sbuf_tensor("bq_bf", [1, D], BF16))
        bk_bf = es_w.enter_context(nc.sbuf_tensor("bk_bf", [1, D], BF16))
        bv_bf = es_w.enter_context(nc.sbuf_tensor("bv_bf", [1, D], BF16))
        bab_bf = es_w.enter_context(nc.sbuf_tensor("bab_bf", [1, 32], BF16))
        kn_s = es_w.enter_context(nc.sbuf_tensor("kn_s", [128, 8], F32))

        @block.gpsimd
        def _(g):
            g.wait_ge(vS, H_DONE)
            g.wait_ge(qC, XSPILL)
            g.wait_ge(tS, C["tS"])
            g.dma_start(out=bq_bf[:], in_=bq).then_inc(qG, 16)
            g.dma_start(out=bk_bf[:], in_=bk).then_inc(qG, 16)
            g.dma_start(out=bv_bf[:], in_=bv).then_inc(qG, 16)
            g.dma_start(out=bab_bf[:], in_=bab).then_inc(qG, 16)
            g.dma_start(out=wab_t[:].rearrange("p (j c) -> p j c", j=NDC),
                        in_=Wab.rearrange("(j p) c -> p j c", p=128)).then_inc(qG, 16)
        inc("qG", 80)
        BIAS_LD = C["qG"]

        wq_ld = []
        q_use = []
        q_stq = []
        for h in range(NH):
            @block.gpsimd
            def _(g, h=h):
                if h >= 2:
                    g.wait_ge(tS, q_use[h - 2])
                g.dma_start(out=wbig[h % 2][:, 0:2048].rearrange("p (j s) -> p j s", j=NDC),
                            in_=Wq.rearrange("(j p) s -> p j s", p=128)
                            [:, :, h * 128:(h + 1) * 128]).then_inc(qG, 16)
            inc("qG", 16)
            wq_ld.append(C["qG"])

            for tt in range(2):
                bk_i = (h * 2 + tt) % 2
                @block.tensor
                def _(t, h=h, tt=tt, bk_i=bk_i):
                    if tt == 0:
                        t.wait_ge(qG, wq_ld[h])
                        if h == 0:
                            t.wait_ge(vS, H_DONE)
                            t.wait_ge(qG, BIAS_LD)
                    bwait(t, bk_i)
                    for j in range(NDC):
                        t.matmul(pb[bk_i][:, :], wbig[h % 2][:, j * 128:(j + 1) * 128],
                                 h_bf[:, j * 1024 + tt * 512: j * 1024 + (tt + 1) * 512],
                                 start=(j == 0), stop=False)
                    t.matmul(pb[bk_i][:, :], bq_bf[0:1, h * 128:(h + 1) * 128],
                             onebf_r[0:1, 0:512], start=False, stop=True)
                    t.drain().then_inc(tS, 1)
                inc("tS", 1)
                mmq = C["tS"]
                if tt == 1:
                    q_use.append(mmq)

                @block.scalar
                def _(s, h=h, tt=tt, bk_i=bk_i, mmq=mmq):
                    s.wait_ge(tS, mmq)
                    idx = h * 2 + tt
                    if idx >= 2:
                        s.wait_ge(qC, q_stq[idx - 2])
                    s.activation(out=bst[bk_i][:], in_=pb[bk_i][:, :], func=AF.Copy
                                 ).then_inc(sS, 1)
                    s.drain()
                    s.dma_start(out=q_sp.ap()[h, :, tt * 512:(tt + 1) * 512],
                                in_=bst[bk_i][:]).then_inc(qC, 16)
                inc("sS", 1)
                inc("qC", 16)
                q_stq.append(C["qC"])
                bankfree[bk_i] = ("sS", C["sS"])
        Q_DONE = C["qC"]
        if dbg:
            @block.gpsimd
            def _(g):
                g.wait_ge(tS, q_use[-1])
                g.dma_start(out=dbg_out["d_wbig"], in_=wbig[15 % 2][:, 0:2048]).then_inc(qG, 16)
                g.dma_start(out=dbg_out["d_hbf"], in_=h_bf[:, 0:2048]).then_inc(qG, 16)
            inc("qG", 32)

        wk_ld = []
        k_use = []
        k_stq = []
        for sg in range(4):
            @block.gpsimd
            def _(g, sg=sg):
                if sg >= 2:
                    g.wait_ge(tS, k_use[sg - 2])
                else:
                    g.wait_ge(tS, q_use[-1])
                g.dma_start(out=wbig[sg % 2][:].rearrange("p (j s) -> p j s", j=NDC),
                            in_=Wk.rearrange("(j p) s -> p j s", p=128)
                            [:, :, sg * 512:(sg + 1) * 512]).then_inc(qG, 16)
            inc("qG", 16)
            wk_ld.append(C["qG"])

            for tc in range(NCH):
                bk_i = 2 + (sg * NCH + tc) % 2
                @block.tensor
                def _(t, sg=sg, tc=tc, bk_i=bk_i):
                    if tc == 0:
                        t.wait_ge(qG, wk_ld[sg])
                    bwait(t, bk_i)
                    for j in range(NDC):
                        t.matmul(pb[bk_i][:, :],
                                 h_bf[:, j * 1024 + tc * 128: j * 1024 + (tc + 1) * 128],
                                 wbig[sg % 2][:, j * 512:(j + 1) * 512],
                                 start=(j == 0), stop=False)
                    t.matmul(pb[bk_i][:, :], onebf_r[0:1, 0:128],
                             bk_bf[0:1, sg * 512:(sg + 1) * 512], start=False, stop=True)
                    t.drain().then_inc(tS, 1)
                inc("tS", 1)
                mmk = C["tS"]
                if tc == NCH - 1:
                    k_use.append(mmk)

                @block.scalar
                def _(s, bk_i=bk_i, mmk=mmk):
                    s.wait_ge(tS, mmk)
                    s.activation(out=stage2[:, 0:512], in_=pb[bk_i][:, :],
                                 func=AF.Square).then_inc(sS, 1)
                inc("sS", 1)
                sqk = C["sS"]

                @block.vector
                def _(v, sqk=sqk):
                    v.wait_ge(sS, sqk)
                    v.tensor_reduce(out=kn_s[:, 0:4],
                                    in_=stage2[:, 0:512].rearrange("p (h s) -> p h s", h=4),
                                    axis=AX, op=ALU.add)
                    v.drain().then_inc(vS, 1)
                inc("vS", 1)
                rdk = C["vS"]

                @block.scalar
                def _(s, rdk=rdk):
                    s.wait_ge(vS, rdk)
                    s.activation(out=kn_s[:, 4:8], in_=kn_s[:, 0:4], func=AF.Ln,
                                 bias=cEPS24[:, 0:1])
                    s.activation(out=kn_s[:, 0:4], in_=kn_s[:, 4:8], func=AF.Exp,
                                 scale=-0.5).then_inc(sS, 1)
                inc("sS", 1)
                lnk = C["sS"]

                @block.vector
                def _(v, sg=sg, tc=tc, bk_i=bk_i, lnk=lnk):
                    v.wait_ge(sS, lnk)
                    idx = sg * NCH + tc
                    if idx >= 2:
                        v.wait_ge(qC, k_stq[idx - 2])
                    else:
                        v.wait_ge(qC, Q_DONE)
                    for hh in range(4):
                        v.tensor_scalar(out=bst[bk_i - 2][:, hh * 128:(hh + 1) * 128],
                                        in0=pb[bk_i][:, hh * 128:(hh + 1) * 128],
                                        scalar1=kn_s[:, hh:hh + 1], scalar2=0.0,
                                        op0=ALU.mult, op1=ALU.add)
                    v.drain().then_inc(vS, 1)
                inc("vS", 1)
                bankfree[bk_i] = ("vS", C["vS"])
                evk = C["vS"]

                @block.scalar
                def _(s, sg=sg, tc=tc, bk_i=bk_i, evk=evk):
                    s.wait_ge(vS, evk)
                    s.dma_start(out=k_sp.ap()[tc, :, sg * 512:(sg + 1) * 512],
                                in_=bst[bk_i - 2][:]).then_inc(qC, 16)
                inc("qC", 16)
                k_stq.append(C["qC"])
        K_DONE = C["qC"]

        wv_ld = []
        v_use = []
        v_stq = []
        for sg in range(4):
            @block.gpsimd
            def _(g, sg=sg):
                if sg >= 2:
                    g.wait_ge(tS, v_use[sg - 2])
                else:
                    g.wait_ge(tS, k_use[-1])
                g.dma_start(out=wbig[sg % 2][:].rearrange("p (j s) -> p j s", j=NDC),
                            in_=Wv.rearrange("(j p) s -> p j s", p=128)
                            [:, :, sg * 512:(sg + 1) * 512]).then_inc(qG, 16)
            inc("qG", 16)
            wv_ld.append(C["qG"])

            for tc in range(NCH):
                bk_i = 4 + (sg * NCH + tc) % 2
                @block.tensor
                def _(t, sg=sg, tc=tc, bk_i=bk_i):
                    if tc == 0:
                        t.wait_ge(qG, wv_ld[sg])
                    bwait(t, bk_i)
                    for j in range(NDC):
                        t.matmul(pb[bk_i][:, :],
                                 h_bf[:, j * 1024 + tc * 128: j * 1024 + (tc + 1) * 128],
                                 wbig[sg % 2][:, j * 512:(j + 1) * 512],
                                 start=(j == 0), stop=False)
                    t.matmul(pb[bk_i][:, :], onebf_r[0:1, 0:128],
                             bv_bf[0:1, sg * 512:(sg + 1) * 512], start=False, stop=True)
                    t.drain().then_inc(tS, 1)
                inc("tS", 1)
                mmv = C["tS"]
                if tc == NCH - 1:
                    v_use.append(mmv)

                @block.scalar
                def _(s, sg=sg, tc=tc, bk_i=bk_i, mmv=mmv):
                    s.wait_ge(tS, mmv)
                    idx = sg * NCH + tc
                    if idx >= 2:
                        s.wait_ge(qC, v_stq[idx - 2])
                    else:
                        s.wait_ge(qC, K_DONE)
                    s.activation(out=bst[bk_i - 4][:], in_=pb[bk_i][:, :], func=AF.Copy
                                 ).then_inc(sS, 1)
                    s.drain()
                    s.dma_start(out=v_sp.ap()[tc, :, sg * 512:(sg + 1) * 512],
                                in_=bst[bk_i - 4][:]).then_inc(qC, 16)
                inc("sS", 1)
                inc("qC", 16)
                v_stq.append(C["qC"])
                bankfree[bk_i] = ("sS", C["sS"])
        V_DONE = C["qC"]

        ab_done = []
        for tc in range(NCH):
            bk_i = 6 + tc % 2
            @block.tensor
            def _(t, tc=tc, bk_i=bk_i):
                if tc == 0:
                    t.wait_ge(qG, BIAS_LD)
                bwait(t, bk_i)
                for j in range(NDC):
                    t.matmul(pb[bk_i][:, 0:32],
                             h_bf[:, j * 1024 + tc * 128: j * 1024 + (tc + 1) * 128],
                             wab_t[:, j * 32:(j + 1) * 32],
                             start=(j == 0), stop=False)
                t.matmul(pb[bk_i][:, 0:32], onebf_r[0:1, 0:128], bab_bf[0:1, 0:32],
                         start=False, stop=True)
                t.drain().then_inc(tS, 1)
            inc("tS", 1)
            mab = C["tS"]

            @block.vector
            def _(v, tc=tc, bk_i=bk_i, mab=mab):
                v.wait_ge(tS, mab)
                v.tensor_copy(zab[:, tc * 32:(tc + 1) * 32], pb[bk_i][:, 0:32]
                              ).then_inc(vS, 1)
            inc("vS", 1)
            ab_done.append(C["vS"])
            bankfree[bk_i] = ("vS", C["vS"])
        AB_DONE = C["vS"]

        if STOP_AFTER == "proj":
            _bail()
            return nc, dbg_out

        es_w.close()
        es_h.close()

        @block.vector
        def _(v):
            v.wait_ge(vS, AB_DONE)
            v.tensor_copy(albe[:, 0:128].rearrange("p (tc c) -> p tc c", tc=8),
                          zab[:].rearrange("p (tc c) -> p tc c", tc=8)[:, :, 0:16])
            v.tensor_copy(albe[:, 128:256].rearrange("p (tc c) -> p tc c", tc=8),
                          zab[:].rearrange("p (tc c) -> p tc c", tc=8)[:, :, 16:32])
            v.drain().then_inc(vS, 1)
        inc("vS", 1)
        AB_G = C["vS"]

        @block.scalar
        def _(s):
            s.wait_ge(vS, AB_G)
            s.activation(out=cd_all[:], in_=albe[:, 0:128], func=AF.Sigmoid)
            s.activation(out=la_all[:], in_=cd_all[:], func=AF.Ln, bias=cEPS6[:, 0:1])
            s.activation(out=cd_all[:], in_=albe[:, 128:256], func=AF.Abs)
            s.activation(out=nclb[:], in_=cd_all[:], func=AF.Exp, scale=-1.0)
            s.activation(out=cd_all[:], in_=nclb[:], func=AF.Ln, bias=cONE[:, 0:1])
            s.drain().then_inc(sS, 1)
        inc("sS", 1)
        SP_A = C["sS"]

        @block.vector
        def _(v):
            v.wait_ge(sS, SP_A)
            v.scalar_tensor_tensor(out=nclb[:], in0=albe[:, 128:256], scalar=0.0,
                                   in1=cd_all[:], op0=ALU.max, op1=ALU.add)
            v.drain().then_inc(vS, 1)
        inc("vS", 1)
        SP_B = C["vS"]

        @block.scalar
        def _(s):
            s.wait_ge(vS, SP_B)
            s.activation(out=lnb_all[:], in_=nclb[:], func=AF.Ln)
            s.drain().then_inc(sS, 1)
        inc("sS", 1)
        LNB_DONE = C["sS"]

        cla_ev = []
        for tc in range(NCH):
            bk_i = 6 + tc % 2
            @block.tensor
            def _(t, tc=tc, bk_i=bk_i):
                if tc == 0:
                    t.wait_ge(sS, LNB_DONE)
                bwait(t, bk_i)
                t.matmul(pb[bk_i][:, 0:16], triu_f[:], la_all[:, tc * 16:(tc + 1) * 16],
                         start=True, stop=True).then_inc(tS, 1)
            inc("tS", 1)
            mcla = C["tS"]

            @block.vector
            def _(v, tc=tc, bk_i=bk_i, mcla=mcla):
                v.wait_ge(tS, mcla)
                v.tensor_copy(cla_all[:, tc * 16:(tc + 1) * 16], pb[bk_i][:, 0:16]
                              ).then_inc(vS, 1)
            inc("vS", 1)
            cla_ev.append(C["vS"])
            bankfree[bk_i] = ("vS", C["vS"])
        CLA_DONE = C["vS"]

        @block.scalar
        def _(s):
            s.wait_ge(vS, CLA_DONE)
            s.activation(out=cd_all[:], in_=cla_all[:], func=AF.Exp)
            s.drain().then_inc(sS, 1)
        inc("sS", 1)
        CD_DONE = C["sS"]

        @block.vector
        def _(v):
            v.wait_ge(vS, CLA_DONE)
            v.tensor_tensor(out=nclb[:], in0=lnb_all[:], in1=cla_all[:], op=ALU.subtract)
            v.drain().then_inc(vS, 1)
        inc("vS", 1)
        NCLB_DONE = C["vS"]

        trev = []
        for tc in range(NCH):
            bk_i = 6 + tc % 2
            @block.tensor
            def _(t, tc=tc, bk_i=bk_i):
                if tc == 0:
                    t.wait_ge(vS, NCLB_DONE)
                bwait(t, bk_i)
                t.matmul(pb[bk_i][0:16, 0:128], cla_all[:, tc * 16:(tc + 1) * 16], id_f[:],
                         start=True, stop=True)
                t.matmul(pb[bk_i][0:16, 128:256], nclb[:, tc * 16:(tc + 1) * 16], id_f[:],
                         start=True, stop=True)
                t.drain().then_inc(tS, 1)
            inc("tS", 1)
            mtr = C["tS"]

            @block.vector
            def _(v, tc=tc, bk_i=bk_i, mtr=mtr):
                v.wait_ge(tS, mtr)
                v.tensor_copy(claT_s[:, tc * 128:(tc + 1) * 128], pb[bk_i][0:16, 0:128])
                v.tensor_copy(nclbT_s[:, tc * 128:(tc + 1) * 128], pb[bk_i][0:16, 128:256])
                v.drain().then_inc(vS, 1)
            inc("vS", 1)
            trev.append(C["vS"])
            bankfree[bk_i] = ("vS", C["vS"])
        TR_DONE = C["vS"]

        @block.tensor
        def _(t):
            bwait(t, 0)
            t.wait_ge(vS, CLA_DONE)
            t.matmul(pb[0][0:1, 0:128], e127_f[:, 0:1], cla_all[:], start=True, stop=True
                     ).then_inc(tS, 1)
        inc("tS", 1)
        CEND_MM = C["tS"]

        @block.vector
        def _(v):
            v.wait_ge(tS, CEND_MM)
            v.tensor_copy(cend_row[:], pb[0][0:1, 0:128]).then_inc(vS, 1)
            inc("vS", 1)
            v.drain()
            v.tensor_reduce(out=g_pub[:], in_=cend_row[:].rearrange("o (t h) -> o h t", t=8),
                            axis=AX, op=ALU.add)
            v.drain().then_inc(vS, 1)
        inc("vS", 1)
        GPUB = C["vS"]
        bankfree[0] = ("vS", GPUB)

        @block.scalar
        def _(s):
            s.wait_ge(vS, GPUB)
            s.activation(out=rowA[0:1, 0:128], in_=cend_row[:], func=AF.Exp)
            s.drain().then_inc(sS, 1)
        inc("sS", 1)
        ETD = C["sS"]

        @block.tensor
        def _(t):
            t.wait_ge(sS, ETD)
            bwait(t, 1, 2)
            t.matmul(pb[1][:, 0:128], onef[0:1, 0:128], rowA[0:1, 0:128],
                     start=True, stop=True)
            t.matmul(pb[2][:, 0:128], onef[0:1, 0:128], cend_row[0:1, 0:128],
                     start=True, stop=True)
            t.drain().then_inc(tS, 1)
        inc("tS", 1)
        TDB_MM = C["tS"]

        @block.vector
        def _(v):
            v.wait_ge(tS, TDB_MM)
            v.tensor_copy(td_b[:], pb[1][:, 0:128]).then_inc(vS, 1)
            inc("vS", 1)
            v.tensor_tensor(out=stage2[:, 0:128], in0=pb[2][:, 0:128], in1=nclb[:],
                            op=ALU.add)
            v.drain().then_inc(vS, 1)
        inc("vS", 1)
        WARG = C["vS"]
        bankfree[1] = ("vS", WARG)
        bankfree[2] = ("vS", WARG)

        @block.scalar
        def _(s):
            s.wait_ge(vS, WARG)
            s.activation(out=wfac_all[:], in_=stage2[:, 0:128], func=AF.Exp)
            s.drain().then_inc(sS, 1)
        inc("sS", 1)
        WFAC = C["sS"]

        if dbg:
            @block.scalar
            def _(s):
                s.wait_ge(vS, GAGB)
                s.drain()
                s.dma_start(out=dbg_out["d_ga"][:, 0:16], in_=ga_t[:]).then_inc(qC, 16)
                s.dma_start(out=dbg_out["d_ga"][:, 16:32], in_=gb_t[:]).then_inc(qC, 16)
            inc("qC", 32)

            @block.scalar
            def _(s):
                s.wait_ge(sS, WFAC)
                s.dma_start(out=dbg_out["d_la"], in_=la_all[:]).then_inc(qC, 16)
                s.dma_start(out=dbg_out["d_lnb"], in_=lnb_all[:]).then_inc(qC, 16)
                s.dma_start(out=dbg_out["d_cla"], in_=cla_all[:]).then_inc(qC, 16)
                s.dma_start(out=dbg_out["d_wfac"], in_=wfac_all[:]).then_inc(qC, 16)
                s.dma_start(out=dbg_out["d_td"], in_=td_b[:]).then_inc(qC, 16)
                s.dma_start(out=dbg_out["d_cd"], in_=cd_all[:]).then_inc(qC, 16)
            inc("qC", 96)

        if STOP_AFTER == "cla":
            _bail()
            return nc, dbg_out

        es_c = contextlib.ExitStack()
        substacks.append(es_c)
        SBC = lambda n, s, d: es_c.enter_context(nc.sbuf_tensor(n, s, d))
        kt = [SBC("kt0", [128, D], BF16), SBC("kt1", [128, D], BF16)]
        vt = [SBC("vt0", [128, D], BF16), SBC("vt1", [128, D], BF16)]
        qt = [SBC("qt0", [128, D], BF16), SBC("qt1", [128, D], BF16)]
        xt = [SBC("xt0", [128, D], F32), SBC("xt1", [128, D], F32)]
        kwb = SBC("kwb", [128, D], BF16)
        state_s = SBC("state_s", [128, D], F32)
        state_bf = SBC("state_bf", [128, D], BF16)
        kfeat = SBC("kfeat", [128, D], BF16)
        Lscr = [SBC("Lscr0", [128, D], F32), SBC("Lscr1", [128, D], F32)]
        dmb = [SBC("dmb0", [128, 512], F32), SBC("dmb1", [128, 512], F32)]
        eDb = [SBC("eDb0", [128, 512], F32), SBC("eDb1", [128, 512], F32)]
        ysb = [SBC("ysb0", [128, 512], F32), SBC("ysb1", [128, 512], F32)]
        atb = [SBC("atb0", [128, 512], BF16), SBC("atb1", [128, 512], BF16)]

        if STOP_AFTER == "clb":
            _bail()
            return nc, dbg_out

        p1_kld = []
        p1_vld = []
        p1_kw = []
        p1_up = []
        for tc in range(NCH):
            @block.sync
            def _(y, tc=tc, F=(C["tS"], C["vS"], C["sS"], C["qC"])):
                if tc == 0:
                    y.wait_ge(tS, F[0]); y.wait_ge(vS, F[1])
                    y.wait_ge(sS, F[2]); y.wait_ge(qC, F[3])
                if tc >= 2:
                    y.wait_ge(vS, p1_kw[tc - 2])
                y.dma_start(out=kt[tc % 2][:], in_=k_sp.ap()[tc]).then_inc(qS, 16)
                inc("qS", 16)
                p1_kld.append(C["qS"])
                if tc >= 2:
                    y.wait_ge(tS, p1_up[tc - 2][0])
                y.dma_start(out=vt[tc % 2][:], in_=v_sp.ap()[tc]).then_inc(qS, 16)
                inc("qS", 16)
                p1_vld.append(C["qS"])

            @block.vector
            def _(v, tc=tc):
                v.wait_ge(qS, p1_kld[tc])
                if tc == 0:
                    v.wait_ge(sS, WFAC)
                for h in range(NH):
                    v.tensor_scalar(out=kwb[:, h * 128:(h + 1) * 128],
                                    in0=kt[tc % 2][:, h * 128:(h + 1) * 128],
                                    scalar1=wfac_all[:, tc * 16 + h:tc * 16 + h + 1],
                                    scalar2=0.0, op0=ALU.mult, op1=ALU.add)
                v.drain().then_inc(vS, 1)
            inc("vS", 1)
            p1_kw.append(C["vS"])

            mm_marks = []
            for grp in range(4):
                bk_i = grp + 4 * (tc % 2)
                @block.tensor
                def _(t, tc=tc, grp=grp, bk_i=bk_i):
                    t.wait_ge(vS, p1_kw[tc])
                    t.wait_ge(qS, p1_vld[tc])
                    bwait(t, bk_i)
                    for hh in range(4):
                        h = grp * 4 + hh
                        t.matmul(pb[bk_i][:, hh * 128:(hh + 1) * 128],
                                 kwb[:, h * 128:(h + 1) * 128],
                                 vt[tc % 2][:, h * 128:(h + 1) * 128],
                                 start=True, stop=True)
                    t.drain().then_inc(tS, 1)
                inc("tS", 1)
                mm_marks.append(C["tS"])

                @block.vector
                def _(v, tc=tc, grp=grp, bk_i=bk_i, mm=mm_marks[grp]):
                    v.wait_ge(tS, mm)
                    for hh in range(4):
                        h = grp * 4 + hh
                        if tc == 0:
                            v.tensor_copy(state_s[:, h * 128:(h + 1) * 128],
                                          pb[bk_i][:, hh * 128:(hh + 1) * 128])
                        else:
                            v.scalar_tensor_tensor(
                                out=state_s[:, h * 128:(h + 1) * 128],
                                in0=state_s[:, h * 128:(h + 1) * 128],
                                scalar=td_b[:, tc * 16 + h:tc * 16 + h + 1],
                                in1=pb[bk_i][:, hh * 128:(hh + 1) * 128],
                                op0=ALU.mult, op1=ALU.add)
                    v.drain().then_inc(vS, 1)
                inc("vS", 1)
                bankfree[bk_i] = ("vS", C["vS"])
            p1_up.append((mm_marks[-1], C["vS"]))
        P1_DONE = C["vS"]

        if STOP_AFTER == "pass1":
            _bail()
            return nc, dbg_out

        @block.gpsimd
        def _(g):
            g.wait_ge(vS, P1_DONE)
            g.dma_start(out=cc_st_in.ap()[0:128 * D].rearrange("(p c) -> p c", p=128),
                        in_=state_s[:]).then_inc(qG, 16)
            inc("qG", 16)
            g.dma_start(out=cc_st_in.ap()[128 * D:128 * D + 16]
                        .rearrange("(o c) -> o c", o=1), in_=g_pub[:]).then_inc(qG, 16)
            inc("qG", 16)
            g.wait_ge(qG, C["qG"])
            g.collective_compute(
                "AllGather", ALU.bypass, replica_groups=[list(range(NB))],
                ins=[cc_st_in.ap().opt()], outs=[cc_st_out.ap().opt()],
            ).then_inc(cS, 1)
        inc("cS", 1)
        AG3 = C["cS"]

        if STOP_AFTER == "ag3":
            _bail()
            return nc, dbg_out

        @block.sync
        def _(y):
            y.wait_ge(cS, AG3)
            y.dma_start(out=gsb[:],
                        in_=cc_st_out.ap().rearrange("(r x) -> r x", r=NB)
                        [:, 128 * D:128 * D + 16]).then_inc(qS, 16)
        inc("qS", 16)
        GS3 = C["qS"]

        @block.tensor
        def _(t):
            t.wait_ge(qS, GS3)
            bwait(t, 6)
            t.matmul(pb[6][0:8, 0:16], selT_f[:], gsb[:], start=True, stop=True
                     ).then_inc(tS, 1)
        inc("tS", 1)
        SEL_MM = C["tS"]

        @block.scalar
        def _(s):
            s.wait_ge(tS, SEL_MM)
            s.activation(out=coef_sb[:], in_=pb[6][0:8, 0:16], func=AF.Exp
                         ).then_inc(sS, 1)
        inc("sS", 1)
        COEF_E = C["sS"]

        @block.vector
        def _(v):
            v.wait_ge(sS, COEF_E)
            v.tensor_scalar(out=coef_sb[:], in0=coef_sb[:], scalar1=amask_f[0:8, 0:1],
                            scalar2=0.0, op0=ALU.mult, op1=ALU.add)
            v.drain().then_inc(vS, 1)
        inc("vS", 1)
        COEF_M = C["vS"]
        bankfree[6] = ("vS", COEF_M)

        @block.vector
        def _(v):
            v.wait_ge(vS, COEF_M)
            for r in range(NB):
                v.tensor_copy(cmsk[:, r * 16:(r + 1) * 16], coef_sb[:, 0:16])
            v.drain()
            v.tensor_tensor(out=cmsk[:], in0=cmsk[:], in1=bm16_f[:], op=ALU.mult)
            v.drain().then_inc(vS, 1)
        inc("vS", 1)
        CMSK = C["vS"]

        @block.tensor
        def _(t):
            t.wait_ge(vS, CMSK)
            bwait(t, 7)
            t.matmul(pb[7][:, 0:128], onef[0:8, 0:128], cmsk[:], start=True, stop=True
                     ).then_inc(tS, 1)
        inc("tS", 1)
        CB_MM = C["tS"]

        @block.vector
        def _(v):
            v.wait_ge(tS, CB_MM)
            v.tensor_copy(coefB[:], pb[7][:, 0:128]).then_inc(vS, 1)
        inc("vS", 1)
        CB_EV = C["vS"]
        bankfree[7] = ("vS", CB_EV)

        if dbg:
            @block.scalar
            def _(s):
                s.wait_ge(vS, CB_EV)
                s.dma_start(out=dbg_out["d_coefb"], in_=coefB[:]).then_inc(qC, 16)
            inc("qC", 16)

        L_ld = []
        L_cb = []
        for r in range(NB):
            @block.sync
            def _(y, r=r):
                if r == 0:
                    y.wait_ge(cS, AG3)
                if r >= 2:
                    y.wait_ge(vS, L_cb[r - 2])
                y.dma_start(out=Lscr[r % 2][:],
                            in_=cc_st_out.ap()[r * ST_SZ:r * ST_SZ + 128 * D]
                            .rearrange("(p c) -> p c", p=128)).then_inc(qS, 16)
            inc("qS", 16)
            L_ld.append(C["qS"])

            @block.vector
            def _(v, r=r):
                v.wait_ge(qS, L_ld[r])
                if r == 0:
                    v.wait_ge(vS, CB_EV)
                for h in range(NH):
                    if r == 0:
                        v.tensor_scalar(out=state_s[:, h * 128:(h + 1) * 128],
                                        in0=Lscr[0][:, h * 128:(h + 1) * 128],
                                        scalar1=coefB[:, h:h + 1], scalar2=0.0,
                                        op0=ALU.mult, op1=ALU.add)
                    else:
                        v.scalar_tensor_tensor(
                            out=state_s[:, h * 128:(h + 1) * 128],
                            in0=Lscr[r % 2][:, h * 128:(h + 1) * 128],
                            scalar=coefB[:, r * 16 + h:r * 16 + h + 1],
                            in1=state_s[:, h * 128:(h + 1) * 128],
                            op0=ALU.mult, op1=ALU.add)
                v.drain().then_inc(vS, 1)
            inc("vS", 1)
            L_cb.append(C["vS"])
        SENTRY = C["vS"]

        if dbg:
            @block.scalar
            def _(s):
                s.wait_ge(vS, SENTRY)
                s.dma_start(out=dbg_out["d_state"], in_=state_s[:]).then_inc(qC, 16)
            inc("qC", 16)

            @block.gpsimd
            def _(g):
                g.wait_ge(qC, V_DONE)
                g.dma_start(out=dbg_out["d_q"].rearrange("(h s) t -> h s t", h=NH),
                            in_=q_sp.ap()).then_inc(qG, 16)
                g.dma_start(out=dbg_out["d_k"].rearrange("(c t) d -> c t d", c=NCH),
                            in_=k_sp.ap()).then_inc(qG, 16)
                g.dma_start(out=dbg_out["d_v"].rearrange("(c t) d -> c t d", c=NCH),
                            in_=v_sp.ap()).then_inc(qG, 16)
            inc("qG", 48)

        if STOP_AFTER == "sentry":
            _bail()
            return nc, dbg_out

        p2_kld = [None] * NCH
        p2_vld = [None] * NCH
        p2_qld = [None] * NCH
        p2_xld = [None] * NCH
        p2_kwm = [None] * NCH
        p2_ktr_use = [None] * NCH
        p2_mm_last = [None] * NCH
        p2_xst = [None] * NCH
        p2_yg_last = [None] * NCH
        dr1_prev = [None, None]
        for tc in range(NCH):
            @block.sync
            def _(y, tc=tc):
                if tc >= 2:
                    y.wait_ge(vS, p2_ktr_use[tc - 2])
                    y.wait_ge(tS, p2_mm_last[tc - 2])
                y.dma_start(out=kt[tc % 2][:], in_=k_sp.ap()[tc]).then_inc(qS, 16)
                inc("qS", 16)
                p2_kld[tc] = C["qS"]
                y.dma_start(out=vt[tc % 2][:], in_=v_sp.ap()[tc]).then_inc(qS, 16)
                inc("qS", 16)
                p2_vld[tc] = C["qS"]
                y.dma_start(out=qt[tc % 2][:].rearrange("p (h c) -> p h c", h=NH),
                            in_=q_sp.ap()[:, :, tc * 128:(tc + 1) * 128]
                            .rearrange("h s c -> s h c")).then_inc(qS, 16)
                inc("qS", 16)
                p2_qld[tc] = C["qS"]
                if tc >= 2:
                    y.wait_ge(qC, p2_xst[tc - 2])
                if tc == 0:
                    y.wait_ge(qC, XSPILL)
                y.dma_start(out=xt[tc % 2][:].rearrange("p (j c) -> p j c", j=NDC),
                            in_=x_sp.ap().rearrange("(j p) t -> p j t", p=128)
                            [:, :, tc * 128:(tc + 1) * 128]).then_inc(qS, 16)
                inc("qS", 16)
                p2_xld[tc] = C["qS"]

            @block.vector
            def _(v, tc=tc):
                if tc == 0:
                    v.wait_ge(vS, SENTRY)
                if tc >= 1 and p2_yg_last[tc - 1] is not None:
                    v.wait_ge(tS, p2_yg_last[tc - 1])
                v.tensor_copy(state_bf[:], state_s[:]).then_inc(vS, 1)
                inc("vS", 1)
                v.wait_ge(qS, p2_kld[tc])
                if tc >= 1 and p2_mm_last[tc - 1] is not None:
                    v.wait_ge(tS, p2_mm_last[tc - 1])
                for h in range(NH):
                    v.tensor_scalar(out=kwb[:, h * 128:(h + 1) * 128],
                                    in0=kt[tc % 2][:, h * 128:(h + 1) * 128],
                                    scalar1=wfac_all[:, tc * 16 + h:tc * 16 + h + 1],
                                    scalar2=0.0, op0=ALU.mult, op1=ALU.add)
                if dr1_prev[0] is not None:
                    v.wait_ge(tS, max(x for x in dr1_prev if x is not None))
                for rr in range(4):
                    v.tensor_copy(ctrep[:, rr * 128:(rr + 1) * 128],
                                  claT_s[:, tc * 128:(tc + 1) * 128])
                v.drain().then_inc(vS, 1)
            inc("vS", 1)
            p2_kwm[tc] = C["vS"]
            CAST_M = C["vS"] - 1

            for grp in range(4):
                pk = (tc * 4 + grp) % 2
                bA = pk * 4 + 0
                bB = pk * 4 + 1
                bC = pk * 4 + 2
                bD = pk * 4 + 3

                @block.tensor
                def _(t, tc=tc, grp=grp, bA=bA):
                    bwait(t, bA)
                    t.wait_ge(qS, p2_kld[tc])
                    for hh in range(4):
                        h = grp * 4 + hh
                        t.transpose(pb[bA][:].bitcast(BF16)[:, hh * 128:(hh + 1) * 128],
                                    kt[tc % 2][:, h * 128:(h + 1) * 128], id_bf[:])
                    t.drain().then_inc(tS, 1)
                inc("tS", 1)
                T1 = C["tS"]

                @block.vector
                def _(v, grp=grp, bA=bA, T1=T1):
                    v.wait_ge(tS, T1)
                    v.tensor_copy(kfeat[:, grp * 512:(grp + 1) * 512],
                                  pb[bA][:].bitcast(BF16)[:, 0:512]).then_inc(vS, 1)
                inc("vS", 1)
                V1 = C["vS"]

                @block.vector
                def _(v, tc=tc, grp=grp, pk=pk):
                    v.wait_ge(vS, p2_kwm[tc])
                    if dr1_prev[pk] is not None:
                        v.wait_ge(tS, dr1_prev[pk])
                    v.tensor_tensor(out=dr1[pk][:], in0=ctrep[:],
                                    in1=psel_f[0:16, grp * 512:(grp + 1) * 512]
                                    .rearrange("k (h c) -> k h c", h=4)
                                    if False else
                                    psel_f[0:16, grp * 512:(grp + 1) * 512],
                                    op=ALU.mult)
                    v.drain().then_inc(vS, 1)
                inc("vS", 1)
                VR1 = C["vS"]

                @block.tensor
                def _(t, tc=tc, grp=grp, pk=pk, bB=bB, bC=bC, V1=V1, VR1=VR1):
                    bwait(t, bB)
                    t.wait_ge(vS, V1)
                    t.wait_ge(qS, p2_qld[tc])
                    for hh in range(4):
                        h = grp * 4 + hh
                        t.matmul(pb[bB][:, hh * 128:(hh + 1) * 128],
                                 kfeat[:, grp * 512 + hh * 128: grp * 512 + (hh + 1) * 128],
                                 qt[tc % 2][:, h * 128:(h + 1) * 128],
                                 start=True, stop=True)
                    t.drain().then_inc(tS, 1)
                    inc("tS", 1)
                    bwait(t, bC)
                    t.wait_ge(vS, VR1)
                    t.matmul(pb[bC][:, 0:512], onef[0:16, 0:128], dr1[pk][:],
                             start=True, stop=False)
                    t.matmul(pb[bC][:, 0:512], nclbT_s[0:16, tc * 128:(tc + 1) * 128],
                             psel_f[0:16, grp * 512:(grp + 1) * 512],
                             start=False, stop=True)
                    t.drain().then_inc(tS, 1)
                inc("tS", 1)
                T23 = C["tS"]
                dr1_prev[pk] = T23

                @block.vector
                def _(v, pk=pk, bC=bC, T23=T23):
                    v.wait_ge(tS, T23)
                    v.tensor_tensor(out=dmb[pk][:], in0=pb[bC][:, :], in1=m0_f[:],
                                    op=ALU.add).then_inc(vS, 1)
                inc("vS", 1)
                V2 = C["vS"]

                @block.scalar
                def _(s, pk=pk, V2=V2):
                    s.wait_ge(vS, V2)
                    s.activation(out=eDb[pk][:], in_=dmb[pk][:], func=AF.Exp
                                 ).then_inc(sS, 1)
                inc("sS", 1)
                S1 = C["sS"]

                @block.vector
                def _(v, pk=pk, bB=bB, S1=S1):
                    v.wait_ge(sS, S1)
                    v.tensor_tensor(out=atb[pk][:], in0=pb[bB][:, :], in1=eDb[pk][:],
                                    op=ALU.mult).then_inc(vS, 1)
                inc("vS", 1)
                V3 = C["vS"]
                if dbg and tc == 0 and grp == 0:
                    @block.gpsimd
                    def _(g, pk=pk, V3=V3):
                        g.wait_ge(vS, V3)
                        g.dma_start(out=dbg_out["d_dm"], in_=dmb[pk][:]).then_inc(qG, 16)
                        g.dma_start(out=dbg_out["d_eD"], in_=eDb[pk][:]).then_inc(qG, 16)
                        g.dma_start(out=dbg_out["d_at"], in_=atb[pk][:]).then_inc(qG, 16)
                    inc("qG", 48)

                @block.tensor
                def _(t, tc=tc, grp=grp, pk=pk, bA=bA, bB=bB, bC=bC, V3=V3, CAST_M=CAST_M):
                    t.wait_ge(vS, V3)
                    t.wait_ge(qS, p2_vld[tc])
                    for hh in range(4):
                        h = grp * 4 + hh
                        t.matmul(pb[bA][:, hh * 128:(hh + 1) * 128],
                                 atb[pk][:, hh * 128:(hh + 1) * 128],
                                 vt[tc % 2][:, h * 128:(h + 1) * 128],
                                 start=True, stop=True)
                    t.drain().then_inc(tS, 1)
                    inc("tS", 1)
                    t.wait_ge(vS, CAST_M)
                    for hh in range(4):
                        h = grp * 4 + hh
                        t.matmul(pb[bC][:, hh * 128:(hh + 1) * 128],
                                 qt[tc % 2][:, h * 128:(h + 1) * 128],
                                 state_bf[:, h * 128:(h + 1) * 128],
                                 start=True, stop=True)
                    t.drain().then_inc(tS, 1)
                    inc("tS", 1)
                    t.wait_ge(vS, p2_kwm[tc])
                    for hh in range(4):
                        h = grp * 4 + hh
                        t.matmul(pb[bB][:, hh * 128:(hh + 1) * 128],
                                 kwb[:, h * 128:(h + 1) * 128],
                                 vt[tc % 2][:, h * 128:(h + 1) * 128],
                                 start=True, stop=True)
                    t.drain().then_inc(tS, 1)
                inc("tS", 1)
                T456 = C["tS"]
                p2_yg_last[tc] = T456 - 1

                @block.vector
                def _(v, tc=tc, grp=grp, pk=pk, bA=bA, bC=bC, T456=T456):
                    v.wait_ge(tS, T456 - 1)
                    for hh in range(4):
                        h = grp * 4 + hh
                        v.tensor_scalar(out=ysb[pk][:, hh * 128:(hh + 1) * 128],
                                        in0=pb[bC][:, hh * 128:(hh + 1) * 128],
                                        scalar1=cd_all[:, tc * 16 + h:tc * 16 + h + 1],
                                        scalar2=0.0, op0=ALU.mult, op1=ALU.add)
                    for hh in range(4):
                        v.scalar_tensor_tensor(
                            out=ysb[pk][:, hh * 128:(hh + 1) * 128],
                            in0=pb[bA][:, hh * 128:(hh + 1) * 128], scalar=1.0,
                            in1=ysb[pk][:, hh * 128:(hh + 1) * 128],
                            op0=ALU.mult, op1=ALU.add)
                    v.drain().then_inc(vS, 1)
                inc("vS", 1)
                V4 = C["vS"]
                bankfree[bA] = ("vS", V4)
                bankfree[bC] = ("vS", V4)

                if dbg and tc == 0 and grp == 0:
                    @block.gpsimd
                    def _(g, pk=pk, V4=V4):
                        g.wait_ge(vS, V4)
                        g.dma_start(out=dbg_out["d_ysb"], in_=ysb[pk][:]).then_inc(qG, 16)
                    inc("qG", 16)

                @block.tensor
                def _(t, pk=pk, bD=bD, V4=V4):
                    bwait(t, bD)
                    t.wait_ge(vS, V4)
                    for hh in range(4):
                        t.transpose(pb[bD][:, hh * 128:(hh + 1) * 128],
                                    ysb[pk][:, hh * 128:(hh + 1) * 128], id_f[:])
                    t.drain().then_inc(tS, 1)
                inc("tS", 1)
                T7 = C["tS"]

                @block.vector
                def _(v, tc=tc, grp=grp, pk=pk, bB=bB, bD=bD, T7=T7):
                    v.wait_ge(tS, T7)
                    v.wait_ge(qS, p2_xld[tc])
                    for hh in range(4):
                        h = grp * 4 + hh
                        v.scalar_tensor_tensor(
                            out=xt[tc % 2][:, h * 128:(h + 1) * 128],
                            in0=pb[bD][:, hh * 128:(hh + 1) * 128],
                            scalar=ga_t[:, h:h + 1],
                            in1=xt[tc % 2][:, h * 128:(h + 1) * 128],
                            op0=ALU.mult, op1=ALU.add)
                    v.drain().then_inc(vS, 1)
                    inc("vS", 1)
                    for hh in range(4):
                        h = grp * 4 + hh
                        v.scalar_tensor_tensor(
                            out=state_s[:, h * 128:(h + 1) * 128],
                            in0=state_s[:, h * 128:(h + 1) * 128],
                            scalar=td_b[:, tc * 16 + h:tc * 16 + h + 1],
                            in1=pb[bB][:, hh * 128:(hh + 1) * 128],
                            op0=ALU.mult, op1=ALU.add)
                    v.drain().then_inc(vS, 1)
                inc("vS", 1)
                V56 = C["vS"]
                bankfree[bB] = ("vS", V56)
                bankfree[bD] = ("vS", V56 - 1)
                if grp == 3:
                    p2_ktr_use[tc] = p2_kwm[tc]
                    p2_mm_last[tc] = T456

            @block.scalar
            def _(s, tc=tc):
                s.wait_ge(vS, C["vS"] - 1)
                s.dma_start(out=x_sp.ap().rearrange("(j p) t -> p j t", p=128)
                            [:, :, tc * 128:(tc + 1) * 128],
                            in_=xt[tc % 2][:].rearrange("p (j c) -> p j c", j=NDC)
                            ).then_inc(qC, 16)
            inc("qC", 16)
            p2_xst[tc] = C["qC"]
        P2_DONE = C["qC"]
        es_c.close()
        es_a.close()

        if STOP_AFTER == "pass2":
            _bail()
            return nc, dbg_out

        es_q = contextlib.ExitStack()
        substacks.append(es_q)
        xq_bf = es_q.enter_context(nc.sbuf_tensor("xq_bf", [128, NDC * 1024], BF16))
        r8 = es_q.enter_context(nc.sbuf_tensor("r8", [8, 1024], F32))
        sb2_ball = es_q.enter_context(nc.sbuf_tensor("sb2_ball", [128, 1024], F32))
        fr4 = es_q.enter_context(nc.sbuf_tensor("fr4", [4, 512], F32))
        sr4 = es_q.enter_context(nc.sbuf_tensor("sr4", [4, 512], F32))
        es_l = contextlib.ExitStack()
        substacks.append(es_l)
        x1_f = es_l.enter_context(nc.sbuf_tensor("x1_f", [128, NDC * 1024], F32))
        x1_bf = es_l.enter_context(nc.sbuf_tensor("x1_bf", [128, NDC * 1024], BF16))
        x2c = [es_l.enter_context(nc.sbuf_tensor("x2c%d" % i, [128, 1024], BF16))
               for i in range(2)]
        rowC2 = es_l.enter_context(nc.sbuf_tensor("rowC2", [1, 1024], F32))
        i8rep = es_l.enter_context(nc.sbuf_tensor("i8rep", [8, 1024], F32))

        @block.sync
        def _(y, F=(C["tS"], C["vS"], C["sS"])):
            y.wait_ge(qC, P2_DONE)
            y.wait_ge(tS, F[0]); y.wait_ge(vS, F[1]); y.wait_ge(sS, F[2])
            y.dma_start(out=x1_f[:].rearrange("p (j t) -> p j t", j=NDC),
                        in_=x_sp.ap().rearrange("(j p) t -> p j t", p=128)
                        ).then_inc(qS, 16)
        inc("qS", 16)
        X1LD = C["qS"]

        @block.gpsimd
        def _(g):
            g.wait_ge(qC, P2_DONE)
            g.dma_start(out=x1_bf[:].rearrange("p (j t) -> p j t", j=NDC),
                        in_=x_sp.ap().rearrange("(j p) t -> p j t", p=128)
                        ).then_inc(qG, 16)
        inc("qG", 16)
        X1BLD = C["qG"]

        sq2 = []
        st2 = []
        for j in range(NDC):
            @block.scalar
            def _(s, j=j):
                if j == 0:
                    s.wait_ge(qS, X1LD)
                if j >= 2:
                    s.wait_ge(tS, st2[j - 2])
                s.activation(out=x2c[j % 2][:], in_=x1_f[:, j * 1024:(j + 1) * 1024],
                             func=AF.Square).then_inc(sS, 1)
            inc("sS", 1)
            sq2.append(C["sS"])

            @block.tensor
            def _(t, j=j):
                if j == 0:
                    t.wait_ge(qG, X1BLD)
                    bwait(t, 4, 5, 6, 7)
                t.wait_ge(sS, sq2[j])
                for tt in range(2):
                    t.matmul(pb[4 + tt][0:1, :], onebf_c[:, 0:1],
                             x1_bf[:, j * 1024 + tt * 512: j * 1024 + (tt + 1) * 512],
                             start=(j == 0), stop=(j == NDC - 1))
                    t.matmul(pb[6 + tt][0:1, :], onebf_c[:, 0:1],
                             x2c[j % 2][:, tt * 512:(tt + 1) * 512],
                             start=(j == 0), stop=(j == NDC - 1))
                t.drain().then_inc(tS, 1)
            inc("tS", 1)
            st2.append(C["tS"])
        STATS2 = C["tS"]

        @block.vector
        def _(v):
            v.wait_ge(tS, STATS2)
            v.tensor_scalar(out=rowA[0:1, 0:512], in0=pb[4][0:1, :], scalar1=1.0 / 2048.0,
                            scalar2=0.0, op0=ALU.mult, op1=ALU.add)
            v.tensor_scalar(out=rowA[0:1, 512:1024], in0=pb[5][0:1, :], scalar1=1.0 / 2048.0,
                            scalar2=0.0, op0=ALU.mult, op1=ALU.add)
            v.tensor_scalar(out=rowB[0:1, 0:512], in0=pb[6][0:1, :], scalar1=1.0 / 2048.0,
                            scalar2=0.0, op0=ALU.mult, op1=ALU.add)
            v.tensor_scalar(out=rowB[0:1, 512:1024], in0=pb[7][0:1, :], scalar1=1.0 / 2048.0,
                            scalar2=0.0, op0=ALU.mult, op1=ALU.add)
            v.drain()
            v.tensor_tensor(out=rowC2[:], in0=rowA[:], in1=rowA[:], op=ALU.mult)
            v.drain()
            v.tensor_tensor(out=rowB[:], in0=rowB[:], in1=rowC2[:], op=ALU.subtract)
            v.drain().then_inc(vS, 1)
        inc("vS", 1)
        VAR2 = C["vS"]
        for b_ in (4, 5, 6, 7):
            bankfree[b_] = ("vS", VAR2)

        @block.scalar
        def _(s):
            s.wait_ge(vS, VAR2)
            s.activation(out=rowC2[:], in_=rowB[:], func=AF.Ln, bias=cEPS5[0:1, 0:1])
            s.activation(out=rowB[:], in_=rowC2[:], func=AF.Exp, scale=-0.5)
            s.drain().then_inc(sS, 1)
        inc("sS", 1)
        ISTD2 = C["sS"]

        @block.tensor
        def _(t):
            t.wait_ge(sS, ISTD2)
            bwait(t, 0, 1, 2, 3)
            for tt in range(2):
                t.matmul(pb[0 + tt][:, :], onef[0:1, 0:128],
                         rowA[0:1, tt * 512:(tt + 1) * 512], start=True, stop=True)
                t.matmul(pb[2 + tt][:, :], onef[0:1, 0:128],
                         rowB[0:1, tt * 512:(tt + 1) * 512], start=True, stop=True)
            t.drain().then_inc(tS, 1)
        inc("tS", 1)
        BC2 = C["tS"]

        @block.vector
        def _(v):
            v.wait_ge(tS, BC2)
            v.tensor_copy(stage0[:, 0:512], pb[0][:, :])
            v.tensor_copy(stage0[:, 512:1024], pb[1][:, :])
            v.tensor_copy(stage1[:, 0:512], pb[2][:, :])
            v.tensor_copy(stage1[:, 512:1024], pb[3][:, :])
            v.drain().then_inc(vS, 1)
        inc("vS", 1)
        MB2 = C["vS"]
        for b_ in (0, 1, 2, 3):
            bankfree[b_] = ("vS", MB2)

        for j in range(NDC):
            @block.vector
            def _(v, j=j):
                if j == 0:
                    v.wait_ge(vS, MB2)
                cs = slice(j * 1024, (j + 1) * 1024)
                v.tensor_tensor(out=stage2[:], in0=x1_f[:, cs], in1=stage0[:],
                                op=ALU.subtract)
                v.tensor_tensor(out=stage2[:], in0=stage2[:], in1=stage1[:], op=ALU.mult)
                v.tensor_scalar(out=x1_f[:, cs], in0=stage2[:],
                                scalar1=cbmod[:, j:j + 1], scalar2=mymod[:, 48 + j:49 + j],
                                op0=ALU.mult, op1=ALU.add)
                v.drain().then_inc(vS, 1)
            inc("vS", 1)
        H2_DONE = C["vS"]

        mxacc = []
        for tsub in range(NCH):
            for jg in range(4):
                bk_i = jg % 2
                @block.tensor
                def _(t, tsub=tsub, jg=jg, bk_i=bk_i):
                    if tsub == 0 and jg == 0:
                        t.wait_ge(vS, H2_DONE)
                    bwait(t, bk_i)
                    for i in range(4):
                        j = jg * 4 + i
                        t.transpose(pb[bk_i][:, i * 128:(i + 1) * 128],
                                    x1_f[:, j * 1024 + tsub * 128: j * 1024 + (tsub + 1) * 128],
                                    id_f[:])
                    t.drain().then_inc(tS, 1)
                inc("tS", 1)
                mtr2 = C["tS"]

                @block.vector
                def _(v, tsub=tsub, jg=jg, bk_i=bk_i, mtr2=mtr2):
                    v.wait_ge(tS, mtr2)
                    v.tensor_reduce(out=gtmp[:, 8:9], in_=pb[bk_i][:, :], axis=AX,
                                    op=ALU.max, apply_absolute_value=True)
                    v.drain()
                    if jg == 0:
                        v.tensor_copy(gtmp[:, tsub:tsub + 1], gtmp[:, 8:9])
                    else:
                        v.tensor_tensor(out=gtmp[:, tsub:tsub + 1],
                                        in0=gtmp[:, tsub:tsub + 1], in1=gtmp[:, 8:9],
                                        op=ALU.max)
                    v.drain().then_inc(vS, 1)
                inc("vS", 1)
                bankfree[bk_i] = ("vS", C["vS"])
        MX_DONE = C["vS"]

        if dbg:
            @block.scalar
            def _(s):
                s.wait_ge(vS, MX_DONE)
                s.dma_start(out=dbg_out["d_mx"], in_=gtmp[:, 0:8]).then_inc(qC, 16)
            inc("qC", 16)

        @block.vector
        def _(v):
            v.wait_ge(vS, MX_DONE)
            v.tensor_scalar(out=wsmall[:, 0:8], in0=gtmp[:, 0:8], scalar1=1.0 / 127.0,
                            scalar2=1e-5, op0=ALU.mult, op1=ALU.add)
            v.drain().then_inc(vS, 1)
        inc("vS", 1)
        S2COL = C["vS"]

        @block.tensor
        def _(t):
            t.wait_ge(vS, S2COL)
            bwait(t, 2)
            t.matmul(pb[2][0:8, 0:128], wsmall[:, 0:8], id_f[:], start=True, stop=True
                     ).then_inc(tS, 1)
        inc("tS", 1)
        S2T_MM = C["tS"]

        @block.vector
        def _(v):
            v.wait_ge(tS, S2T_MM)
            v.tensor_copy(r8[:, 0:128], pb[2][0:8, 0:128]).then_inc(vS, 1)
            inc("vS", 1)
            v.drain()
            v.reciprocal(r8[:, 128:256], r8[:, 0:128])
            v.drain().then_inc(vS, 1)
        inc("vS", 1)
        INV2 = C["vS"]
        bankfree[2] = ("vS", INV2)

        if dbg:
            @block.scalar
            def _(s):
                s.wait_ge(vS, INV2)
                s.dma_start(out=dbg_out["d_s2"], in_=r8[0:8, 0:128]).then_inc(qC, 16)
            inc("qC", 16)

        @block.vector
        def _(v):
            v.wait_ge(vS, INV2)
            for rr in range(8):
                v.tensor_copy(i8rep[:, rr * 128:(rr + 1) * 128], r8[:, 128:256])
            v.drain()
            v.tensor_tensor(out=i8rep[:], in0=i8rep[:], in1=pos8_f[:], op=ALU.mult)
            v.drain().then_inc(vS, 1)
        inc("vS", 1)
        I8R = C["vS"]

        @block.tensor
        def _(t):
            t.wait_ge(vS, I8R)
            bwait(t, 0, 1)
            t.matmul(pb[0][:, :], onef[0:8, 0:128], i8rep[:, 0:512], start=True, stop=True)
            t.matmul(pb[1][:, :], onef[0:8, 0:128], i8rep[:, 512:1024], start=True, stop=True)
            t.drain().then_inc(tS, 1)
        inc("tS", 1)
        I2B_MM = C["tS"]

        @block.vector
        def _(v):
            v.wait_ge(tS, I2B_MM)
            v.tensor_copy(stage0[:, 0:512], pb[0][:, :])
            v.tensor_copy(stage0[:, 512:1024], pb[1][:, :])
            for rr in range(8):
                v.tensor_copy(i8rep[:, rr * 128:(rr + 1) * 128], r8[:, 0:128])
            v.drain()
            v.tensor_tensor(out=i8rep[:], in0=i8rep[:], in1=pos8_f[:], op=ALU.mult)
            v.drain().then_inc(vS, 1)
        inc("vS", 1)
        I2B = C["vS"]
        bankfree[0] = ("vS", I2B)
        bankfree[1] = ("vS", I2B)

        @block.tensor
        def _(t):
            t.wait_ge(vS, I2B)
            bwait(t, 2, 3)
            t.matmul(pb[2][:, :], onef[0:8, 0:128], i8rep[:, 0:512], start=True, stop=True)
            t.matmul(pb[3][:, :], onef[0:8, 0:128], i8rep[:, 512:1024], start=True, stop=True)
            t.drain().then_inc(tS, 1)
        inc("tS", 1)
        S2B_MM = C["tS"]

        @block.vector
        def _(v):
            v.wait_ge(tS, S2B_MM)
            v.tensor_copy(sb2_ball[:, 0:512], pb[2][:, :])
            v.tensor_copy(sb2_ball[:, 512:1024], pb[3][:, :])
            v.drain().then_inc(vS, 1)
        inc("vS", 1)
        S2BALL = C["vS"]
        bankfree[2] = ("vS", S2BALL)
        bankfree[3] = ("vS", S2BALL)

        for j in range(NDC):
            @block.vector
            def _(v, j=j):
                if j == 0:
                    v.wait_ge(vS, I2B)
                cs = slice(j * 1024, (j + 1) * 1024)
                v.tensor_tensor(out=stage2[:], in0=x1_f[:, cs], in1=stage0[:], op=ALU.mult)
                v.tensor_scalar(out=xq_bf[:, cs], in0=stage2[:], scalar1=MAGIC,
                                scalar2=MAGIC, op0=ALU.add, op1=ALU.subtract)
                v.drain().then_inc(vS, 1)
            inc("vS", 1)
        XQ_DONE = C["vS"]

        if dbg:
            @block.gpsimd
            def _(g):
                g.wait_ge(qC, P2_DONE)
                g.dma_start(out=dbg_out["d_x1"], in_=x_sp.ap()).then_inc(qG, 16)
            inc("qG", 16)

            @block.gpsimd
            def _(g):
                g.wait_ge(vS, XQ_DONE)
                g.dma_start(out=dbg_out["d_xq"].rearrange("(j p) t -> p j t", p=128),
                            in_=xq_bf[:].rearrange("p (j t) -> p j t", j=NDC)).then_inc(qG, 16)
            inc("qG", 16)
        es_l.close()

        if STOP_AFTER == "ln2":
            _bail()
            return nc, dbg_out

        es_f1 = contextlib.ExitStack()
        substacks.append(es_f1)
        gu0 = es_f1.enter_context(nc.sbuf_tensor("gu0", [128, 64 * 512], FP16))
        guq = gu0[:].bitcast(BF16)

        for th in range(2):
            es_f2 = contextlib.ExitStack()
            substacks.append(es_f2)
            wgb = [[es_f2.enter_context(nc.sbuf_tensor("wgb%d%d_%d" % (i, mtx, th),
                                                       [128, 2048], BF16))
                    for mtx in range(2)] for i in range(2)]
            g2buf = es_f2.enter_context(nc.sbuf_tensor("g2buf_%d" % th, [128, 512], F32))

            @block.vector
            def _(v, th=th):
                if th == 0:
                    v.wait_ge(vS, XQ_DONE)
                v.tensor_scalar(out=stage1[:, 0:512],
                                in0=sb2_ball[:, th * 512:(th + 1) * 512],
                                scalar1=SCAL[:, 0:1], scalar2=0.0,
                                op0=ALU.mult, op1=ALU.add)
                v.tensor_scalar(out=stage0[:, 0:512],
                                in0=sb2_ball[:, th * 512:(th + 1) * 512],
                                scalar1=SCAL[:, 1:2], scalar2=0.0,
                                op0=ALU.mult, op1=ALU.add)
                v.drain().then_inc(vS, 1)
            inc("vS", 1)
            S2GB = C["vS"]

            @block.gpsimd
            def _(g, th=th):
                if th == 0:
                    g.wait_ge(cS, AG2)
                g.sem_inc(qG, 1)
            inc("qG", 1)
            AG2W = C["qG"]

            wg_ld = [None] * 64
            chain_m = [None] * 64
            for ic in range(64):
                r = ic // 8
                cb = (ic % 8) * 128

                @block.sync
                def _(y, ic=ic, r=r, cb=cb, th=th,
                      F=(C["tS"], C["vS"], C["sS"], C["qC"])):
                    if ic == 0:
                        y.wait_ge(tS, F[0]); y.wait_ge(vS, F[1])
                        y.wait_ge(sS, F[2]); y.wait_ge(qC, F[3])
                        y.wait_ge(cS, AG2)
                    if ic >= 2 and chain_m[ic - 2] is not None:
                        y.wait_ge(tS, chain_m[ic - 2][0])
                    for mtx in range(2):
                        base = r * WQIN_SZ + mtx * WG_SZ
                        src = (cc_wq_out.ap()[base:base + WG_SZ]
                               .rearrange("(j p i) -> p j i", j=NDC, p=128)
                               [:, :, cb:cb + 128])
                        y.dma_start(out=wgb[ic % 2][mtx][:]
                                    .rearrange("p (j c) -> p j c", j=NDC),
                                    in_=src).then_inc(qS, 16)
                        inc("qS", 16)
                wg_ld[ic] = C["qS"]

                bset = (ic % 2) * 2 + 4
                @block.tensor
                def _(t, ic=ic, th=th, bset=bset):
                    t.wait_ge(qS, wg_ld[ic])
                    if ic == 0:
                        t.wait_ge(vS, S2GB)
                        t.wait_ge(qG, AG2W)
                    bwait(t, bset, bset + 1)
                    for mtx in range(2):
                        for j in range(NDC):
                            t.matmul(pb[bset + mtx][:, :],
                                     wgb[ic % 2][mtx][:, j * 128:(j + 1) * 128],
                                     xq_bf[:, j * 1024 + th * 512: j * 1024 + (th + 1) * 512],
                                     start=(j == 0), stop=(j == NDC - 1))
                    t.drain().then_inc(tS, 1)
                inc("tS", 1)
                GU_MM = C["tS"]

                @block.vector
                def _(v, ic=ic, bset=bset, GU_MM=GU_MM):
                    v.wait_ge(tS, GU_MM)
                    v.tensor_tensor(out=stage2[:, 0:512], in0=pb[bset][:, :],
                                    in1=stage1[:, 0:512], op=ALU.mult).then_inc(vS, 1)
                inc("vS", 1)
                G1 = C["vS"]

                @block.scalar
                def _(s, G1=G1):
                    s.wait_ge(vS, G1)
                    s.activation(out=g2buf[:], in_=stage2[:, 0:512], func=AF.Silu
                                 ).then_inc(sS, 1)
                inc("sS", 1)
                SIL = C["sS"]

                @block.vector
                def _(v, ic=ic, bset=bset, SIL=SIL):
                    v.wait_ge(sS, SIL)
                    v.tensor_tensor(out=stage2[:, 512:1024], in0=pb[bset + 1][:, :],
                                    in1=stage0[:, 0:512], op=ALU.mult)
                    v.tensor_tensor(out=gu0[:, ic * 512:(ic + 1) * 512], in0=g2buf[:],
                                    in1=stage2[:, 512:1024], op=ALU.mult).then_inc(vS, 1)
                inc("vS", 1)
                GU0 = C["vS"]
                bankfree[bset] = ("vS", GU0)
                bankfree[bset + 1] = ("vS", GU0)

                @block.scalar
                def _(s, ic=ic, GU0=GU0):
                    s.wait_ge(vS, GU0)
                    s.activation(out=g2buf[:], in_=gu0[:, ic * 512:(ic + 1) * 512],
                                 func=AF.Abs).then_inc(sS, 1)
                inc("sS", 1)
                ABSM = C["sS"]

                @block.vector
                def _(v, ic=ic, ABSM=ABSM):
                    v.wait_ge(sS, ABSM)
                    if ic == 0:
                        v.tensor_copy(stage1[:, 512:1024], g2buf[:])
                    else:
                        v.tensor_tensor(out=stage1[:, 512:1024], in0=stage1[:, 512:1024],
                                        in1=g2buf[:], op=ALU.max)
                    v.drain().then_inc(vS, 1)
                inc("vS", 1)
                chain_m[ic] = (GU_MM, C["vS"])
            GUP_DONE = C["vS"]

            mx_ev = []
            for tb in range(4):
                bk_i = tb % 2
                @block.tensor
                def _(t, tb=tb, bk_i=bk_i):
                    if tb == 0:
                        t.wait_ge(vS, GUP_DONE)
                    bwait(t, bk_i)
                    t.transpose(pb[bk_i][:, 0:128],
                                stage1[:, 512 + tb * 128: 512 + (tb + 1) * 128], id_f[:]
                                ).then_inc(tS, 1)
                inc("tS", 1)
                mtb = C["tS"]

                @block.vector
                def _(v, tb=tb, bk_i=bk_i, mtb=mtb):
                    v.wait_ge(tS, mtb)
                    v.tensor_reduce(out=wsmall[:, 8 + tb:9 + tb] if False else gtmp[:, 16 + tb:17 + tb],
                                    in_=pb[bk_i][:, 0:128], axis=AX, op=ALU.max)
                    v.drain().then_inc(vS, 1)
                inc("vS", 1)
                mx_ev.append(C["vS"])
                bankfree[bk_i] = ("vS", C["vS"])
            MX4 = C["vS"]

            @block.vector
            def _(v, th=th):
                v.wait_ge(vS, MX4)
                v.tensor_scalar(out=gtmp[:, 24:28], in0=gtmp[:, 16:20],
                                scalar1=1.0 / 127.0, scalar2=1e-5,
                                op0=ALU.mult, op1=ALU.add)
                v.drain()
                v.reciprocal(gtmp[:, 16:20], gtmp[:, 24:28])
                v.drain().then_inc(vS, 1)
            inc("vS", 1)
            F4COL = C["vS"]

            @block.tensor
            def _(t):
                t.wait_ge(vS, F4COL)
                bwait(t, 2)
                t.matmul(pb[2][0:4, 0:128], gtmp[:, 16:20], id_f[:], start=True, stop=True)
                t.matmul(pb[2][0:4, 128:256], gtmp[:, 24:28], id_f[:], start=True, stop=True)
                t.drain().then_inc(tS, 1)
            inc("tS", 1)
            F4T_MM = C["tS"]

            @block.vector
            def _(v, F4T_MM=F4T_MM):
                v.wait_ge(tS, F4T_MM)
                v.tensor_copy(r8[0:4, 768:896], pb[2][0:4, 0:128])
                v.tensor_copy(r8[0:4, 512:640], pb[2][0:4, 128:256])
                v.drain().then_inc(vS, 1)
            inc("vS", 1)
            F4ROWS = C["vS"]
            bankfree[2] = ("vS", F4ROWS)

            if dbg and th == 0:
                @block.scalar
                def _(s):
                    s.wait_ge(vS, F4ROWS)
                    s.dma_start(out=dbg_out["d_f4"][0:4, :], in_=r8[0:4, 768:896]
                                ).then_inc(qC, 16)
                inc("qC", 16)

            @block.vector
            def _(v):
                v.wait_ge(vS, F4ROWS)
                for rr in range(4):
                    v.tensor_copy(fr4[:, rr * 128:(rr + 1) * 128], r8[0:4, 768:896])
                    v.tensor_copy(sr4[:, rr * 128:(rr + 1) * 128], r8[0:4, 512:640])
                v.drain()
                v.tensor_tensor(out=fr4[:], in0=fr4[:], in1=pos8_f[0:4, 0:512], op=ALU.mult)
                v.tensor_tensor(out=sr4[:], in0=sr4[:], in1=pos8_f[0:4, 0:512], op=ALU.mult)
                v.drain().then_inc(vS, 1)
            inc("vS", 1)
            FR4M = C["vS"]

            @block.tensor
            def _(t, FR4M=FR4M):
                t.wait_ge(vS, FR4M)
                bwait(t, 0, 1)
                t.matmul(pb[0][:, :], onef[0:4, 0:128], fr4[:], start=True, stop=True)
                t.matmul(pb[1][:, :], onef[0:4, 0:128], sr4[:], start=True, stop=True)
                t.drain().then_inc(tS, 1)
            inc("tS", 1)
            F4B_MM = C["tS"]

            @block.vector
            def _(v, F4B_MM=F4B_MM):
                v.wait_ge(tS, F4B_MM)
                v.tensor_copy(stage0[:, 0:512], pb[0][:, :])
                v.tensor_scalar(out=stage1[:, 0:512], in0=pb[1][:, :],
                                scalar1=SCAL[:, 2:3], scalar2=0.0,
                                op0=ALU.mult, op1=ALU.add)
                v.drain().then_inc(vS, 1)
            inc("vS", 1)
            F4B = C["vS"]
            bankfree[0] = ("vS", F4B)
            bankfree[1] = ("vS", F4B)

            rnd_m = []
            for ic in range(64):
                @block.vector
                def _(v, ic=ic):
                    if ic == 0:
                        v.wait_ge(vS, F4B)
                    cs = slice(ic * 512, (ic + 1) * 512)
                    v.tensor_tensor(out=stage2[:, 0:512], in0=gu0[:, cs],
                                    in1=stage0[:, 0:512], op=ALU.mult)
                    v.tensor_scalar(out=guq[:, cs], in0=stage2[:, 0:512], scalar1=MAGIC,
                                    scalar2=MAGIC, op0=ALU.add, op1=ALU.subtract)
                    v.drain().then_inc(vS, 1)
                inc("vS", 1)
                rnd_m.append(C["vS"])
            RND_DONE = C["vS"]

            es_f2.close()

            es_f3 = contextlib.ExitStack()
            substacks.append(es_f3)
            wdb = [es_f3.enter_context(nc.sbuf_tensor("wdb%d_%d" % (i, th), [128, 8192], BF16))
                   for i in range(2)]
            xj = [es_f3.enter_context(nc.sbuf_tensor("xj%d_%d" % (i, th), [128, 512], F32))
                  for i in range(2)]
            outst = [es_f3.enter_context(nc.sbuf_tensor("outst%d_%d" % (i, th), [128, 512], F32))
                     for i in range(2)]

            wd_ld = [None] * NDC
            dn_ev = [None] * NDC
            dn_mm = [None] * NDC
            dn_st = [None] * NDC
            for j in range(NDC):
                @block.sync
                def _(y, j=j, th=th, F=(C["tS"], C["vS"], C["sS"])):
                    if j == 0:
                        y.wait_ge(tS, F[0]); y.wait_ge(vS, F[1]); y.wait_ge(sS, F[2])
                    if j >= 2:
                        y.wait_ge(tS, dn_mm[j - 2])
                    for r in range(NB):
                        base = r * WQIN_SZ + 2 * WG_SZ + j * 131072
                        y.dma_start(
                            out=wdb[j % 2][:, r * 1024:(r + 1) * 1024]
                            .rearrange("p (il c) -> p il c", il=8),
                            in_=cc_wq_out.ap()[base:base + 131072]
                            .rearrange("(il p c) -> p il c", il=8, p=128)
                            ).then_inc(qS, 16)
                        inc("qS", 16)
                    wd_ld[j] = C["qS"]
                    if j >= 2:
                        y.wait_ge(vS, dn_ev[j - 2])
                    y.dma_start(out=xj[j % 2][:],
                                in_=x_sp.ap()[j * 128:(j + 1) * 128,
                                              th * 512:(th + 1) * 512]).then_inc(qS, 16)
                    inc("qS", 16)

                XJLD = C["qS"]
                bk_i = 2 + j % 2

                @block.tensor
                def _(t, j=j, th=th, bk_i=bk_i):
                    t.wait_ge(qS, wd_ld[j])
                    if j == 0:
                        t.wait_ge(vS, RND_DONE)
                    bwait(t, bk_i)
                    for ic in range(64):
                        t.matmul(pb[bk_i][:, :], wdb[j % 2][:, ic * 128:(ic + 1) * 128],
                                 guq[:, ic * 512:(ic + 1) * 512],
                                 start=(ic == 0), stop=(ic == 63))
                    t.drain().then_inc(tS, 1)
                inc("tS", 1)
                dn_mm[j] = C["tS"]

                @block.vector
                def _(v, j=j, bk_i=bk_i, XJLD=XJLD):
                    v.wait_ge(tS, dn_mm[j])
                    v.wait_ge(qS, XJLD)
                    if j >= 2:
                        v.wait_ge(qC, dn_st[j - 2])
                    v.tensor_tensor(out=outst[j % 2][:], in0=pb[bk_i][:, :],
                                    in1=stage1[:, 0:512], op=ALU.mult)
                    v.scalar_tensor_tensor(out=outst[j % 2][:], in0=outst[j % 2][:],
                                           scalar=gb_t[:, j:j + 1], in1=xj[j % 2][:],
                                           op0=ALU.mult, op1=ALU.add)
                    v.drain().then_inc(vS, 1)
                inc("vS", 1)
                dn_ev[j] = C["vS"]
                bankfree[bk_i] = ("vS", dn_ev[j])

                @block.scalar
                def _(s, j=j, th=th):
                    s.wait_ge(vS, dn_ev[j])
                    s.dma_start(out=out[j * 128:(j + 1) * 128, th * 512:(th + 1) * 512],
                                in_=outst[j % 2][:]).then_inc(qC, 16)
                inc("qC", 16)
                dn_st[j] = C["qC"]
            es_f3.close()
        es_f1.close()
        es_q.close()

        @block.sync
        def _(y):
            y.wait_ge(qC, C["qC"])
            y.wait_ge(qS, C["qS"])
            y.wait_ge(qG, C["qG"])

    return nc, dbg_out


_NC_CACHE = {}


def _get_nc(dbg=False):
    if dbg not in _NC_CACHE:
        _NC_CACHE[dbg] = build(dbg)
    return _NC_CACHE[dbg]


def _host_inputs(L0_attn_x, L0_attn_Wq, L0_attn_bq, L0_attn_Wk, L0_attn_bk,
                 L0_attn_Wv, L0_attn_bv, L0_attn_Wbeta, L0_attn_bbeta,
                 L0_attn_Walpha, L0_attn_balpha, L0_ffn_w_gate, L0_ffn_w_up,
                 L0_ffn_w_down, L0_c, L0_wmod):
    f32 = lambda a: np.ascontiguousarray(np.asarray(a, dtype=np.float32))
    x = f32(L0_attn_x)
    Wq = f32(L0_attn_Wq); Wk = f32(L0_attn_Wk); Wv = f32(L0_attn_Wv)
    bq = f32(L0_attn_bq).reshape(1, D)
    bk = f32(L0_attn_bk).reshape(1, D)
    bv = f32(L0_attn_bv).reshape(1, D)
    Wab = f32(np.concatenate([np.asarray(L0_attn_Walpha), np.asarray(L0_attn_Wbeta)], axis=1))
    bab = f32(np.concatenate([np.asarray(L0_attn_balpha).ravel(),
                              np.asarray(L0_attn_bbeta).ravel()])).reshape(1, 32)
    wg = f32(L0_ffn_w_gate); wu = f32(L0_ffn_w_up); wd = f32(L0_ffn_w_down)
    c = f32(L0_c); wmod = f32(L0_wmod)

    ident = np.eye(128, dtype=np.float32)
    ones_full = np.ones((128, 128), dtype=np.float32)
    triu_np = np.triu(np.ones((128, 128), dtype=np.float32))
    ee = np.arange(128)
    m0_1 = np.where(ee[None, :] >= ee[:, None], 0.0, -1e30).astype(np.float32)
    m0c = np.ascontiguousarray(np.tile(m0_1, (1, 4)))
    e127_np = np.zeros((128, 1), dtype=np.float32); e127_np[127, 0] = 1.0
    kk = np.arange(16)
    posisel_np = (kk[:, None] == (np.arange(2048) // 128)[None, :]).astype(np.float32)
    k8 = np.arange(8)
    pos8_np = (k8[:, None] == (np.arange(1024) // 128)[None, :]).astype(np.float32)
    bm16_np = (k8[:, None] == (np.arange(128) // 16)[None, :]).astype(np.float32)
    cT32 = np.ascontiguousarray(
        c.T.reshape(16, 128, 2).transpose(1, 0, 2).reshape(128, 32))

    in_maps = []
    for m in range(NB):
        b = m // 4
        t0 = (m % 4) * TC
        grp0 = (m // 4) * 4
        bsel_np = np.zeros((128, 2), dtype=np.float32)
        bsel_np[:, 0] = 1.0 if b == 0 else 0.0
        bsel_np[:, 1] = 1.0 if b == 1 else 0.0
        selT_np = np.zeros((8, 8), dtype=np.float32)
        amask_np = np.zeros((8, 1), dtype=np.float32)
        for r in range(8):
            if grp0 <= r < m:
                amask_np[r, 0] = 1.0
                for rp in range(8):
                    if r < rp < m and grp0 <= rp:
                        selT_np[rp, r] = 1.0
        d = dict(
            xT=np.ascontiguousarray(x[b, t0:t0 + TC, :].T),
            Wq=Wq, Wk=Wk, Wv=Wv, bq=bq, bk=bk, bv=bv, Wab=Wab, bab=bab,
            wmod_sl=np.ascontiguousarray(wmod[:, m * 1536:(m + 1) * 1536]),
            cT32=cT32,
            wgT_sl=np.ascontiguousarray(wg[m * ISL:(m + 1) * ISL, :].T),
            wuT_sl=np.ascontiguousarray(wu[m * ISL:(m + 1) * ISL, :].T),
            wdT_sl=np.ascontiguousarray(wd[:, m * ISL:(m + 1) * ISL].T),
            ident=ident, ones_full=ones_full, triu=triu_np, m0c=m0c,
            e127=e127_np, bsel=bsel_np, selT=selT_np, amask=amask_np,
            posisel=posisel_np, pos8=pos8_np, bm16=bm16_np,
        )
        in_maps.append(d)
    return in_maps


def kernel(**inputs):
    in_maps = _host_inputs(**inputs)
    nc, _ = _get_nc(DEBUG)
    res = run_bass_kernel_spmd(nc, in_maps, core_ids=list(range(NB)))
    full = np.empty((2, 4096, D), dtype=np.float32)
    for m in range(NB):
        b = m // 4
        t0 = (m % 4) * TC
        full[b, t0:t0 + TC, :] = res.results[m]["out"].T
    if DEBUG:
        kernel.last_results = res.results
    return full


# version 03
# speedup vs baseline: 1.0070x; 1.0070x over previous
import contextlib
import numpy as np
import ml_dtypes
import concourse.bass as bass
import concourse.mybir as mybir
from concourse.bass_utils import run_bass_kernel_spmd

F32 = mybir.dt.float32
BF16 = mybir.dt.bfloat16
FP16 = mybir.dt.float16
AF = mybir.ActivationFunctionType
ALU = mybir.AluOpType
AX = mybir.AxisListType.X
MAGIC = float(np.float32(1.5 * 2 ** 23))

D = 2048
TC = 1024
NH = 16
NCH = 8
NDC = 16
ISL = 1024
NB = 8

WG_SZ = D * ISL
WQIN_SZ = 3 * WG_SZ
ST_SZ = 128 * D + 32
MOD_SZ = 2 * 12 * 128 + 16

DEBUG = False
STOP_AFTER = None
SAFE = False


def build(dbg=False):
    nc = bass.Bass("TRN2", target_bir_lowering=False)

    def IN(n, s):
        return nc.dram_tensor(n, s, F32, kind="ExternalInput").ap()

    xT = IN("xT", [D, TC])
    Wq = IN("Wq", [D, D]); Wk = IN("Wk", [D, D]); Wv = IN("Wv", [D, D])
    bq = IN("bq", [1, D]); bk = IN("bk", [1, D]); bv = IN("bv", [1, D])
    Wab = IN("Wab", [D, 32]); bab = IN("bab", [1, 32])
    wmod_sl = IN("wmod_sl", [D, 1536])
    cT32 = IN("cT32", [128, 32])
    wgT_sl = IN("wgT_sl", [D, ISL])
    wuT_sl = IN("wuT_sl", [D, ISL])
    wdT_sl = IN("wdT_sl", [ISL, D])
    ident = IN("ident", [128, 128])
    ones_full = IN("ones_full", [128, 128])
    triu = IN("triu", [128, 128])
    m0c = IN("m0c", [128, 512])
    e127 = IN("e127", [128, 1])
    bsel = IN("bsel", [128, 2])
    selT = IN("selT", [8, 8])
    amask = IN("amask", [8, 1])
    posisel = IN("posisel", [16, 2048])
    pos8 = IN("pos8", [8, 1024])
    bm16 = IN("bm16", [8, 128])

    out = nc.dram_tensor("out", [D, TC], F32, kind="ExternalOutput").ap()
    dbg_out = {}
    if dbg:
        for n, s in [("d_mbg", [128, 192]), ("d_ga", [128, 32]), ("d_ysb", [128, 512]),
                     ("d_dm", [128, 512]), ("d_eD", [128, 512]),
                     ("d_mymod", [128, 96]), ("d_la", [128, 128]),
                     ("d_lnb", [128, 128]), ("d_cla", [128, 128]),
                     ("d_wfac", [128, 128]), ("d_mrow", [1, 1024]),
                     ("d_istd", [1, 1024]), ("d_scal", [128, 16]),
                     ("d_coefb", [128, 128]), ("d_mx", [128, 8]),
                     ("d_x1", [D, TC]), ("d_xq", [D, TC]),
                     ("d_h", [D, TC]), ("d_state", [128, D]),
                     ("d_td", [128, 128]), ("d_cd", [128, 128]),
                     ("d_f4", [8, 128]), ("d_s2", [8, 128])]:
            dbg_out[n] = nc.dram_tensor(n, s, F32, kind="ExternalOutput").ap()
        for n, s in [("d_q", [NH * 128, TC]), ("d_k", [NCH * 128, D]),
                     ("d_v", [NCH * 128, D]), ("d_wbig", [128, 2048]),
                     ("d_hbf", [128, 2048]), ("d_at", [128, 512])]:
            dbg_out[n] = nc.dram_tensor(n, s, BF16, kind="ExternalOutput").ap()

    cc_mod_in = nc.dram_tensor("cc_mod_in", [MOD_SZ], F32, kind="Internal")
    cc_mod_out = nc.dram_tensor("cc_mod_out", [NB * MOD_SZ], F32, kind="Internal")
    cc_wq_in = nc.dram_tensor("cc_wq_in", [WQIN_SZ], BF16, kind="Internal")
    cc_wq_out = nc.dram_tensor("cc_wq_out", [NB * WQIN_SZ], BF16, kind="Internal")
    cc_st_in = nc.dram_tensor("cc_st_in", [ST_SZ], F32, kind="Internal")
    cc_st_out = nc.dram_tensor("cc_st_out", [NB * ST_SZ], F32, kind="Internal")
    q_sp = nc.dram_tensor("q_sp", [NH, 128, TC], BF16, kind="Internal")
    k_sp = nc.dram_tensor("k_sp", [NCH, 128, D], BF16, kind="Internal")
    v_sp = nc.dram_tensor("v_sp", [NCH, 128, D], BF16, kind="Internal")
    x_sp = nc.dram_tensor("x_sp", [D, TC], F32, kind="Internal")

    es = contextlib.ExitStack()
    SB = lambda n, s, d: es.enter_context(nc.sbuf_tensor(n, s, d))
    PS = lambda n, s, d: es.enter_context(nc.psum_tensor(n, s, d))
    SEM = lambda n: es.enter_context(nc.semaphore(n))

    id_f = SB("id_f", [128, 128], F32)
    id_bf = SB("id_bf", [128, 128], BF16)
    onef = SB("onef", [128, 128], F32)
    onebf_c = SB("onebf_c", [128, 1], BF16)
    onebf_r = SB("onebf_r", [1, 512], BF16)
    cEPS5 = SB("cEPS5", [128, 1], F32)
    cEPS6 = SB("cEPS6", [128, 1], F32)
    cONE = SB("cONE", [128, 1], F32)
    cEPS24 = SB("cEPS24", [128, 1], F32)
    SCAL = SB("SCAL", [128, 16], F32)
    mymod = SB("mymod", [128, 96], F32)
    camod = SB("camod", [128, 16], F32)
    cbmod = SB("cbmod", [128, 16], F32)
    ga_t = SB("ga_t", [128, 16], F32)
    gb_t = SB("gb_t", [128, 16], F32)
    gtmp = SB("gtmp", [128, 32], F32)
    rowA = SB("rowA", [1, 1024], F32)
    rowB = SB("rowB", [1, 1024], F32)
    stage0 = SB("stage0", [128, 1024], F32)
    stage1 = SB("stage1", [128, 1024], F32)
    stage2 = SB("stage2", [128, 1024], F32)
    bst = [SB("bst0", [128, 512], BF16), SB("bst1", [128, 512], BF16)]
    wsum = SB("wsum", [128, 52], F32)
    wsmall = SB("wsmall", [128, 8], F32)
    scalrow = SB("scalrow", [1, 16], F32)
    pos8_f = SB("pos8_f", [8, 1024], F32)
    substacks = []
    es_a = contextlib.ExitStack()
    substacks.append(es_a)
    SBA = lambda n, s, d: es_a.enter_context(nc.sbuf_tensor(n, s, d))
    triu_f = SBA("triu_f", [128, 128], F32)
    m0_f = SBA("m0_f", [128, 512], F32)
    e127_f = SBA("e127_f", [128, 1], F32)
    bsel_f = SBA("bsel_f", [128, 2], F32)
    selT_f = SBA("selT_f", [8, 8], F32)
    amask_f = SBA("amask_f", [8, 1], F32)
    zab = SBA("zab", [128, 256], F32)
    la_all = SBA("la_all", [128, 128], F32)
    lnb_all = SBA("lnb_all", [128, 128], F32)
    albe = SBA("albe", [128, 256], F32)
    cla_all = SBA("cla_all", [128, 128], F32)
    cd_all = SBA("cd_all", [128, 128], F32)
    nclb = SBA("nclb", [128, 128], F32)
    claT_s = SBA("claT_s", [16, 1024], F32)
    nclbT_s = SBA("nclbT_s", [16, 1024], F32)
    wfac_all = SBA("wfac_all", [128, 128], F32)
    td_b = SBA("td_b", [128, 128], F32)
    cend_row = SBA("cend_row", [1, 128], F32)
    g_pub = SBA("g_pub", [1, 16], F32)
    gsb = SBA("gsb", [8, 16], F32)
    coef_sb = SBA("coef_sb", [8, 16], F32)
    coefB = SBA("coefB", [128, 128], F32)
    mb_g = SBA("mb_g", [128, 192], F32)
    psel_f = SBA("psel_f", [16, 2048], F32)
    bm16_f = SBA("bm16_f", [8, 128], F32)
    cmsk = SBA("cmsk", [8, 128], F32)
    ctrep = SBA("ctrep", [16, 512], F32)
    dr1 = [SBA("dr10", [16, 512], F32), SBA("dr11", [16, 512], F32)]

    pb = [PS("pb%d" % i, [128, 512], F32) for i in range(8)]

    qS = SEM("qS"); qG = SEM("qG"); qC = SEM("qC")
    cS = SEM("cS"); tS = SEM("tS"); vS = SEM("vS"); sS = SEM("sS")
    C = {"qS": 0, "qG": 0, "qC": 0, "cS": 0, "tS": 0, "vS": 0, "sS": 0}
    SMS = {"qS": qS, "qG": qG, "qC": qC, "cS": cS, "tS": tS, "vS": vS, "sS": sS}

    def inc(name, n=1):
        C[name] += n
        return C[name]

    bankfree = {}

    def _bail():
        for _st in reversed(substacks):
            _st.close()
        @block.sync
        def _(y):
            y.wait_ge(qC, C["qC"])
            y.wait_ge(qS, C["qS"])
            y.wait_ge(qG, C["qG"])
            y.wait_ge(vS, C["vS"])
            y.wait_ge(sS, C["sS"])
            y.wait_ge(tS, C["tS"])

    def bwait(eng, *banks):
        for i in banks:
            if i in bankfree:
                sn, cnt = bankfree[i]
                eng.wait_ge(SMS[sn], cnt)

    with es, nc.Block() as _block:
        class _SafeBlock:
            def __getattr__(self, eng):
                dec = getattr(_block, eng)
                def wrap(fn):
                    def body(e, fn=fn):
                        if SAFE:
                            for nm in ("qS", "qG", "qC", "cS", "tS", "vS", "sS"):
                                if C[nm]:
                                    e.wait_ge(SMS[nm], C[nm])
                        return fn(e)
                    return dec(body)
                return wrap
        block = _SafeBlock()
        @block.sync
        def _(y):
            y.dma_start(out=id_f[:], in_=ident).then_inc(qS, 16); inc("qS", 16)
            y.dma_start(out=onef[:], in_=ones_full).then_inc(qS, 16); inc("qS", 16)
            y.dma_start(out=triu_f[:], in_=triu).then_inc(qS, 16); inc("qS", 16)
            y.dma_start(out=m0_f[:], in_=m0c).then_inc(qS, 16); inc("qS", 16)
            y.dma_start(out=e127_f[:], in_=e127).then_inc(qS, 16); inc("qS", 16)
            y.dma_start(out=bsel_f[:], in_=bsel).then_inc(qS, 16); inc("qS", 16)
            y.dma_start(out=selT_f[:], in_=selT).then_inc(qS, 16); inc("qS", 16)
            y.dma_start(out=amask_f[:], in_=amask).then_inc(qS, 16); inc("qS", 16)
            y.dma_start(out=psel_f[:], in_=posisel).then_inc(qS, 16); inc("qS", 16)
            y.dma_start(out=bm16_f[:], in_=bm16).then_inc(qS, 16); inc("qS", 16)
            y.dma_start(out=pos8_f[:], in_=pos8).then_inc(qS, 16); inc("qS", 16)
            y.dma_start(out=stage0[:, 0:32], in_=cT32).then_inc(qS, 16); inc("qS", 16)
        CONSTS_S = C["qS"]

        @block.gpsimd
        def _(g):
            g.dma_start(out=id_bf[:], in_=ident).then_inc(qG, 16); inc("qG", 16)
            g.memset(onebf_c[:], 1.0)
            g.memset(onebf_r[:], 1.0)
            g.memset(cEPS5[:], 1e-5)
            g.memset(cEPS6[:], 1e-6)
            g.memset(cONE[:], 1.0)
            g.memset(cEPS24[:], 1e-24)
            g.drain().then_inc(qG, 1); inc("qG", 1)
        CONSTS_G = C["qG"]

        scT = stage1[:, 0:64].bitcast(BF16)
        @block.scalar
        def _(s):
            s.wait_ge(qS, CONSTS_S)
            s.wait_ge(qG, CONSTS_G)
            s.activation(out=scT[:, 0:32], in_=stage0[:, 0:32], func=AF.Silu)
            s.drain().then_inc(sS, 1); inc("sS", 1)
        SILU_C = C["sS"]

        if STOP_AFTER == "consts":
            _bail()
            return nc, dbg_out

        es_p0b = contextlib.ExitStack()
        substacks.append(es_p0b)
        SBPB = lambda n, s, d: es_p0b.enter_context(nc.sbuf_tensor(n, s, d))
        modst = SBPB("modst", [128, 24], F32)
        wqstg = [SBPB("wqstg0", [128, 1024], BF16), SBPB("wqstg1", [128, 1024], BF16)]
        es_p0 = contextlib.ExitStack()
        substacks.append(es_p0)
        SBP = lambda n, s, d: es_p0.enter_context(nc.sbuf_tensor(n, s, d))
        wm = [SBP("wm0", [128, 1536], BF16), SBP("wm1", [128, 1536], BF16)]
        wm_ld = []
        wm_used = []
        mod_ev = []
        for j in range(NDC):
            @block.gpsimd
            def _(g, j=j):
                if j >= 2:
                    g.wait_ge(tS, wm_used[j - 2])
                g.dma_start(out=wm[j % 2][:],
                            in_=wmod_sl[j * 128:(j + 1) * 128, :]).then_inc(qG, 16)
            inc("qG", 16)
            wm_ld.append(C["qG"])

            bk_i = j % 2
            @block.tensor
            def _(t, j=j, bk_i=bk_i):
                if j == 0:
                    t.wait_ge(sS, SILU_C)
                bwait(t, bk_i)
                t.wait_ge(qG, wm_ld[j])
                for ql in range(12):
                    t.matmul(pb[bk_i][:, ql * 2:ql * 2 + 2],
                             wm[j % 2][:, ql * 128:(ql + 1) * 128],
                             scT[:, 2 * j:2 * j + 2], start=True, stop=True)
                t.drain().then_inc(tS, 1)
            inc("tS", 1)
            wm_used.append(C["tS"])

            @block.vector
            def _(v, j=j, bk_i=bk_i):
                v.wait_ge(tS, wm_used[j])
                if j == 0:
                    v.tensor_copy(modst[:], pb[bk_i][:, 0:24])
                else:
                    v.drain()
                    v.tensor_tensor(out=modst[:], in0=modst[:], in1=pb[bk_i][:, 0:24],
                                    op=ALU.add)
                v.drain().then_inc(vS, 1)
            inc("vS", 1)
            mod_ev.append(C["vS"])
            bankfree[bk_i] = ("vS", C["vS"])
        MOD_EVAC = C["vS"]

        if STOP_AFTER == "mod":
            _bail()
            return nc, dbg_out

        wsl = [SBP("wsl0", [128, 1024], F32), SBP("wsl1", [128, 1024], F32)]
        absdump = SBP("absdump", [128, 1024], F32)
        wtiles = []
        for j in range(NDC):
            wtiles.append(wgT_sl[j * 128:(j + 1) * 128, :])
        for j in range(NDC):
            wtiles.append(wuT_sl[j * 128:(j + 1) * 128, :])
        for it in range(8):
            wtiles.append(wdT_sl[it * 128:(it + 1) * 128, 0:1024])
            wtiles.append(wdT_sl[it * 128:(it + 1) * 128, 1024:2048])
        wt_ld = []
        wt_used = []
        for i in range(48):
            @block.sync
            def _(y, i=i):
                if i >= 2:
                    y.wait_ge(sS, wt_used[i - 2])
                y.dma_start(out=wsl[i % 2][:], in_=wtiles[i]).then_inc(qS, 16)
            inc("qS", 16)
            wt_ld.append(C["qS"])

            @block.scalar
            def _(s, i=i):
                s.wait_ge(qS, wt_ld[i])
                s.activation(out=absdump[:], in_=wsl[i % 2][:], func=AF.Abs,
                             accum_out=wsum[:, i:i + 1]).then_inc(sS, 1)
            inc("sS", 1)
            wt_used.append(C["sS"])
        ABS_DONE = C["sS"]

        @block.vector
        def _(v):
            v.wait_ge(sS, ABS_DONE)
            v.tensor_reduce(out=wsmall[:, 0:1], in_=wsum[:, 0:16], axis=AX, op=ALU.add)
            v.tensor_reduce(out=wsmall[:, 1:2], in_=wsum[:, 16:32], axis=AX, op=ALU.add)
            v.tensor_reduce(out=wsmall[:, 2:3], in_=wsum[:, 32:48], axis=AX, op=ALU.add)
            v.drain().then_inc(vS, 1)
        inc("vS", 1)
        WRED = C["vS"]

        @block.tensor
        def _(t):
            t.wait_ge(vS, WRED)
            bwait(t, 1)
            t.matmul(pb[1][0:1, 0:3], onef[:, 0:1], wsmall[:, 0:3],
                     start=True, stop=True).then_inc(tS, 1)
        inc("tS", 1)
        WRED_MM = C["tS"]

        @block.vector
        def _(v):
            v.wait_ge(tS, WRED_MM)
            v.tensor_copy(scalrow[:, 0:3], pb[1][0:1, 0:3]).then_inc(vS, 1)
        inc("vS", 1)
        WROW = C["vS"]
        bankfree[1] = ("vS", WROW)

        if STOP_AFTER == "absw":
            _bail()
            return nc, dbg_out

        @block.gpsimd
        def _(g):
            g.wait_ge(vS, max(MOD_EVAC, WROW))
            g.dma_start(out=cc_mod_in.ap()[0:3072].rearrange("(p c) -> p c", p=128),
                        in_=modst[:]).then_inc(qG, 16)
            inc("qG", 16)
            g.dma_start(out=cc_mod_in.ap()[3072:3088].rearrange("(o c) -> o c", o=1),
                        in_=scalrow[:, 0:16]).then_inc(qG, 16)
            inc("qG", 16)
            g.wait_ge(qG, C["qG"])
            g.collective_compute(
                "AllGather", ALU.bypass, replica_groups=[list(range(NB))],
                ins=[cc_mod_in.ap().opt()], outs=[cc_mod_out.ap().opt()],
            ).then_inc(cS, 1)
        inc("cS", 1)
        AG1 = C["cS"]

        @block.sync
        def _(y):
            y.wait_ge(cS, AG1)
            y.dma_start(out=gsb[:],
                        in_=cc_mod_out.ap().rearrange("(r x) -> r x", r=NB)[:, 3072:3088]
                        ).then_inc(qS, 16)
            inc("qS", 16)
            y.dma_start(out=mb_g[:].rearrange("p (r c) -> p r c", r=NB),
                        in_=cc_mod_out.ap().rearrange("(r x) -> r x", r=NB)[:, 0:3072]
                        .rearrange("r (p c) -> p r c", p=128)).then_inc(qS, 16)
            inc("qS", 16)
        AG1_LD = C["qS"]

        @block.tensor
        def _(t):
            t.wait_ge(qS, AG1_LD)
            bwait(t, 2)
            t.matmul(pb[2][0:1, 0:16], onef[0:8, 0:1], gsb[:, 0:16],
                     start=True, stop=True).then_inc(tS, 1)
        inc("tS", 1)
        GS_MM = C["tS"]

        @block.vector
        def _(v):
            v.wait_ge(tS, GS_MM)
            v.tensor_scalar(out=scalrow[:, 0:3], in0=pb[2][0:1, 0:3],
                            scalar1=1.0 / (8192.0 * 2048.0), scalar2=1e-5,
                            op0=ALU.mult, op1=ALU.add)
            v.drain()
            v.reciprocal(scalrow[:, 3:6], scalrow[:, 0:3])
            v.drain().then_inc(vS, 1)
        inc("vS", 1)
        SCROW = C["vS"]
        bankfree[2] = ("vS", SCROW)

        @block.tensor
        def _(t):
            t.wait_ge(vS, SCROW)
            bwait(t, 3)
            t.matmul(pb[3][:, 0:16], onef[0:1, 0:128], scalrow[:, 0:16],
                     start=True, stop=True).then_inc(tS, 1)
        inc("tS", 1)
        SC_MM = C["tS"]

        @block.vector
        def _(v):
            v.wait_ge(tS, SC_MM)
            v.tensor_copy(SCAL[:], pb[3][:, 0:16]).then_inc(vS, 1)
            inc("vS", 1)
            v.wait_ge(qS, AG1_LD)
            mbv = mb_g[:].rearrange("p (r q b) -> p r q b", r=NB, b=2)
            v.tensor_scalar(out=mymod[:].rearrange("p (r q) -> p r q", r=NB),
                            in0=mbv[:, :, :, 0], scalar1=bsel_f[:, 0:1],
                            scalar2=0.0, op0=ALU.mult, op1=ALU.add)
            v.drain()
            v.scalar_tensor_tensor(out=mymod[:].rearrange("p (r q) -> p r q", r=NB),
                                   in0=mbv[:, :, :, 1], scalar=bsel_f[:, 1:2],
                                   in1=mymod[:].rearrange("p (r q) -> p r q", r=NB),
                                   op0=ALU.mult, op1=ALU.add)
            v.drain()
            v.tensor_scalar(out=camod[:], in0=mymod[:, 16:32], scalar1=1.0, scalar2=0.0,
                            op0=ALU.add, op1=ALU.add)
            v.tensor_scalar(out=cbmod[:], in0=mymod[:, 64:80], scalar1=1.0, scalar2=0.0,
                            op0=ALU.add, op1=ALU.add)
            v.drain().then_inc(vS, 1)
        inc("vS", 1)
        MYMOD = C["vS"]
        bankfree[3] = ("vS", MYMOD)

        @block.vector
        def _(v):
            v.wait_ge(vS, MYMOD)
            for (gt, col) in ((ga_t, 32), (gb_t, 80)):
                off = 0 if col == 32 else 16
                v.tensor_scalar_mul(gtmp[:, off:off + 16], mymod[:, col:col + 16], 1e30)
                v.drain()
                v.tensor_scalar(out=gtmp[:, off:off + 16], in0=gtmp[:, off:off + 16],
                                scalar1=1.0, scalar2=-1.0, op0=ALU.min, op1=ALU.max)
                v.drain()
                v.tensor_tensor(out=gt[:], in0=mymod[:, col:col + 16],
                                in1=gtmp[:, off:off + 16], op=ALU.mult)
                v.drain()
                v.tensor_scalar(out=gt[:], in0=gt[:], scalar1=0.25, scalar2=0.0,
                                op0=ALU.max, op1=ALU.add)
                v.drain()
                v.tensor_tensor(out=gt[:], in0=gt[:], in1=gtmp[:, off:off + 16],
                                op=ALU.mult)
            v.drain().then_inc(vS, 1)
        inc("vS", 1)
        GAGB = C["vS"]

        if STOP_AFTER == "ag1c":
            _bail()
            return nc, dbg_out

        wq_dst = []
        for j in range(NDC):
            wq_dst.append(("gu", j * 131072))
        for j in range(NDC):
            wq_dst.append(("gu", WG_SZ + j * 131072))
        for it in range(8):
            wq_dst.append(("wd", (it, 0)))
            wq_dst.append(("wd", (it, 1)))
        q_ld = []
        q_sc = []
        q_vc = []
        q_st = []
        for i in range(48):
            invcol = 3 + (0 if i < 16 else (1 if i < 32 else 2))

            @block.sync
            def _(y, i=i):
                if i >= 2:
                    y.wait_ge(sS, q_sc[i - 2])
                if i == 0:
                    y.wait_ge(vS, SCROW)
                y.dma_start(out=wsl[i % 2][:], in_=wtiles[i]).then_inc(qS, 16)
            inc("qS", 16)
            q_ld.append(C["qS"])

            @block.scalar
            def _(s, i=i, invcol=invcol):
                s.wait_ge(qS, q_ld[i])
                if i == 0:
                    s.wait_ge(vS, SCROW + 1)
                if i >= 2:
                    s.wait_ge(vS, q_vc[i - 2])
                s.activation(out=stage2[:], in_=wsl[i % 2][:], func=AF.Copy,
                             scale=SCAL[:, invcol:invcol + 1], bias=MAGIC
                             ).then_inc(sS, 1)
            inc("sS", 1)
            q_sc.append(C["sS"])

            @block.vector
            def _(v, i=i):
                v.wait_ge(sS, q_sc[i])
                if i >= 2:
                    v.wait_ge(qG, q_st[i - 2])
                v.tensor_scalar(out=stage2[:], in0=stage2[:], scalar1=MAGIC + 1.0,
                                scalar2=MAGIC - 1.0, op0=ALU.min, op1=ALU.max)
                v.tensor_scalar(out=wqstg[i % 2][:], in0=stage2[:], scalar1=MAGIC,
                                scalar2=0.0, op0=ALU.subtract, op1=ALU.add
                                ).then_inc(vS, 1)
            inc("vS", 1)
            q_vc.append(C["vS"])

            kind, off = wq_dst[i]

            @block.gpsimd
            def _(g, i=i, kind=kind, off=off):
                g.wait_ge(vS, q_vc[i])
                if kind == "gu":
                    g.dma_start(out=cc_wq_in.ap()[off:off + 131072]
                                .rearrange("(p c) -> p c", p=128),
                                in_=wqstg[i % 2][:]).then_inc(qG, 16)
                else:
                    it, half = off
                    w0 = 2 * WG_SZ + half * 8 * 131072
                    dst = (cc_wq_in.ap()[w0:w0 + 8 * 131072]
                           .rearrange("(hc i h) -> hc i h", hc=8, i=1024)
                           [:, it * 128:(it + 1) * 128, :]
                           .rearrange("hc p h -> p hc h"))
                    g.dma_start(out=dst,
                                in_=wqstg[i % 2][:].rearrange("p (hc h) -> p hc h", hc=8)
                                ).then_inc(qG, 16)
            inc("qG", 16)
            q_st.append(C["qG"])
        QPUB = C["qG"]

        @block.gpsimd
        def _(g):
            g.wait_ge(qG, QPUB)
            g.collective_compute(
                "AllGather", ALU.bypass, replica_groups=[list(range(NB))],
                ins=[cc_wq_in.ap().opt()], outs=[cc_wq_out.ap().opt()],
            ).then_inc(cS, 1)
        inc("cS", 1)
        AG2 = C["cS"]

        if STOP_AFTER == "quant":
            _bail()
            return nc, dbg_out

        es_p0.close()
        es_h = contextlib.ExitStack()
        substacks.append(es_h)
        h_bf = es_h.enter_context(nc.sbuf_tensor("h_bf", [128, NDC * 1024], BF16))
        es_x = contextlib.ExitStack()
        substacks.append(es_x)
        x_f = es_x.enter_context(nc.sbuf_tensor("x_f", [128, NDC * 1024], F32))
        x_bf = es_x.enter_context(nc.sbuf_tensor("x_bf", [128, NDC * 1024], BF16))
        x2b = [es_x.enter_context(nc.sbuf_tensor("x2b%d" % i, [128, 1024], BF16))
               for i in range(2)]
        rowC = es_x.enter_context(nc.sbuf_tensor("rowC", [1, 1024], F32))

        @block.sync
        def _(y, F=(C["tS"], C["vS"], C["sS"])):
            y.wait_ge(tS, F[0])
            y.wait_ge(vS, F[1])
            y.wait_ge(sS, F[2])
            y.dma_start(out=x_f[:].rearrange("p (j t) -> p j t", j=NDC),
                        in_=xT.rearrange("(j p) t -> p j t", p=128)).then_inc(qS, 16)
        inc("qS", 16)
        XLD = C["qS"]

        @block.gpsimd
        def _(g, F=(C["tS"], C["vS"], C["sS"])):
            g.wait_ge(tS, F[0])
            g.wait_ge(vS, F[1])
            g.wait_ge(sS, F[2])
            g.dma_start(out=x_bf[:].rearrange("p (j t) -> p j t", j=NDC),
                        in_=xT.rearrange("(j p) t -> p j t", p=128)).then_inc(qG, 16)
        inc("qG", 16)
        XBLD = C["qG"]

        sq_done = []
        st_done = []
        for j in range(NDC):
            @block.scalar
            def _(s, j=j):
                if j == 0:
                    s.wait_ge(qS, XLD)
                if j >= 2:
                    s.wait_ge(tS, st_done[j - 2])
                s.activation(out=x2b[j % 2][:], in_=x_f[:, j * 1024:(j + 1) * 1024],
                             func=AF.Square).then_inc(sS, 1)
            inc("sS", 1)
            sq_done.append(C["sS"])

            @block.tensor
            def _(t, j=j):
                if j == 0:
                    t.wait_ge(qG, XBLD)
                    bwait(t, 4, 5, 6, 7)
                t.wait_ge(sS, sq_done[j])
                for tt in range(2):
                    t.matmul(pb[4 + tt][0:1, :], onebf_c[:, 0:1],
                             x_bf[:, j * 1024 + tt * 512: j * 1024 + (tt + 1) * 512],
                             start=(j == 0), stop=(j == NDC - 1))
                    t.matmul(pb[6 + tt][0:1, :], onebf_c[:, 0:1],
                             x2b[j % 2][:, tt * 512:(tt + 1) * 512],
                             start=(j == 0), stop=(j == NDC - 1))
                t.drain().then_inc(tS, 1)
            inc("tS", 1)
            st_done.append(C["tS"])
        STATS_MM = C["tS"]

        @block.vector
        def _(v):
            v.wait_ge(tS, STATS_MM)
            v.tensor_scalar(out=rowA[0:1, 0:512], in0=pb[4][0:1, :], scalar1=1.0 / 2048.0,
                            scalar2=0.0, op0=ALU.mult, op1=ALU.add)
            v.tensor_scalar(out=rowA[0:1, 512:1024], in0=pb[5][0:1, :], scalar1=1.0 / 2048.0,
                            scalar2=0.0, op0=ALU.mult, op1=ALU.add)
            v.tensor_scalar(out=rowB[0:1, 0:512], in0=pb[6][0:1, :], scalar1=1.0 / 2048.0,
                            scalar2=0.0, op0=ALU.mult, op1=ALU.add)
            v.tensor_scalar(out=rowB[0:1, 512:1024], in0=pb[7][0:1, :], scalar1=1.0 / 2048.0,
                            scalar2=0.0, op0=ALU.mult, op1=ALU.add)
            v.drain()
            v.tensor_tensor(out=rowC[:], in0=rowA[:], in1=rowA[:], op=ALU.mult)
            v.drain()
            v.tensor_tensor(out=rowB[:], in0=rowB[:], in1=rowC[:], op=ALU.subtract)
            v.drain().then_inc(vS, 1)
        inc("vS", 1)
        VAR1 = C["vS"]
        for b_ in (4, 5, 6, 7):
            bankfree[b_] = ("vS", VAR1)

        @block.scalar
        def _(s):
            s.wait_ge(vS, VAR1)
            s.activation(out=rowC[:], in_=rowB[:], func=AF.Ln, bias=cEPS5[0:1, 0:1])
            s.activation(out=rowB[:], in_=rowC[:], func=AF.Exp, scale=-0.5)
            s.drain().then_inc(sS, 1)
        inc("sS", 1)
        ISTD1 = C["sS"]

        if dbg:
            @block.scalar
            def _(s):
                s.wait_ge(vS, MYMOD)
                s.wait_ge(sS, ISTD1)
                s.drain()
                s.dma_start(out=dbg_out["d_mbg"], in_=mb_g[:]).then_inc(qC, 16)
                s.dma_start(out=dbg_out["d_mrow"], in_=rowA[:]).then_inc(qC, 16)
                s.dma_start(out=dbg_out["d_istd"], in_=rowB[:]).then_inc(qC, 16)
                s.dma_start(out=dbg_out["d_mymod"], in_=mymod[:]).then_inc(qC, 16)
                s.dma_start(out=dbg_out["d_scal"], in_=SCAL[:]).then_inc(qC, 16)
            inc("qC", 80)

        @block.tensor
        def _(t):
            t.wait_ge(sS, ISTD1)
            bwait(t, 0, 1, 2, 3)
            for tt in range(2):
                t.matmul(pb[0 + tt][:, :], onef[0:1, 0:128],
                         rowA[0:1, tt * 512:(tt + 1) * 512], start=True, stop=True)
                t.matmul(pb[2 + tt][:, :], onef[0:1, 0:128],
                         rowB[0:1, tt * 512:(tt + 1) * 512], start=True, stop=True)
            t.drain().then_inc(tS, 1)
        inc("tS", 1)
        BC1 = C["tS"]

        @block.vector
        def _(v):
            v.wait_ge(tS, BC1)
            v.tensor_copy(stage0[:, 0:512], pb[0][:, :])
            v.tensor_copy(stage0[:, 512:1024], pb[1][:, :])
            v.tensor_copy(stage1[:, 0:512], pb[2][:, :])
            v.tensor_copy(stage1[:, 512:1024], pb[3][:, :])
            v.drain().then_inc(vS, 1)
        inc("vS", 1)
        MB1 = C["vS"]
        for b_ in (0, 1, 2, 3):
            bankfree[b_] = ("vS", MB1)

        for j in range(NDC):
            @block.vector
            def _(v, j=j):
                if j == 0:
                    v.wait_ge(vS, max(MB1, GAGB))
                cs = slice(j * 1024, (j + 1) * 1024)
                v.tensor_tensor(out=stage2[:], in0=x_f[:, cs], in1=stage0[:], op=ALU.subtract)
                v.tensor_tensor(out=stage2[:], in0=stage2[:], in1=stage1[:], op=ALU.mult)
                v.tensor_scalar(out=h_bf[:, cs], in0=stage2[:],
                                scalar1=camod[:, j:j + 1], scalar2=mymod[:, j:j + 1],
                                op0=ALU.mult, op1=ALU.add)
                v.drain().then_inc(vS, 1)
            inc("vS", 1)
        H_DONE = C["vS"]

        if dbg:
            @block.gpsimd
            def _(g):
                g.wait_ge(vS, H_DONE)
                g.dma_start(out=dbg_out["d_h"].rearrange("(j p) t -> p j t", p=128),
                            in_=h_bf[:].rearrange("p (j t) -> p j t", j=NDC)).then_inc(qG, 16)
            inc("qG", 16)

        @block.scalar
        def _(s):
            s.wait_ge(vS, H_DONE)
            s.dma_start(out=x_sp.ap().rearrange("(j p) t -> p j t", p=128),
                        in_=x_f[:].rearrange("p (j t) -> p j t", j=NDC)).then_inc(qC, 16)
        inc("qC", 16)
        XSPILL = C["qC"]
        es_x.close()

        if STOP_AFTER == "ln1":
            _bail()
            return nc, dbg_out

        es_w = contextlib.ExitStack()
        substacks.append(es_w)
        wbig = [es_w.enter_context(nc.sbuf_tensor("wbig%d" % i, [128, 8192], BF16))
                for i in range(2)]
        wab_t = es_w.enter_context(nc.sbuf_tensor("wab_t", [128, 512], BF16))
        bq_bf = es_w.enter_context(nc.sbuf_tensor("bq_bf", [1, D], BF16))
        bk_bf = es_w.enter_context(nc.sbuf_tensor("bk_bf", [1, D], BF16))
        bv_bf = es_w.enter_context(nc.sbuf_tensor("bv_bf", [1, D], BF16))
        bab_bf = es_w.enter_context(nc.sbuf_tensor("bab_bf", [1, 32], BF16))
        kn_s = es_w.enter_context(nc.sbuf_tensor("kn_s", [128, 8], F32))

        @block.gpsimd
        def _(g):
            g.wait_ge(vS, H_DONE)
            g.wait_ge(qC, XSPILL)
            g.wait_ge(tS, C["tS"])
            g.dma_start(out=bq_bf[:], in_=bq).then_inc(qG, 16)
            g.dma_start(out=bk_bf[:], in_=bk).then_inc(qG, 16)
            g.dma_start(out=bv_bf[:], in_=bv).then_inc(qG, 16)
            g.dma_start(out=bab_bf[:], in_=bab).then_inc(qG, 16)
            g.dma_start(out=wab_t[:].rearrange("p (j c) -> p j c", j=NDC),
                        in_=Wab.rearrange("(j p) c -> p j c", p=128)).then_inc(qG, 16)
        inc("qG", 80)
        BIAS_LD = C["qG"]

        wq_ld = []
        q_use = []
        q_stq = []
        for h in range(NH):
            @block.gpsimd
            def _(g, h=h):
                if h >= 2:
                    g.wait_ge(tS, q_use[h - 2])
                g.dma_start(out=wbig[h % 2][:, 0:2048].rearrange("p (j s) -> p j s", j=NDC),
                            in_=Wq.rearrange("(j p) s -> p j s", p=128)
                            [:, :, h * 128:(h + 1) * 128]).then_inc(qG, 16)
            inc("qG", 16)
            wq_ld.append(C["qG"])

            for tt in range(2):
                bk_i = (h * 2 + tt) % 2
                @block.tensor
                def _(t, h=h, tt=tt, bk_i=bk_i):
                    if tt == 0:
                        t.wait_ge(qG, wq_ld[h])
                        if h == 0:
                            t.wait_ge(vS, H_DONE)
                            t.wait_ge(qG, BIAS_LD)
                    bwait(t, bk_i)
                    for j in range(NDC):
                        t.matmul(pb[bk_i][:, :], wbig[h % 2][:, j * 128:(j + 1) * 128],
                                 h_bf[:, j * 1024 + tt * 512: j * 1024 + (tt + 1) * 512],
                                 start=(j == 0), stop=False)
                    t.matmul(pb[bk_i][:, :], bq_bf[0:1, h * 128:(h + 1) * 128],
                             onebf_r[0:1, 0:512], start=False, stop=True)
                    t.drain().then_inc(tS, 1)
                inc("tS", 1)
                mmq = C["tS"]
                if tt == 1:
                    q_use.append(mmq)

                @block.scalar
                def _(s, h=h, tt=tt, bk_i=bk_i, mmq=mmq):
                    s.wait_ge(tS, mmq)
                    idx = h * 2 + tt
                    if idx >= 2:
                        s.wait_ge(qC, q_stq[idx - 2])
                    s.activation(out=bst[bk_i][:], in_=pb[bk_i][:, :], func=AF.Copy
                                 ).then_inc(sS, 1)
                    s.drain()
                    s.dma_start(out=q_sp.ap()[h, :, tt * 512:(tt + 1) * 512],
                                in_=bst[bk_i][:]).then_inc(qC, 16)
                inc("sS", 1)
                inc("qC", 16)
                q_stq.append(C["qC"])
                bankfree[bk_i] = ("sS", C["sS"])
        Q_DONE = C["qC"]
        if dbg:
            @block.gpsimd
            def _(g):
                g.wait_ge(tS, q_use[-1])
                g.dma_start(out=dbg_out["d_wbig"], in_=wbig[15 % 2][:, 0:2048]).then_inc(qG, 16)
                g.dma_start(out=dbg_out["d_hbf"], in_=h_bf[:, 0:2048]).then_inc(qG, 16)
            inc("qG", 32)

        wk_ld = []
        k_use = []
        k_stq = []
        for sg in range(4):
            @block.gpsimd
            def _(g, sg=sg):
                if sg >= 2:
                    g.wait_ge(tS, k_use[sg - 2])
                else:
                    g.wait_ge(tS, q_use[-1])
                g.dma_start(out=wbig[sg % 2][:].rearrange("p (j s) -> p j s", j=NDC),
                            in_=Wk.rearrange("(j p) s -> p j s", p=128)
                            [:, :, sg * 512:(sg + 1) * 512]).then_inc(qG, 16)
            inc("qG", 16)
            wk_ld.append(C["qG"])

            for tc in range(NCH):
                bk_i = 2 + (sg * NCH + tc) % 2
                @block.tensor
                def _(t, sg=sg, tc=tc, bk_i=bk_i):
                    if tc == 0:
                        t.wait_ge(qG, wk_ld[sg])
                    bwait(t, bk_i)
                    for j in range(NDC):
                        t.matmul(pb[bk_i][:, :],
                                 h_bf[:, j * 1024 + tc * 128: j * 1024 + (tc + 1) * 128],
                                 wbig[sg % 2][:, j * 512:(j + 1) * 512],
                                 start=(j == 0), stop=False)
                    t.matmul(pb[bk_i][:, :], onebf_r[0:1, 0:128],
                             bk_bf[0:1, sg * 512:(sg + 1) * 512], start=False, stop=True)
                    t.drain().then_inc(tS, 1)
                inc("tS", 1)
                mmk = C["tS"]
                if tc == NCH - 1:
                    k_use.append(mmk)

                @block.scalar
                def _(s, bk_i=bk_i, mmk=mmk):
                    s.wait_ge(tS, mmk)
                    s.activation(out=stage2[:, 0:512], in_=pb[bk_i][:, :],
                                 func=AF.Square).then_inc(sS, 1)
                inc("sS", 1)
                sqk = C["sS"]

                @block.vector
                def _(v, sqk=sqk):
                    v.wait_ge(sS, sqk)
                    v.tensor_reduce(out=kn_s[:, 0:4],
                                    in_=stage2[:, 0:512].rearrange("p (h s) -> p h s", h=4),
                                    axis=AX, op=ALU.add)
                    v.drain().then_inc(vS, 1)
                inc("vS", 1)
                rdk = C["vS"]

                @block.scalar
                def _(s, rdk=rdk):
                    s.wait_ge(vS, rdk)
                    s.activation(out=kn_s[:, 4:8], in_=kn_s[:, 0:4], func=AF.Ln,
                                 bias=cEPS24[:, 0:1])
                    s.activation(out=kn_s[:, 0:4], in_=kn_s[:, 4:8], func=AF.Exp,
                                 scale=-0.5).then_inc(sS, 1)
                inc("sS", 1)
                lnk = C["sS"]

                @block.vector
                def _(v, sg=sg, tc=tc, bk_i=bk_i, lnk=lnk):
                    v.wait_ge(sS, lnk)
                    idx = sg * NCH + tc
                    if idx >= 2:
                        v.wait_ge(qC, k_stq[idx - 2])
                    else:
                        v.wait_ge(qC, Q_DONE)
                    for hh in range(4):
                        v.tensor_scalar(out=bst[bk_i - 2][:, hh * 128:(hh + 1) * 128],
                                        in0=pb[bk_i][:, hh * 128:(hh + 1) * 128],
                                        scalar1=kn_s[:, hh:hh + 1], scalar2=0.0,
                                        op0=ALU.mult, op1=ALU.add)
                    v.drain().then_inc(vS, 1)
                inc("vS", 1)
                bankfree[bk_i] = ("vS", C["vS"])
                evk = C["vS"]

                @block.scalar
                def _(s, sg=sg, tc=tc, bk_i=bk_i, evk=evk):
                    s.wait_ge(vS, evk)
                    s.dma_start(out=k_sp.ap()[tc, :, sg * 512:(sg + 1) * 512],
                                in_=bst[bk_i - 2][:]).then_inc(qC, 16)
                inc("qC", 16)
                k_stq.append(C["qC"])
        K_DONE = C["qC"]

        wv_ld = []
        v_use = []
        v_stq = []
        for sg in range(4):
            @block.gpsimd
            def _(g, sg=sg):
                if sg >= 2:
                    g.wait_ge(tS, v_use[sg - 2])
                else:
                    g.wait_ge(tS, k_use[-1])
                g.dma_start(out=wbig[sg % 2][:].rearrange("p (j s) -> p j s", j=NDC),
                            in_=Wv.rearrange("(j p) s -> p j s", p=128)
                            [:, :, sg * 512:(sg + 1) * 512]).then_inc(qG, 16)
            inc("qG", 16)
            wv_ld.append(C["qG"])

            for tc in range(NCH):
                bk_i = 4 + (sg * NCH + tc) % 2
                @block.tensor
                def _(t, sg=sg, tc=tc, bk_i=bk_i):
                    if tc == 0:
                        t.wait_ge(qG, wv_ld[sg])
                    bwait(t, bk_i)
                    for j in range(NDC):
                        t.matmul(pb[bk_i][:, :],
                                 h_bf[:, j * 1024 + tc * 128: j * 1024 + (tc + 1) * 128],
                                 wbig[sg % 2][:, j * 512:(j + 1) * 512],
                                 start=(j == 0), stop=False)
                    t.matmul(pb[bk_i][:, :], onebf_r[0:1, 0:128],
                             bv_bf[0:1, sg * 512:(sg + 1) * 512], start=False, stop=True)
                    t.drain().then_inc(tS, 1)
                inc("tS", 1)
                mmv = C["tS"]
                if tc == NCH - 1:
                    v_use.append(mmv)

                @block.scalar
                def _(s, sg=sg, tc=tc, bk_i=bk_i, mmv=mmv):
                    s.wait_ge(tS, mmv)
                    idx = sg * NCH + tc
                    if idx >= 2:
                        s.wait_ge(qC, v_stq[idx - 2])
                    else:
                        s.wait_ge(qC, K_DONE)
                    s.activation(out=bst[bk_i - 4][:], in_=pb[bk_i][:, :], func=AF.Copy
                                 ).then_inc(sS, 1)
                    s.drain()
                    s.dma_start(out=v_sp.ap()[tc, :, sg * 512:(sg + 1) * 512],
                                in_=bst[bk_i - 4][:]).then_inc(qC, 16)
                inc("sS", 1)
                inc("qC", 16)
                v_stq.append(C["qC"])
                bankfree[bk_i] = ("sS", C["sS"])
        V_DONE = C["qC"]

        ab_done = []
        for tc in range(NCH):
            bk_i = 6 + tc % 2
            @block.tensor
            def _(t, tc=tc, bk_i=bk_i):
                if tc == 0:
                    t.wait_ge(qG, BIAS_LD)
                bwait(t, bk_i)
                for j in range(NDC):
                    t.matmul(pb[bk_i][:, 0:32],
                             h_bf[:, j * 1024 + tc * 128: j * 1024 + (tc + 1) * 128],
                             wab_t[:, j * 32:(j + 1) * 32],
                             start=(j == 0), stop=False)
                t.matmul(pb[bk_i][:, 0:32], onebf_r[0:1, 0:128], bab_bf[0:1, 0:32],
                         start=False, stop=True)
                t.drain().then_inc(tS, 1)
            inc("tS", 1)
            mab = C["tS"]

            @block.vector
            def _(v, tc=tc, bk_i=bk_i, mab=mab):
                v.wait_ge(tS, mab)
                v.tensor_copy(zab[:, tc * 32:(tc + 1) * 32], pb[bk_i][:, 0:32]
                              ).then_inc(vS, 1)
            inc("vS", 1)
            ab_done.append(C["vS"])
            bankfree[bk_i] = ("vS", C["vS"])
        AB_DONE = C["vS"]

        if STOP_AFTER == "proj":
            _bail()
            return nc, dbg_out

        es_w.close()
        es_h.close()

        @block.vector
        def _(v):
            v.wait_ge(vS, AB_DONE)
            v.tensor_copy(albe[:, 0:128].rearrange("p (tc c) -> p tc c", tc=8),
                          zab[:].rearrange("p (tc c) -> p tc c", tc=8)[:, :, 0:16])
            v.tensor_copy(albe[:, 128:256].rearrange("p (tc c) -> p tc c", tc=8),
                          zab[:].rearrange("p (tc c) -> p tc c", tc=8)[:, :, 16:32])
            v.drain().then_inc(vS, 1)
        inc("vS", 1)
        AB_G = C["vS"]

        @block.scalar
        def _(s):
            s.wait_ge(vS, AB_G)
            s.activation(out=cd_all[:], in_=albe[:, 0:128], func=AF.Sigmoid)
            s.activation(out=la_all[:], in_=cd_all[:], func=AF.Ln, bias=cEPS6[:, 0:1])
            s.activation(out=cd_all[:], in_=albe[:, 128:256], func=AF.Abs)
            s.activation(out=nclb[:], in_=cd_all[:], func=AF.Exp, scale=-1.0)
            s.activation(out=cd_all[:], in_=nclb[:], func=AF.Ln, bias=cONE[:, 0:1])
            s.drain().then_inc(sS, 1)
        inc("sS", 1)
        SP_A = C["sS"]

        @block.vector
        def _(v):
            v.wait_ge(sS, SP_A)
            v.scalar_tensor_tensor(out=nclb[:], in0=albe[:, 128:256], scalar=0.0,
                                   in1=cd_all[:], op0=ALU.max, op1=ALU.add)
            v.drain().then_inc(vS, 1)
        inc("vS", 1)
        SP_B = C["vS"]

        @block.scalar
        def _(s):
            s.wait_ge(vS, SP_B)
            s.activation(out=lnb_all[:], in_=nclb[:], func=AF.Ln)
            s.drain().then_inc(sS, 1)
        inc("sS", 1)
        LNB_DONE = C["sS"]

        cla_ev = []
        for tc in range(NCH):
            bk_i = 6 + tc % 2
            @block.tensor
            def _(t, tc=tc, bk_i=bk_i):
                if tc == 0:
                    t.wait_ge(sS, LNB_DONE)
                bwait(t, bk_i)
                t.matmul(pb[bk_i][:, 0:16], triu_f[:], la_all[:, tc * 16:(tc + 1) * 16],
                         start=True, stop=True).then_inc(tS, 1)
            inc("tS", 1)
            mcla = C["tS"]

            @block.vector
            def _(v, tc=tc, bk_i=bk_i, mcla=mcla):
                v.wait_ge(tS, mcla)
                v.tensor_copy(cla_all[:, tc * 16:(tc + 1) * 16], pb[bk_i][:, 0:16]
                              ).then_inc(vS, 1)
            inc("vS", 1)
            cla_ev.append(C["vS"])
            bankfree[bk_i] = ("vS", C["vS"])
        CLA_DONE = C["vS"]

        @block.scalar
        def _(s):
            s.wait_ge(vS, CLA_DONE)
            s.activation(out=cd_all[:], in_=cla_all[:], func=AF.Exp)
            s.drain().then_inc(sS, 1)
        inc("sS", 1)
        CD_DONE = C["sS"]

        @block.vector
        def _(v):
            v.wait_ge(vS, CLA_DONE)
            v.tensor_tensor(out=nclb[:], in0=lnb_all[:], in1=cla_all[:], op=ALU.subtract)
            v.drain().then_inc(vS, 1)
        inc("vS", 1)
        NCLB_DONE = C["vS"]

        trev = []
        for tc in range(NCH):
            bk_i = 6 + tc % 2
            @block.tensor
            def _(t, tc=tc, bk_i=bk_i):
                if tc == 0:
                    t.wait_ge(vS, NCLB_DONE)
                bwait(t, bk_i)
                t.matmul(pb[bk_i][0:16, 0:128], cla_all[:, tc * 16:(tc + 1) * 16], id_f[:],
                         start=True, stop=True)
                t.matmul(pb[bk_i][0:16, 128:256], nclb[:, tc * 16:(tc + 1) * 16], id_f[:],
                         start=True, stop=True)
                t.drain().then_inc(tS, 1)
            inc("tS", 1)
            mtr = C["tS"]

            @block.vector
            def _(v, tc=tc, bk_i=bk_i, mtr=mtr):
                v.wait_ge(tS, mtr)
                v.tensor_copy(claT_s[:, tc * 128:(tc + 1) * 128], pb[bk_i][0:16, 0:128])
                v.tensor_copy(nclbT_s[:, tc * 128:(tc + 1) * 128], pb[bk_i][0:16, 128:256])
                v.drain().then_inc(vS, 1)
            inc("vS", 1)
            trev.append(C["vS"])
            bankfree[bk_i] = ("vS", C["vS"])
        TR_DONE = C["vS"]

        @block.tensor
        def _(t):
            bwait(t, 0)
            t.wait_ge(vS, CLA_DONE)
            t.matmul(pb[0][0:1, 0:128], e127_f[:, 0:1], cla_all[:], start=True, stop=True
                     ).then_inc(tS, 1)
        inc("tS", 1)
        CEND_MM = C["tS"]

        @block.vector
        def _(v):
            v.wait_ge(tS, CEND_MM)
            v.tensor_copy(cend_row[:], pb[0][0:1, 0:128]).then_inc(vS, 1)
            inc("vS", 1)
            v.drain()
            v.tensor_reduce(out=g_pub[:], in_=cend_row[:].rearrange("o (t h) -> o h t", t=8),
                            axis=AX, op=ALU.add)
            v.drain().then_inc(vS, 1)
        inc("vS", 1)
        GPUB = C["vS"]
        bankfree[0] = ("vS", GPUB)

        @block.scalar
        def _(s):
            s.wait_ge(vS, GPUB)
            s.activation(out=rowA[0:1, 0:128], in_=cend_row[:], func=AF.Exp)
            s.drain().then_inc(sS, 1)
        inc("sS", 1)
        ETD = C["sS"]

        @block.tensor
        def _(t):
            t.wait_ge(sS, ETD)
            bwait(t, 1, 2)
            t.matmul(pb[1][:, 0:128], onef[0:1, 0:128], rowA[0:1, 0:128],
                     start=True, stop=True)
            t.matmul(pb[2][:, 0:128], onef[0:1, 0:128], cend_row[0:1, 0:128],
                     start=True, stop=True)
            t.drain().then_inc(tS, 1)
        inc("tS", 1)
        TDB_MM = C["tS"]

        @block.vector
        def _(v):
            v.wait_ge(tS, TDB_MM)
            v.tensor_copy(td_b[:], pb[1][:, 0:128]).then_inc(vS, 1)
            inc("vS", 1)
            v.tensor_tensor(out=stage2[:, 0:128], in0=pb[2][:, 0:128], in1=nclb[:],
                            op=ALU.add)
            v.drain().then_inc(vS, 1)
        inc("vS", 1)
        WARG = C["vS"]
        bankfree[1] = ("vS", WARG)
        bankfree[2] = ("vS", WARG)

        @block.scalar
        def _(s):
            s.wait_ge(vS, WARG)
            s.activation(out=wfac_all[:], in_=stage2[:, 0:128], func=AF.Exp)
            s.drain().then_inc(sS, 1)
        inc("sS", 1)
        WFAC = C["sS"]

        if dbg:
            @block.scalar
            def _(s):
                s.wait_ge(vS, GAGB)
                s.drain()
                s.dma_start(out=dbg_out["d_ga"][:, 0:16], in_=ga_t[:]).then_inc(qC, 16)
                s.dma_start(out=dbg_out["d_ga"][:, 16:32], in_=gb_t[:]).then_inc(qC, 16)
            inc("qC", 32)

            @block.scalar
            def _(s):
                s.wait_ge(sS, WFAC)
                s.dma_start(out=dbg_out["d_la"], in_=la_all[:]).then_inc(qC, 16)
                s.dma_start(out=dbg_out["d_lnb"], in_=lnb_all[:]).then_inc(qC, 16)
                s.dma_start(out=dbg_out["d_cla"], in_=cla_all[:]).then_inc(qC, 16)
                s.dma_start(out=dbg_out["d_wfac"], in_=wfac_all[:]).then_inc(qC, 16)
                s.dma_start(out=dbg_out["d_td"], in_=td_b[:]).then_inc(qC, 16)
                s.dma_start(out=dbg_out["d_cd"], in_=cd_all[:]).then_inc(qC, 16)
            inc("qC", 96)

        if STOP_AFTER == "cla":
            _bail()
            return nc, dbg_out

        es_c = contextlib.ExitStack()
        substacks.append(es_c)
        SBC = lambda n, s, d: es_c.enter_context(nc.sbuf_tensor(n, s, d))
        kt = [SBC("kt0", [128, D], BF16), SBC("kt1", [128, D], BF16)]
        vt = [SBC("vt0", [128, D], BF16), SBC("vt1", [128, D], BF16)]
        qt = [SBC("qt0", [128, D], BF16), SBC("qt1", [128, D], BF16)]
        xt = [SBC("xt0", [128, D], F32), SBC("xt1", [128, D], F32)]
        kwb = SBC("kwb", [128, D], BF16)
        state_s = SBC("state_s", [128, D], F32)
        state_bf = SBC("state_bf", [128, D], BF16)
        kfeat = SBC("kfeat", [128, D], BF16)
        Lscr = [SBC("Lscr0", [128, D], F32), SBC("Lscr1", [128, D], F32)]
        dmb = [SBC("dmb0", [128, 512], F32), SBC("dmb1", [128, 512], F32)]
        eDb = [SBC("eDb0", [128, 512], F32), SBC("eDb1", [128, 512], F32)]
        ysb = [SBC("ysb0", [128, 512], F32), SBC("ysb1", [128, 512], F32)]
        atb = [SBC("atb0", [128, 512], BF16), SBC("atb1", [128, 512], BF16)]

        if STOP_AFTER == "clb":
            _bail()
            return nc, dbg_out

        p1_kld = []
        p1_vld = []
        p1_kw = []
        p1_up = []
        for tc in range(NCH):
            @block.sync
            def _(y, tc=tc, F=(C["tS"], C["vS"], C["sS"], C["qC"])):
                if tc == 0:
                    y.wait_ge(tS, F[0]); y.wait_ge(vS, F[1])
                    y.wait_ge(sS, F[2]); y.wait_ge(qC, F[3])
                if tc >= 2:
                    y.wait_ge(vS, p1_kw[tc - 2])
                y.dma_start(out=kt[tc % 2][:], in_=k_sp.ap()[tc]).then_inc(qS, 16)
                inc("qS", 16)
                p1_kld.append(C["qS"])
                if tc >= 2:
                    y.wait_ge(tS, p1_up[tc - 2][0])
                y.dma_start(out=vt[tc % 2][:], in_=v_sp.ap()[tc]).then_inc(qS, 16)
                inc("qS", 16)
                p1_vld.append(C["qS"])

            @block.vector
            def _(v, tc=tc):
                v.wait_ge(qS, p1_kld[tc])
                if tc == 0:
                    v.wait_ge(sS, WFAC)
                for h in range(NH):
                    v.tensor_scalar(out=kwb[:, h * 128:(h + 1) * 128],
                                    in0=kt[tc % 2][:, h * 128:(h + 1) * 128],
                                    scalar1=wfac_all[:, tc * 16 + h:tc * 16 + h + 1],
                                    scalar2=0.0, op0=ALU.mult, op1=ALU.add)
                v.drain().then_inc(vS, 1)
            inc("vS", 1)
            p1_kw.append(C["vS"])

            mm_marks = []
            for grp in range(4):
                bk_i = grp + 4 * (tc % 2)
                @block.tensor
                def _(t, tc=tc, grp=grp, bk_i=bk_i):
                    t.wait_ge(vS, p1_kw[tc])
                    t.wait_ge(qS, p1_vld[tc])
                    bwait(t, bk_i)
                    for hh in range(4):
                        h = grp * 4 + hh
                        t.matmul(pb[bk_i][:, hh * 128:(hh + 1) * 128],
                                 kwb[:, h * 128:(h + 1) * 128],
                                 vt[tc % 2][:, h * 128:(h + 1) * 128],
                                 start=True, stop=True)
                    t.drain().then_inc(tS, 1)
                inc("tS", 1)
                mm_marks.append(C["tS"])

                @block.vector
                def _(v, tc=tc, grp=grp, bk_i=bk_i, mm=mm_marks[grp]):
                    v.wait_ge(tS, mm)
                    for hh in range(4):
                        h = grp * 4 + hh
                        if tc == 0:
                            v.tensor_copy(state_s[:, h * 128:(h + 1) * 128],
                                          pb[bk_i][:, hh * 128:(hh + 1) * 128])
                        else:
                            v.scalar_tensor_tensor(
                                out=state_s[:, h * 128:(h + 1) * 128],
                                in0=state_s[:, h * 128:(h + 1) * 128],
                                scalar=td_b[:, tc * 16 + h:tc * 16 + h + 1],
                                in1=pb[bk_i][:, hh * 128:(hh + 1) * 128],
                                op0=ALU.mult, op1=ALU.add)
                    v.drain().then_inc(vS, 1)
                inc("vS", 1)
                bankfree[bk_i] = ("vS", C["vS"])
            p1_up.append((mm_marks[-1], C["vS"]))
        P1_DONE = C["vS"]

        if STOP_AFTER == "pass1":
            _bail()
            return nc, dbg_out

        @block.gpsimd
        def _(g):
            g.wait_ge(vS, P1_DONE)
            g.dma_start(out=cc_st_in.ap()[0:128 * D].rearrange("(p c) -> p c", p=128),
                        in_=state_s[:]).then_inc(qG, 16)
            inc("qG", 16)
            g.dma_start(out=cc_st_in.ap()[128 * D:128 * D + 16]
                        .rearrange("(o c) -> o c", o=1), in_=g_pub[:]).then_inc(qG, 16)
            inc("qG", 16)
            g.wait_ge(qG, C["qG"])
            g.collective_compute(
                "AllGather", ALU.bypass, replica_groups=[list(range(NB))],
                ins=[cc_st_in.ap().opt()], outs=[cc_st_out.ap().opt()],
            ).then_inc(cS, 1)
        inc("cS", 1)
        AG3 = C["cS"]

        if STOP_AFTER == "ag3":
            _bail()
            return nc, dbg_out

        @block.sync
        def _(y):
            y.wait_ge(cS, AG3)
            y.dma_start(out=gsb[:],
                        in_=cc_st_out.ap().rearrange("(r x) -> r x", r=NB)
                        [:, 128 * D:128 * D + 16]).then_inc(qS, 16)
        inc("qS", 16)
        GS3 = C["qS"]

        @block.tensor
        def _(t):
            t.wait_ge(qS, GS3)
            bwait(t, 6)
            t.matmul(pb[6][0:8, 0:16], selT_f[:], gsb[:], start=True, stop=True
                     ).then_inc(tS, 1)
        inc("tS", 1)
        SEL_MM = C["tS"]

        @block.scalar
        def _(s):
            s.wait_ge(tS, SEL_MM)
            s.activation(out=coef_sb[:], in_=pb[6][0:8, 0:16], func=AF.Exp
                         ).then_inc(sS, 1)
        inc("sS", 1)
        COEF_E = C["sS"]

        @block.vector
        def _(v):
            v.wait_ge(sS, COEF_E)
            v.tensor_scalar(out=coef_sb[:], in0=coef_sb[:], scalar1=amask_f[0:8, 0:1],
                            scalar2=0.0, op0=ALU.mult, op1=ALU.add)
            v.drain().then_inc(vS, 1)
        inc("vS", 1)
        COEF_M = C["vS"]
        bankfree[6] = ("vS", COEF_M)

        @block.vector
        def _(v):
            v.wait_ge(vS, COEF_M)
            for r in range(NB):
                v.tensor_copy(cmsk[:, r * 16:(r + 1) * 16], coef_sb[:, 0:16])
            v.drain()
            v.tensor_tensor(out=cmsk[:], in0=cmsk[:], in1=bm16_f[:], op=ALU.mult)
            v.drain().then_inc(vS, 1)
        inc("vS", 1)
        CMSK = C["vS"]

        @block.tensor
        def _(t):
            t.wait_ge(vS, CMSK)
            bwait(t, 7)
            t.matmul(pb[7][:, 0:128], onef[0:8, 0:128], cmsk[:], start=True, stop=True
                     ).then_inc(tS, 1)
        inc("tS", 1)
        CB_MM = C["tS"]

        @block.vector
        def _(v):
            v.wait_ge(tS, CB_MM)
            v.tensor_copy(coefB[:], pb[7][:, 0:128]).then_inc(vS, 1)
        inc("vS", 1)
        CB_EV = C["vS"]
        bankfree[7] = ("vS", CB_EV)

        if dbg:
            @block.scalar
            def _(s):
                s.wait_ge(vS, CB_EV)
                s.dma_start(out=dbg_out["d_coefb"], in_=coefB[:]).then_inc(qC, 16)
            inc("qC", 16)

        L_ld = []
        L_cb = []
        for r in range(NB):
            @block.sync
            def _(y, r=r):
                if r == 0:
                    y.wait_ge(cS, AG3)
                if r >= 2:
                    y.wait_ge(vS, L_cb[r - 2])
                y.dma_start(out=Lscr[r % 2][:],
                            in_=cc_st_out.ap()[r * ST_SZ:r * ST_SZ + 128 * D]
                            .rearrange("(p c) -> p c", p=128)).then_inc(qS, 16)
            inc("qS", 16)
            L_ld.append(C["qS"])

            @block.vector
            def _(v, r=r):
                v.wait_ge(qS, L_ld[r])
                if r == 0:
                    v.wait_ge(vS, CB_EV)
                for h in range(NH):
                    if r == 0:
                        v.tensor_scalar(out=state_s[:, h * 128:(h + 1) * 128],
                                        in0=Lscr[0][:, h * 128:(h + 1) * 128],
                                        scalar1=coefB[:, h:h + 1], scalar2=0.0,
                                        op0=ALU.mult, op1=ALU.add)
                    else:
                        v.scalar_tensor_tensor(
                            out=state_s[:, h * 128:(h + 1) * 128],
                            in0=Lscr[r % 2][:, h * 128:(h + 1) * 128],
                            scalar=coefB[:, r * 16 + h:r * 16 + h + 1],
                            in1=state_s[:, h * 128:(h + 1) * 128],
                            op0=ALU.mult, op1=ALU.add)
                v.drain().then_inc(vS, 1)
            inc("vS", 1)
            L_cb.append(C["vS"])
        SENTRY = C["vS"]

        if dbg:
            @block.scalar
            def _(s):
                s.wait_ge(vS, SENTRY)
                s.dma_start(out=dbg_out["d_state"], in_=state_s[:]).then_inc(qC, 16)
            inc("qC", 16)

            @block.gpsimd
            def _(g):
                g.wait_ge(qC, V_DONE)
                g.dma_start(out=dbg_out["d_q"].rearrange("(h s) t -> h s t", h=NH),
                            in_=q_sp.ap()).then_inc(qG, 16)
                g.dma_start(out=dbg_out["d_k"].rearrange("(c t) d -> c t d", c=NCH),
                            in_=k_sp.ap()).then_inc(qG, 16)
                g.dma_start(out=dbg_out["d_v"].rearrange("(c t) d -> c t d", c=NCH),
                            in_=v_sp.ap()).then_inc(qG, 16)
            inc("qG", 48)

        if STOP_AFTER == "sentry":
            _bail()
            return nc, dbg_out

        p2_kld = [None] * NCH
        p2_vld = [None] * NCH
        p2_qld = [None] * NCH
        p2_xld = [None] * NCH
        p2_kwm = [None] * NCH
        p2_ktr_use = [None] * NCH
        p2_mm_last = [None] * NCH
        p2_xst = [None] * NCH
        p2_yg_last = [None] * NCH
        dr1_prev = [None, None]
        for tc in range(NCH):
            @block.sync
            def _(y, tc=tc):
                if tc >= 2:
                    y.wait_ge(vS, p2_ktr_use[tc - 2])
                    y.wait_ge(tS, p2_mm_last[tc - 2])
                y.dma_start(out=kt[tc % 2][:], in_=k_sp.ap()[tc]).then_inc(qS, 16)
                inc("qS", 16)
                p2_kld[tc] = C["qS"]
                y.dma_start(out=vt[tc % 2][:], in_=v_sp.ap()[tc]).then_inc(qS, 16)
                inc("qS", 16)
                p2_vld[tc] = C["qS"]
                y.dma_start(out=qt[tc % 2][:].rearrange("p (h c) -> p h c", h=NH),
                            in_=q_sp.ap()[:, :, tc * 128:(tc + 1) * 128]
                            .rearrange("h s c -> s h c")).then_inc(qS, 16)
                inc("qS", 16)
                p2_qld[tc] = C["qS"]
                if tc >= 2:
                    y.wait_ge(qC, p2_xst[tc - 2])
                if tc == 0:
                    y.wait_ge(qC, XSPILL)
                y.dma_start(out=xt[tc % 2][:].rearrange("p (j c) -> p j c", j=NDC),
                            in_=x_sp.ap().rearrange("(j p) t -> p j t", p=128)
                            [:, :, tc * 128:(tc + 1) * 128]).then_inc(qS, 16)
                inc("qS", 16)
                p2_xld[tc] = C["qS"]

            @block.vector
            def _(v, tc=tc):
                if tc == 0:
                    v.wait_ge(vS, SENTRY)
                if tc >= 1 and p2_yg_last[tc - 1] is not None:
                    v.wait_ge(tS, p2_yg_last[tc - 1])
                v.tensor_copy(state_bf[:], state_s[:]).then_inc(vS, 1)
                inc("vS", 1)
                v.wait_ge(qS, p2_kld[tc])
                if tc >= 1 and p2_mm_last[tc - 1] is not None:
                    v.wait_ge(tS, p2_mm_last[tc - 1])
                for h in range(NH):
                    v.tensor_scalar(out=kwb[:, h * 128:(h + 1) * 128],
                                    in0=kt[tc % 2][:, h * 128:(h + 1) * 128],
                                    scalar1=wfac_all[:, tc * 16 + h:tc * 16 + h + 1],
                                    scalar2=0.0, op0=ALU.mult, op1=ALU.add)
                if dr1_prev[0] is not None:
                    v.wait_ge(tS, max(x for x in dr1_prev if x is not None))
                for rr in range(4):
                    v.tensor_copy(ctrep[:, rr * 128:(rr + 1) * 128],
                                  claT_s[:, tc * 128:(tc + 1) * 128])
                v.drain().then_inc(vS, 1)
            inc("vS", 1)
            p2_kwm[tc] = C["vS"]
            CAST_M = C["vS"] - 1

            for grp in range(4):
                pk = (tc * 4 + grp) % 2
                bA = pk * 4 + 0
                bB = pk * 4 + 1
                bC = pk * 4 + 2
                bD = pk * 4 + 3

                @block.tensor
                def _(t, tc=tc, grp=grp, bA=bA):
                    bwait(t, bA)
                    t.wait_ge(qS, p2_kld[tc])
                    for hh in range(4):
                        h = grp * 4 + hh
                        t.transpose(pb[bA][:].bitcast(BF16)[:, hh * 128:(hh + 1) * 128],
                                    kt[tc % 2][:, h * 128:(h + 1) * 128], id_bf[:])
                    t.drain().then_inc(tS, 1)
                inc("tS", 1)
                T1 = C["tS"]

                @block.vector
                def _(v, grp=grp, bA=bA, T1=T1):
                    v.wait_ge(tS, T1)
                    v.tensor_copy(kfeat[:, grp * 512:(grp + 1) * 512],
                                  pb[bA][:].bitcast(BF16)[:, 0:512]).then_inc(vS, 1)
                inc("vS", 1)
                V1 = C["vS"]

                @block.vector
                def _(v, tc=tc, grp=grp, pk=pk):
                    v.wait_ge(vS, p2_kwm[tc])
                    if dr1_prev[pk] is not None:
                        v.wait_ge(tS, dr1_prev[pk])
                    v.tensor_tensor(out=dr1[pk][:], in0=ctrep[:],
                                    in1=psel_f[0:16, grp * 512:(grp + 1) * 512]
                                    .rearrange("k (h c) -> k h c", h=4)
                                    if False else
                                    psel_f[0:16, grp * 512:(grp + 1) * 512],
                                    op=ALU.mult)
                    v.drain().then_inc(vS, 1)
                inc("vS", 1)
                VR1 = C["vS"]

                @block.tensor
                def _(t, tc=tc, grp=grp, pk=pk, bB=bB, bC=bC, V1=V1, VR1=VR1):
                    bwait(t, bB)
                    t.wait_ge(vS, V1)
                    t.wait_ge(qS, p2_qld[tc])
                    for hh in range(4):
                        h = grp * 4 + hh
                        t.matmul(pb[bB][:, hh * 128:(hh + 1) * 128],
                                 kfeat[:, grp * 512 + hh * 128: grp * 512 + (hh + 1) * 128],
                                 qt[tc % 2][:, h * 128:(h + 1) * 128],
                                 start=True, stop=True)
                    t.drain().then_inc(tS, 1)
                    inc("tS", 1)
                    bwait(t, bC)
                    t.wait_ge(vS, VR1)
                    t.matmul(pb[bC][:, 0:512], onef[0:16, 0:128], dr1[pk][:],
                             start=True, stop=False)
                    t.matmul(pb[bC][:, 0:512], nclbT_s[0:16, tc * 128:(tc + 1) * 128],
                             psel_f[0:16, grp * 512:(grp + 1) * 512],
                             start=False, stop=True)
                    t.drain().then_inc(tS, 1)
                inc("tS", 1)
                T23 = C["tS"]
                dr1_prev[pk] = T23

                @block.vector
                def _(v, pk=pk, bC=bC, T23=T23):
                    v.wait_ge(tS, T23)
                    v.tensor_tensor(out=dmb[pk][:], in0=pb[bC][:, :], in1=m0_f[:],
                                    op=ALU.add).then_inc(vS, 1)
                inc("vS", 1)
                V2 = C["vS"]

                @block.scalar
                def _(s, pk=pk, V2=V2):
                    s.wait_ge(vS, V2)
                    s.activation(out=eDb[pk][:], in_=dmb[pk][:], func=AF.Exp
                                 ).then_inc(sS, 1)
                inc("sS", 1)
                S1 = C["sS"]

                @block.vector
                def _(v, pk=pk, bB=bB, S1=S1):
                    v.wait_ge(sS, S1)
                    v.tensor_tensor(out=atb[pk][:], in0=pb[bB][:, :], in1=eDb[pk][:],
                                    op=ALU.mult).then_inc(vS, 1)
                inc("vS", 1)
                V3 = C["vS"]
                if dbg and tc == 0 and grp == 0:
                    @block.gpsimd
                    def _(g, pk=pk, V3=V3):
                        g.wait_ge(vS, V3)
                        g.dma_start(out=dbg_out["d_dm"], in_=dmb[pk][:]).then_inc(qG, 16)
                        g.dma_start(out=dbg_out["d_eD"], in_=eDb[pk][:]).then_inc(qG, 16)
                        g.dma_start(out=dbg_out["d_at"], in_=atb[pk][:]).then_inc(qG, 16)
                    inc("qG", 48)

                @block.tensor
                def _(t, tc=tc, grp=grp, pk=pk, bA=bA, bB=bB, bC=bC, V3=V3, CAST_M=CAST_M):
                    t.wait_ge(vS, V3)
                    t.wait_ge(qS, p2_vld[tc])
                    for hh in range(4):
                        h = grp * 4 + hh
                        t.matmul(pb[bA][:, hh * 128:(hh + 1) * 128],
                                 atb[pk][:, hh * 128:(hh + 1) * 128],
                                 vt[tc % 2][:, h * 128:(h + 1) * 128],
                                 start=True, stop=True)
                    t.drain().then_inc(tS, 1)
                    inc("tS", 1)
                    t.wait_ge(vS, CAST_M)
                    for hh in range(4):
                        h = grp * 4 + hh
                        t.matmul(pb[bC][:, hh * 128:(hh + 1) * 128],
                                 qt[tc % 2][:, h * 128:(h + 1) * 128],
                                 state_bf[:, h * 128:(h + 1) * 128],
                                 start=True, stop=True)
                    t.drain().then_inc(tS, 1)
                    inc("tS", 1)
                    t.wait_ge(vS, p2_kwm[tc])
                    for hh in range(4):
                        h = grp * 4 + hh
                        t.matmul(pb[bB][:, hh * 128:(hh + 1) * 128],
                                 kwb[:, h * 128:(h + 1) * 128],
                                 vt[tc % 2][:, h * 128:(h + 1) * 128],
                                 start=True, stop=True)
                    t.drain().then_inc(tS, 1)
                inc("tS", 1)
                T456 = C["tS"]
                p2_yg_last[tc] = T456 - 1

                @block.vector
                def _(v, tc=tc, grp=grp, pk=pk, bA=bA, bC=bC, T456=T456):
                    v.wait_ge(tS, T456 - 1)
                    for hh in range(4):
                        h = grp * 4 + hh
                        v.tensor_scalar(out=ysb[pk][:, hh * 128:(hh + 1) * 128],
                                        in0=pb[bC][:, hh * 128:(hh + 1) * 128],
                                        scalar1=cd_all[:, tc * 16 + h:tc * 16 + h + 1],
                                        scalar2=0.0, op0=ALU.mult, op1=ALU.add)
                    for hh in range(4):
                        v.scalar_tensor_tensor(
                            out=ysb[pk][:, hh * 128:(hh + 1) * 128],
                            in0=pb[bA][:, hh * 128:(hh + 1) * 128], scalar=1.0,
                            in1=ysb[pk][:, hh * 128:(hh + 1) * 128],
                            op0=ALU.mult, op1=ALU.add)
                    v.drain().then_inc(vS, 1)
                inc("vS", 1)
                V4 = C["vS"]
                bankfree[bA] = ("vS", V4)
                bankfree[bC] = ("vS", V4)

                if dbg and tc == 0 and grp == 0:
                    @block.gpsimd
                    def _(g, pk=pk, V4=V4):
                        g.wait_ge(vS, V4)
                        g.dma_start(out=dbg_out["d_ysb"], in_=ysb[pk][:]).then_inc(qG, 16)
                    inc("qG", 16)

                @block.tensor
                def _(t, pk=pk, bD=bD, V4=V4):
                    bwait(t, bD)
                    t.wait_ge(vS, V4)
                    for hh in range(4):
                        t.transpose(pb[bD][:, hh * 128:(hh + 1) * 128],
                                    ysb[pk][:, hh * 128:(hh + 1) * 128], id_f[:])
                    t.drain().then_inc(tS, 1)
                inc("tS", 1)
                T7 = C["tS"]

                @block.vector
                def _(v, tc=tc, grp=grp, pk=pk, bB=bB, bD=bD, T7=T7):
                    v.wait_ge(tS, T7)
                    v.wait_ge(qS, p2_xld[tc])
                    for hh in range(4):
                        h = grp * 4 + hh
                        v.scalar_tensor_tensor(
                            out=xt[tc % 2][:, h * 128:(h + 1) * 128],
                            in0=pb[bD][:, hh * 128:(hh + 1) * 128],
                            scalar=ga_t[:, h:h + 1],
                            in1=xt[tc % 2][:, h * 128:(h + 1) * 128],
                            op0=ALU.mult, op1=ALU.add)
                    v.drain().then_inc(vS, 1)
                    inc("vS", 1)
                    for hh in range(4):
                        h = grp * 4 + hh
                        v.scalar_tensor_tensor(
                            out=state_s[:, h * 128:(h + 1) * 128],
                            in0=state_s[:, h * 128:(h + 1) * 128],
                            scalar=td_b[:, tc * 16 + h:tc * 16 + h + 1],
                            in1=pb[bB][:, hh * 128:(hh + 1) * 128],
                            op0=ALU.mult, op1=ALU.add)
                    v.drain().then_inc(vS, 1)
                inc("vS", 1)
                V56 = C["vS"]
                bankfree[bB] = ("vS", V56)
                bankfree[bD] = ("vS", V56 - 1)
                if grp == 3:
                    p2_ktr_use[tc] = p2_kwm[tc]
                    p2_mm_last[tc] = T456

            @block.scalar
            def _(s, tc=tc):
                s.wait_ge(vS, C["vS"] - 1)
                s.dma_start(out=x_sp.ap().rearrange("(j p) t -> p j t", p=128)
                            [:, :, tc * 128:(tc + 1) * 128],
                            in_=xt[tc % 2][:].rearrange("p (j c) -> p j c", j=NDC)
                            ).then_inc(qC, 16)
            inc("qC", 16)
            p2_xst[tc] = C["qC"]
        P2_DONE = C["qC"]
        es_c.close()
        es_p0b.close()
        es_a.close()

        if STOP_AFTER == "pass2":
            _bail()
            return nc, dbg_out

        es_q = contextlib.ExitStack()
        substacks.append(es_q)
        xq_bf = es_q.enter_context(nc.sbuf_tensor("xq_bf", [128, NDC * 1024], BF16))
        r8 = es_q.enter_context(nc.sbuf_tensor("r8", [8, 1024], F32))
        sb2_ball = es_q.enter_context(nc.sbuf_tensor("sb2_ball", [128, 1024], F32))
        fr4 = es_q.enter_context(nc.sbuf_tensor("fr4", [4, 512], F32))
        sr4 = es_q.enter_context(nc.sbuf_tensor("sr4", [4, 512], F32))
        es_l = contextlib.ExitStack()
        substacks.append(es_l)
        x1_f = es_l.enter_context(nc.sbuf_tensor("x1_f", [128, NDC * 1024], F32))
        x1_bf = es_l.enter_context(nc.sbuf_tensor("x1_bf", [128, NDC * 1024], BF16))
        x2c = [es_l.enter_context(nc.sbuf_tensor("x2c%d" % i, [128, 1024], BF16))
               for i in range(2)]
        rowC2 = es_l.enter_context(nc.sbuf_tensor("rowC2", [1, 1024], F32))
        i8rep = es_l.enter_context(nc.sbuf_tensor("i8rep", [8, 1024], F32))

        @block.sync
        def _(y, F=(C["tS"], C["vS"], C["sS"])):
            y.wait_ge(qC, P2_DONE)
            y.wait_ge(tS, F[0]); y.wait_ge(vS, F[1]); y.wait_ge(sS, F[2])
            y.dma_start(out=x1_f[:].rearrange("p (j t) -> p j t", j=NDC),
                        in_=x_sp.ap().rearrange("(j p) t -> p j t", p=128)
                        ).then_inc(qS, 16)
        inc("qS", 16)
        X1LD = C["qS"]

        @block.gpsimd
        def _(g):
            g.wait_ge(qC, P2_DONE)
            g.dma_start(out=x1_bf[:].rearrange("p (j t) -> p j t", j=NDC),
                        in_=x_sp.ap().rearrange("(j p) t -> p j t", p=128)
                        ).then_inc(qG, 16)
        inc("qG", 16)
        X1BLD = C["qG"]

        sq2 = []
        st2 = []
        for j in range(NDC):
            @block.scalar
            def _(s, j=j):
                if j == 0:
                    s.wait_ge(qS, X1LD)
                if j >= 2:
                    s.wait_ge(tS, st2[j - 2])
                s.activation(out=x2c[j % 2][:], in_=x1_f[:, j * 1024:(j + 1) * 1024],
                             func=AF.Square).then_inc(sS, 1)
            inc("sS", 1)
            sq2.append(C["sS"])

            @block.tensor
            def _(t, j=j):
                if j == 0:
                    t.wait_ge(qG, X1BLD)
                    bwait(t, 4, 5, 6, 7)
                t.wait_ge(sS, sq2[j])
                for tt in range(2):
                    t.matmul(pb[4 + tt][0:1, :], onebf_c[:, 0:1],
                             x1_bf[:, j * 1024 + tt * 512: j * 1024 + (tt + 1) * 512],
                             start=(j == 0), stop=(j == NDC - 1))
                    t.matmul(pb[6 + tt][0:1, :], onebf_c[:, 0:1],
                             x2c[j % 2][:, tt * 512:(tt + 1) * 512],
                             start=(j == 0), stop=(j == NDC - 1))
                t.drain().then_inc(tS, 1)
            inc("tS", 1)
            st2.append(C["tS"])
        STATS2 = C["tS"]

        @block.vector
        def _(v):
            v.wait_ge(tS, STATS2)
            v.tensor_scalar(out=rowA[0:1, 0:512], in0=pb[4][0:1, :], scalar1=1.0 / 2048.0,
                            scalar2=0.0, op0=ALU.mult, op1=ALU.add)
            v.tensor_scalar(out=rowA[0:1, 512:1024], in0=pb[5][0:1, :], scalar1=1.0 / 2048.0,
                            scalar2=0.0, op0=ALU.mult, op1=ALU.add)
            v.tensor_scalar(out=rowB[0:1, 0:512], in0=pb[6][0:1, :], scalar1=1.0 / 2048.0,
                            scalar2=0.0, op0=ALU.mult, op1=ALU.add)
            v.tensor_scalar(out=rowB[0:1, 512:1024], in0=pb[7][0:1, :], scalar1=1.0 / 2048.0,
                            scalar2=0.0, op0=ALU.mult, op1=ALU.add)
            v.drain()
            v.tensor_tensor(out=rowC2[:], in0=rowA[:], in1=rowA[:], op=ALU.mult)
            v.drain()
            v.tensor_tensor(out=rowB[:], in0=rowB[:], in1=rowC2[:], op=ALU.subtract)
            v.drain().then_inc(vS, 1)
        inc("vS", 1)
        VAR2 = C["vS"]
        for b_ in (4, 5, 6, 7):
            bankfree[b_] = ("vS", VAR2)

        @block.scalar
        def _(s):
            s.wait_ge(vS, VAR2)
            s.activation(out=rowC2[:], in_=rowB[:], func=AF.Ln, bias=cEPS5[0:1, 0:1])
            s.activation(out=rowB[:], in_=rowC2[:], func=AF.Exp, scale=-0.5)
            s.drain().then_inc(sS, 1)
        inc("sS", 1)
        ISTD2 = C["sS"]

        @block.tensor
        def _(t):
            t.wait_ge(sS, ISTD2)
            bwait(t, 0, 1, 2, 3)
            for tt in range(2):
                t.matmul(pb[0 + tt][:, :], onef[0:1, 0:128],
                         rowA[0:1, tt * 512:(tt + 1) * 512], start=True, stop=True)
                t.matmul(pb[2 + tt][:, :], onef[0:1, 0:128],
                         rowB[0:1, tt * 512:(tt + 1) * 512], start=True, stop=True)
            t.drain().then_inc(tS, 1)
        inc("tS", 1)
        BC2 = C["tS"]

        @block.vector
        def _(v):
            v.wait_ge(tS, BC2)
            v.tensor_copy(stage0[:, 0:512], pb[0][:, :])
            v.tensor_copy(stage0[:, 512:1024], pb[1][:, :])
            v.tensor_copy(stage1[:, 0:512], pb[2][:, :])
            v.tensor_copy(stage1[:, 512:1024], pb[3][:, :])
            v.drain().then_inc(vS, 1)
        inc("vS", 1)
        MB2 = C["vS"]
        for b_ in (0, 1, 2, 3):
            bankfree[b_] = ("vS", MB2)

        for j in range(NDC):
            @block.vector
            def _(v, j=j):
                if j == 0:
                    v.wait_ge(vS, MB2)
                cs = slice(j * 1024, (j + 1) * 1024)
                v.tensor_tensor(out=stage2[:], in0=x1_f[:, cs], in1=stage0[:],
                                op=ALU.subtract)
                v.tensor_tensor(out=stage2[:], in0=stage2[:], in1=stage1[:], op=ALU.mult)
                v.tensor_scalar(out=x1_f[:, cs], in0=stage2[:],
                                scalar1=cbmod[:, j:j + 1], scalar2=mymod[:, 48 + j:49 + j],
                                op0=ALU.mult, op1=ALU.add)
                v.drain().then_inc(vS, 1)
            inc("vS", 1)
        H2_DONE = C["vS"]

        mxacc = []
        for tsub in range(NCH):
            for jg in range(4):
                bk_i = jg % 2
                @block.tensor
                def _(t, tsub=tsub, jg=jg, bk_i=bk_i):
                    if tsub == 0 and jg == 0:
                        t.wait_ge(vS, H2_DONE)
                    bwait(t, bk_i)
                    for i in range(4):
                        j = jg * 4 + i
                        t.transpose(pb[bk_i][:, i * 128:(i + 1) * 128],
                                    x1_f[:, j * 1024 + tsub * 128: j * 1024 + (tsub + 1) * 128],
                                    id_f[:])
                    t.drain().then_inc(tS, 1)
                inc("tS", 1)
                mtr2 = C["tS"]

                @block.vector
                def _(v, tsub=tsub, jg=jg, bk_i=bk_i, mtr2=mtr2):
                    v.wait_ge(tS, mtr2)
                    v.tensor_reduce(out=gtmp[:, 8:9], in_=pb[bk_i][:, :], axis=AX,
                                    op=ALU.max, apply_absolute_value=True)
                    v.drain()
                    if jg == 0:
                        v.tensor_copy(gtmp[:, tsub:tsub + 1], gtmp[:, 8:9])
                    else:
                        v.tensor_tensor(out=gtmp[:, tsub:tsub + 1],
                                        in0=gtmp[:, tsub:tsub + 1], in1=gtmp[:, 8:9],
                                        op=ALU.max)
                    v.drain().then_inc(vS, 1)
                inc("vS", 1)
                bankfree[bk_i] = ("vS", C["vS"])
        MX_DONE = C["vS"]

        if dbg:
            @block.scalar
            def _(s):
                s.wait_ge(vS, MX_DONE)
                s.dma_start(out=dbg_out["d_mx"], in_=gtmp[:, 0:8]).then_inc(qC, 16)
            inc("qC", 16)

        @block.vector
        def _(v):
            v.wait_ge(vS, MX_DONE)
            v.tensor_scalar(out=wsmall[:, 0:8], in0=gtmp[:, 0:8], scalar1=1.0 / 127.0,
                            scalar2=1e-5, op0=ALU.mult, op1=ALU.add)
            v.drain().then_inc(vS, 1)
        inc("vS", 1)
        S2COL = C["vS"]

        @block.tensor
        def _(t):
            t.wait_ge(vS, S2COL)
            bwait(t, 2)
            t.matmul(pb[2][0:8, 0:128], wsmall[:, 0:8], id_f[:], start=True, stop=True
                     ).then_inc(tS, 1)
        inc("tS", 1)
        S2T_MM = C["tS"]

        @block.vector
        def _(v):
            v.wait_ge(tS, S2T_MM)
            v.tensor_copy(r8[:, 0:128], pb[2][0:8, 0:128]).then_inc(vS, 1)
            inc("vS", 1)
            v.drain()
            v.reciprocal(r8[:, 128:256], r8[:, 0:128])
            v.drain().then_inc(vS, 1)
        inc("vS", 1)
        INV2 = C["vS"]
        bankfree[2] = ("vS", INV2)

        if dbg:
            @block.scalar
            def _(s):
                s.wait_ge(vS, INV2)
                s.dma_start(out=dbg_out["d_s2"], in_=r8[0:8, 0:128]).then_inc(qC, 16)
            inc("qC", 16)

        @block.vector
        def _(v):
            v.wait_ge(vS, INV2)
            for rr in range(8):
                v.tensor_copy(i8rep[:, rr * 128:(rr + 1) * 128], r8[:, 128:256])
            v.drain()
            v.tensor_tensor(out=i8rep[:], in0=i8rep[:], in1=pos8_f[:], op=ALU.mult)
            v.drain().then_inc(vS, 1)
        inc("vS", 1)
        I8R = C["vS"]

        @block.tensor
        def _(t):
            t.wait_ge(vS, I8R)
            bwait(t, 0, 1)
            t.matmul(pb[0][:, :], onef[0:8, 0:128], i8rep[:, 0:512], start=True, stop=True)
            t.matmul(pb[1][:, :], onef[0:8, 0:128], i8rep[:, 512:1024], start=True, stop=True)
            t.drain().then_inc(tS, 1)
        inc("tS", 1)
        I2B_MM = C["tS"]

        @block.vector
        def _(v):
            v.wait_ge(tS, I2B_MM)
            v.tensor_copy(stage0[:, 0:512], pb[0][:, :])
            v.tensor_copy(stage0[:, 512:1024], pb[1][:, :])
            for rr in range(8):
                v.tensor_copy(i8rep[:, rr * 128:(rr + 1) * 128], r8[:, 0:128])
            v.drain()
            v.tensor_tensor(out=i8rep[:], in0=i8rep[:], in1=pos8_f[:], op=ALU.mult)
            v.drain().then_inc(vS, 1)
        inc("vS", 1)
        I2B = C["vS"]
        bankfree[0] = ("vS", I2B)
        bankfree[1] = ("vS", I2B)

        @block.tensor
        def _(t):
            t.wait_ge(vS, I2B)
            bwait(t, 2, 3)
            t.matmul(pb[2][:, :], onef[0:8, 0:128], i8rep[:, 0:512], start=True, stop=True)
            t.matmul(pb[3][:, :], onef[0:8, 0:128], i8rep[:, 512:1024], start=True, stop=True)
            t.drain().then_inc(tS, 1)
        inc("tS", 1)
        S2B_MM = C["tS"]

        @block.vector
        def _(v):
            v.wait_ge(tS, S2B_MM)
            v.tensor_copy(sb2_ball[:, 0:512], pb[2][:, :])
            v.tensor_copy(sb2_ball[:, 512:1024], pb[3][:, :])
            v.drain().then_inc(vS, 1)
        inc("vS", 1)
        S2BALL = C["vS"]
        bankfree[2] = ("vS", S2BALL)
        bankfree[3] = ("vS", S2BALL)

        for j in range(NDC):
            @block.vector
            def _(v, j=j):
                if j == 0:
                    v.wait_ge(vS, I2B)
                cs = slice(j * 1024, (j + 1) * 1024)
                v.tensor_tensor(out=stage2[:], in0=x1_f[:, cs], in1=stage0[:], op=ALU.mult)
                v.tensor_scalar(out=xq_bf[:, cs], in0=stage2[:], scalar1=MAGIC,
                                scalar2=MAGIC, op0=ALU.add, op1=ALU.subtract)
                v.drain().then_inc(vS, 1)
            inc("vS", 1)
        XQ_DONE = C["vS"]

        if dbg:
            @block.gpsimd
            def _(g):
                g.wait_ge(qC, P2_DONE)
                g.dma_start(out=dbg_out["d_x1"], in_=x_sp.ap()).then_inc(qG, 16)
            inc("qG", 16)

            @block.gpsimd
            def _(g):
                g.wait_ge(vS, XQ_DONE)
                g.dma_start(out=dbg_out["d_xq"].rearrange("(j p) t -> p j t", p=128),
                            in_=xq_bf[:].rearrange("p (j t) -> p j t", j=NDC)).then_inc(qG, 16)
            inc("qG", 16)
        es_l.close()

        if STOP_AFTER == "ln2":
            _bail()
            return nc, dbg_out

        es_f1 = contextlib.ExitStack()
        substacks.append(es_f1)
        gu0 = es_f1.enter_context(nc.sbuf_tensor("gu0", [128, 64 * 512], FP16))
        guq = gu0[:].bitcast(BF16)

        for th in range(2):
            es_f2 = contextlib.ExitStack()
            substacks.append(es_f2)
            wgb = [[es_f2.enter_context(nc.sbuf_tensor("wgb%d%d_%d" % (i, mtx, th),
                                                       [128, 2048], BF16))
                    for mtx in range(2)] for i in range(2)]
            g2buf = es_f2.enter_context(nc.sbuf_tensor("g2buf_%d" % th, [128, 512], F32))

            @block.vector
            def _(v, th=th):
                if th == 0:
                    v.wait_ge(vS, XQ_DONE)
                v.tensor_scalar(out=stage1[:, 0:512],
                                in0=sb2_ball[:, th * 512:(th + 1) * 512],
                                scalar1=SCAL[:, 0:1], scalar2=0.0,
                                op0=ALU.mult, op1=ALU.add)
                v.tensor_scalar(out=stage0[:, 0:512],
                                in0=sb2_ball[:, th * 512:(th + 1) * 512],
                                scalar1=SCAL[:, 1:2], scalar2=0.0,
                                op0=ALU.mult, op1=ALU.add)
                v.drain().then_inc(vS, 1)
            inc("vS", 1)
            S2GB = C["vS"]

            @block.gpsimd
            def _(g, th=th):
                if th == 0:
                    g.wait_ge(cS, AG2)
                g.sem_inc(qG, 1)
            inc("qG", 1)
            AG2W = C["qG"]

            wg_ld = [None] * 64
            chain_m = [None] * 64
            for ic in range(64):
                r = ic // 8
                cb = (ic % 8) * 128

                @block.sync
                def _(y, ic=ic, r=r, cb=cb, th=th,
                      F=(C["tS"], C["vS"], C["sS"], C["qC"])):
                    if ic == 0:
                        y.wait_ge(tS, F[0]); y.wait_ge(vS, F[1])
                        y.wait_ge(sS, F[2]); y.wait_ge(qC, F[3])
                        y.wait_ge(cS, AG2)
                    if ic >= 2 and chain_m[ic - 2] is not None:
                        y.wait_ge(tS, chain_m[ic - 2][0])
                    for mtx in range(2):
                        base = r * WQIN_SZ + mtx * WG_SZ
                        src = (cc_wq_out.ap()[base:base + WG_SZ]
                               .rearrange("(j p i) -> p j i", j=NDC, p=128)
                               [:, :, cb:cb + 128])
                        y.dma_start(out=wgb[ic % 2][mtx][:]
                                    .rearrange("p (j c) -> p j c", j=NDC),
                                    in_=src).then_inc(qS, 16)
                        inc("qS", 16)
                wg_ld[ic] = C["qS"]

                bset = (ic % 2) * 2 + 4
                @block.tensor
                def _(t, ic=ic, th=th, bset=bset):
                    t.wait_ge(qS, wg_ld[ic])
                    if ic == 0:
                        t.wait_ge(vS, S2GB)
                        t.wait_ge(qG, AG2W)
                    bwait(t, bset, bset + 1)
                    for mtx in range(2):
                        for j in range(NDC):
                            t.matmul(pb[bset + mtx][:, :],
                                     wgb[ic % 2][mtx][:, j * 128:(j + 1) * 128],
                                     xq_bf[:, j * 1024 + th * 512: j * 1024 + (th + 1) * 512],
                                     start=(j == 0), stop=(j == NDC - 1))
                    t.drain().then_inc(tS, 1)
                inc("tS", 1)
                GU_MM = C["tS"]

                @block.vector
                def _(v, ic=ic, bset=bset, GU_MM=GU_MM):
                    v.wait_ge(tS, GU_MM)
                    v.tensor_tensor(out=stage2[:, 0:512], in0=pb[bset][:, :],
                                    in1=stage1[:, 0:512], op=ALU.mult).then_inc(vS, 1)
                inc("vS", 1)
                G1 = C["vS"]

                @block.scalar
                def _(s, G1=G1):
                    s.wait_ge(vS, G1)
                    s.activation(out=g2buf[:], in_=stage2[:, 0:512], func=AF.Silu
                                 ).then_inc(sS, 1)
                inc("sS", 1)
                SIL = C["sS"]

                @block.vector
                def _(v, ic=ic, bset=bset, SIL=SIL):
                    v.wait_ge(sS, SIL)
                    v.tensor_tensor(out=stage2[:, 512:1024], in0=pb[bset + 1][:, :],
                                    in1=stage0[:, 0:512], op=ALU.mult)
                    v.tensor_tensor(out=gu0[:, ic * 512:(ic + 1) * 512], in0=g2buf[:],
                                    in1=stage2[:, 512:1024], op=ALU.mult).then_inc(vS, 1)
                inc("vS", 1)
                GU0 = C["vS"]
                bankfree[bset] = ("vS", GU0)
                bankfree[bset + 1] = ("vS", GU0)

                @block.scalar
                def _(s, ic=ic, GU0=GU0):
                    s.wait_ge(vS, GU0)
                    s.activation(out=g2buf[:], in_=gu0[:, ic * 512:(ic + 1) * 512],
                                 func=AF.Abs).then_inc(sS, 1)
                inc("sS", 1)
                ABSM = C["sS"]

                @block.vector
                def _(v, ic=ic, ABSM=ABSM):
                    v.wait_ge(sS, ABSM)
                    if ic == 0:
                        v.tensor_copy(stage1[:, 512:1024], g2buf[:])
                    else:
                        v.tensor_tensor(out=stage1[:, 512:1024], in0=stage1[:, 512:1024],
                                        in1=g2buf[:], op=ALU.max)
                    v.drain().then_inc(vS, 1)
                inc("vS", 1)
                chain_m[ic] = (GU_MM, C["vS"])
            GUP_DONE = C["vS"]

            mx_ev = []
            for tb in range(4):
                bk_i = tb % 2
                @block.tensor
                def _(t, tb=tb, bk_i=bk_i):
                    if tb == 0:
                        t.wait_ge(vS, GUP_DONE)
                    bwait(t, bk_i)
                    t.transpose(pb[bk_i][:, 0:128],
                                stage1[:, 512 + tb * 128: 512 + (tb + 1) * 128], id_f[:]
                                ).then_inc(tS, 1)
                inc("tS", 1)
                mtb = C["tS"]

                @block.vector
                def _(v, tb=tb, bk_i=bk_i, mtb=mtb):
                    v.wait_ge(tS, mtb)
                    v.tensor_reduce(out=wsmall[:, 8 + tb:9 + tb] if False else gtmp[:, 16 + tb:17 + tb],
                                    in_=pb[bk_i][:, 0:128], axis=AX, op=ALU.max)
                    v.drain().then_inc(vS, 1)
                inc("vS", 1)
                mx_ev.append(C["vS"])
                bankfree[bk_i] = ("vS", C["vS"])
            MX4 = C["vS"]

            @block.vector
            def _(v, th=th):
                v.wait_ge(vS, MX4)
                v.tensor_scalar(out=gtmp[:, 24:28], in0=gtmp[:, 16:20],
                                scalar1=1.0 / 127.0, scalar2=1e-5,
                                op0=ALU.mult, op1=ALU.add)
                v.drain()
                v.reciprocal(gtmp[:, 16:20], gtmp[:, 24:28])
                v.drain().then_inc(vS, 1)
            inc("vS", 1)
            F4COL = C["vS"]

            @block.tensor
            def _(t):
                t.wait_ge(vS, F4COL)
                bwait(t, 2)
                t.matmul(pb[2][0:4, 0:128], gtmp[:, 16:20], id_f[:], start=True, stop=True)
                t.matmul(pb[2][0:4, 128:256], gtmp[:, 24:28], id_f[:], start=True, stop=True)
                t.drain().then_inc(tS, 1)
            inc("tS", 1)
            F4T_MM = C["tS"]

            @block.vector
            def _(v, F4T_MM=F4T_MM):
                v.wait_ge(tS, F4T_MM)
                v.tensor_copy(r8[0:4, 768:896], pb[2][0:4, 0:128])
                v.tensor_copy(r8[0:4, 512:640], pb[2][0:4, 128:256])
                v.drain().then_inc(vS, 1)
            inc("vS", 1)
            F4ROWS = C["vS"]
            bankfree[2] = ("vS", F4ROWS)

            if dbg and th == 0:
                @block.scalar
                def _(s):
                    s.wait_ge(vS, F4ROWS)
                    s.dma_start(out=dbg_out["d_f4"][0:4, :], in_=r8[0:4, 768:896]
                                ).then_inc(qC, 16)
                inc("qC", 16)

            @block.vector
            def _(v):
                v.wait_ge(vS, F4ROWS)
                for rr in range(4):
                    v.tensor_copy(fr4[:, rr * 128:(rr + 1) * 128], r8[0:4, 768:896])
                    v.tensor_copy(sr4[:, rr * 128:(rr + 1) * 128], r8[0:4, 512:640])
                v.drain()
                v.tensor_tensor(out=fr4[:], in0=fr4[:], in1=pos8_f[0:4, 0:512], op=ALU.mult)
                v.tensor_tensor(out=sr4[:], in0=sr4[:], in1=pos8_f[0:4, 0:512], op=ALU.mult)
                v.drain().then_inc(vS, 1)
            inc("vS", 1)
            FR4M = C["vS"]

            @block.tensor
            def _(t, FR4M=FR4M):
                t.wait_ge(vS, FR4M)
                bwait(t, 0, 1)
                t.matmul(pb[0][:, :], onef[0:4, 0:128], fr4[:], start=True, stop=True)
                t.matmul(pb[1][:, :], onef[0:4, 0:128], sr4[:], start=True, stop=True)
                t.drain().then_inc(tS, 1)
            inc("tS", 1)
            F4B_MM = C["tS"]

            @block.vector
            def _(v, F4B_MM=F4B_MM):
                v.wait_ge(tS, F4B_MM)
                v.tensor_copy(stage0[:, 0:512], pb[0][:, :])
                v.tensor_scalar(out=stage1[:, 0:512], in0=pb[1][:, :],
                                scalar1=SCAL[:, 2:3], scalar2=0.0,
                                op0=ALU.mult, op1=ALU.add)
                v.drain().then_inc(vS, 1)
            inc("vS", 1)
            F4B = C["vS"]
            bankfree[0] = ("vS", F4B)
            bankfree[1] = ("vS", F4B)

            rnd_m = []
            for ic in range(64):
                @block.vector
                def _(v, ic=ic):
                    if ic == 0:
                        v.wait_ge(vS, F4B)
                    cs = slice(ic * 512, (ic + 1) * 512)
                    v.tensor_tensor(out=stage2[:, 0:512], in0=gu0[:, cs],
                                    in1=stage0[:, 0:512], op=ALU.mult)
                    v.tensor_scalar(out=guq[:, cs], in0=stage2[:, 0:512], scalar1=MAGIC,
                                    scalar2=MAGIC, op0=ALU.add, op1=ALU.subtract)
                    v.drain().then_inc(vS, 1)
                inc("vS", 1)
                rnd_m.append(C["vS"])
            RND_DONE = C["vS"]

            es_f2.close()

            es_f3 = contextlib.ExitStack()
            substacks.append(es_f3)
            wdb = [es_f3.enter_context(nc.sbuf_tensor("wdb%d_%d" % (i, th), [128, 8192], BF16))
                   for i in range(2)]
            xj = [es_f3.enter_context(nc.sbuf_tensor("xj%d_%d" % (i, th), [128, 512], F32))
                  for i in range(2)]
            outst = [es_f3.enter_context(nc.sbuf_tensor("outst%d_%d" % (i, th), [128, 512], F32))
                     for i in range(2)]

            wd_ld = [None] * NDC
            dn_ev = [None] * NDC
            dn_mm = [None] * NDC
            dn_st = [None] * NDC
            for j in range(NDC):
                @block.sync
                def _(y, j=j, th=th, F=(C["tS"], C["vS"], C["sS"])):
                    if j == 0:
                        y.wait_ge(tS, F[0]); y.wait_ge(vS, F[1]); y.wait_ge(sS, F[2])
                    if j >= 2:
                        y.wait_ge(tS, dn_mm[j - 2])
                    for r in range(NB):
                        base = r * WQIN_SZ + 2 * WG_SZ + j * 131072
                        y.dma_start(
                            out=wdb[j % 2][:, r * 1024:(r + 1) * 1024]
                            .rearrange("p (il c) -> p il c", il=8),
                            in_=cc_wq_out.ap()[base:base + 131072]
                            .rearrange("(il p c) -> p il c", il=8, p=128)
                            ).then_inc(qS, 16)
                        inc("qS", 16)
                    wd_ld[j] = C["qS"]
                    if j >= 2:
                        y.wait_ge(vS, dn_ev[j - 2])
                    y.dma_start(out=xj[j % 2][:],
                                in_=x_sp.ap()[j * 128:(j + 1) * 128,
                                              th * 512:(th + 1) * 512]).then_inc(qS, 16)
                    inc("qS", 16)

                XJLD = C["qS"]
                bk_i = 2 + j % 2

                @block.tensor
                def _(t, j=j, th=th, bk_i=bk_i):
                    t.wait_ge(qS, wd_ld[j])
                    if j == 0:
                        t.wait_ge(vS, RND_DONE)
                    bwait(t, bk_i)
                    for ic in range(64):
                        t.matmul(pb[bk_i][:, :], wdb[j % 2][:, ic * 128:(ic + 1) * 128],
                                 guq[:, ic * 512:(ic + 1) * 512],
                                 start=(ic == 0), stop=(ic == 63))
                    t.drain().then_inc(tS, 1)
                inc("tS", 1)
                dn_mm[j] = C["tS"]

                @block.vector
                def _(v, j=j, bk_i=bk_i, XJLD=XJLD):
                    v.wait_ge(tS, dn_mm[j])
                    v.wait_ge(qS, XJLD)
                    if j >= 2:
                        v.wait_ge(qC, dn_st[j - 2])
                    v.tensor_tensor(out=outst[j % 2][:], in0=pb[bk_i][:, :],
                                    in1=stage1[:, 0:512], op=ALU.mult)
                    v.scalar_tensor_tensor(out=outst[j % 2][:], in0=outst[j % 2][:],
                                           scalar=gb_t[:, j:j + 1], in1=xj[j % 2][:],
                                           op0=ALU.mult, op1=ALU.add)
                    v.drain().then_inc(vS, 1)
                inc("vS", 1)
                dn_ev[j] = C["vS"]
                bankfree[bk_i] = ("vS", dn_ev[j])

                @block.scalar
                def _(s, j=j, th=th):
                    s.wait_ge(vS, dn_ev[j])
                    s.dma_start(out=out[j * 128:(j + 1) * 128, th * 512:(th + 1) * 512],
                                in_=outst[j % 2][:]).then_inc(qC, 16)
                inc("qC", 16)
                dn_st[j] = C["qC"]
            es_f3.close()
        es_f1.close()
        es_q.close()

        @block.sync
        def _(y):
            y.wait_ge(qC, C["qC"])
            y.wait_ge(qS, C["qS"])
            y.wait_ge(qG, C["qG"])

    return nc, dbg_out


_NC_CACHE = {}


def _get_nc(dbg=False):
    if dbg not in _NC_CACHE:
        _NC_CACHE[dbg] = build(dbg)
    return _NC_CACHE[dbg]


def _host_inputs(L0_attn_x, L0_attn_Wq, L0_attn_bq, L0_attn_Wk, L0_attn_bk,
                 L0_attn_Wv, L0_attn_bv, L0_attn_Wbeta, L0_attn_bbeta,
                 L0_attn_Walpha, L0_attn_balpha, L0_ffn_w_gate, L0_ffn_w_up,
                 L0_ffn_w_down, L0_c, L0_wmod):
    f32 = lambda a: np.ascontiguousarray(np.asarray(a, dtype=np.float32))
    x = f32(L0_attn_x)
    Wq = f32(L0_attn_Wq); Wk = f32(L0_attn_Wk); Wv = f32(L0_attn_Wv)
    bq = f32(L0_attn_bq).reshape(1, D)
    bk = f32(L0_attn_bk).reshape(1, D)
    bv = f32(L0_attn_bv).reshape(1, D)
    Wab = f32(np.concatenate([np.asarray(L0_attn_Walpha), np.asarray(L0_attn_Wbeta)], axis=1))
    bab = f32(np.concatenate([np.asarray(L0_attn_balpha).ravel(),
                              np.asarray(L0_attn_bbeta).ravel()])).reshape(1, 32)
    wg = f32(L0_ffn_w_gate); wu = f32(L0_ffn_w_up); wd = f32(L0_ffn_w_down)
    c = f32(L0_c); wmod = f32(L0_wmod)

    ident = np.eye(128, dtype=np.float32)
    ones_full = np.ones((128, 128), dtype=np.float32)
    triu_np = np.triu(np.ones((128, 128), dtype=np.float32))
    ee = np.arange(128)
    m0_1 = np.where(ee[None, :] >= ee[:, None], 0.0, -1e30).astype(np.float32)
    m0c = np.ascontiguousarray(np.tile(m0_1, (1, 4)))
    e127_np = np.zeros((128, 1), dtype=np.float32); e127_np[127, 0] = 1.0
    kk = np.arange(16)
    posisel_np = (kk[:, None] == (np.arange(2048) // 128)[None, :]).astype(np.float32)
    k8 = np.arange(8)
    pos8_np = (k8[:, None] == (np.arange(1024) // 128)[None, :]).astype(np.float32)
    bm16_np = (k8[:, None] == (np.arange(128) // 16)[None, :]).astype(np.float32)
    cT32 = np.ascontiguousarray(
        c.T.reshape(16, 128, 2).transpose(1, 0, 2).reshape(128, 32))

    in_maps = []
    for m in range(NB):
        b = m // 4
        t0 = (m % 4) * TC
        grp0 = (m // 4) * 4
        bsel_np = np.zeros((128, 2), dtype=np.float32)
        bsel_np[:, 0] = 1.0 if b == 0 else 0.0
        bsel_np[:, 1] = 1.0 if b == 1 else 0.0
        selT_np = np.zeros((8, 8), dtype=np.float32)
        amask_np = np.zeros((8, 1), dtype=np.float32)
        for r in range(8):
            if grp0 <= r < m:
                amask_np[r, 0] = 1.0
                for rp in range(8):
                    if r < rp < m and grp0 <= rp:
                        selT_np[rp, r] = 1.0
        d = dict(
            xT=np.ascontiguousarray(x[b, t0:t0 + TC, :].T),
            Wq=Wq, Wk=Wk, Wv=Wv, bq=bq, bk=bk, bv=bv, Wab=Wab, bab=bab,
            wmod_sl=np.ascontiguousarray(wmod[:, m * 1536:(m + 1) * 1536]),
            cT32=cT32,
            wgT_sl=np.ascontiguousarray(wg[m * ISL:(m + 1) * ISL, :].T),
            wuT_sl=np.ascontiguousarray(wu[m * ISL:(m + 1) * ISL, :].T),
            wdT_sl=np.ascontiguousarray(wd[:, m * ISL:(m + 1) * ISL].T),
            ident=ident, ones_full=ones_full, triu=triu_np, m0c=m0c,
            e127=e127_np, bsel=bsel_np, selT=selT_np, amask=amask_np,
            posisel=posisel_np, pos8=pos8_np, bm16=bm16_np,
        )
        in_maps.append(d)
    return in_maps


def kernel(**inputs):
    in_maps = _host_inputs(**inputs)
    nc, _ = _get_nc(DEBUG)
    res = run_bass_kernel_spmd(nc, in_maps, core_ids=list(range(NB)))
    full = np.empty((2, 4096, D), dtype=np.float32)
    for m in range(NB):
        b = m // 4
        t0 = (m % 4) * TC
        full[b, t0:t0 + TC, :] = res.results[m]["out"].T
    if DEBUG:
        kernel.last_results = res.results
    return full


# version 04
# speedup vs baseline: 1.0628x; 1.0555x over previous
import contextlib
import numpy as np
import ml_dtypes
import concourse.bass as bass
import concourse.mybir as mybir
from concourse.bass_utils import run_bass_kernel_spmd

F32 = mybir.dt.float32
BF16 = mybir.dt.bfloat16
FP16 = mybir.dt.float16
AF = mybir.ActivationFunctionType
ALU = mybir.AluOpType
AX = mybir.AxisListType.X
MAGIC = float(np.float32(1.5 * 2 ** 23))

D = 2048
TC = 1024
NH = 16
NCH = 8
NDC = 16
ISL = 1024
NB = 8

WG_SZ = D * ISL
WQIN_SZ = 3 * WG_SZ
ST_SZ = 128 * D + 32
MOD_SZ = 2 * 12 * 128 + 16

DEBUG = False
STOP_AFTER = None
SAFE = False


def build(dbg=False):
    nc = bass.Bass("TRN2", target_bir_lowering=False)

    def IN(n, s):
        return nc.dram_tensor(n, s, F32, kind="ExternalInput").ap()

    xT = IN("xT", [D, TC])
    Wq = IN("Wq", [D, D]); Wk = IN("Wk", [D, D]); Wv = IN("Wv", [D, D])
    bq = IN("bq", [1, D]); bk = IN("bk", [1, D]); bv = IN("bv", [1, D])
    Wab = IN("Wab", [D, 32]); bab = IN("bab", [1, 32])
    wmod_sl = IN("wmod_sl", [D, 1536])
    cT32 = IN("cT32", [128, 32])
    wgT_sl = IN("wgT_sl", [D, ISL])
    wuT_sl = IN("wuT_sl", [D, ISL])
    wdT_sl = IN("wdT_sl", [ISL, D])
    ident = IN("ident", [128, 128])
    ones_full = IN("ones_full", [128, 128])
    triu = IN("triu", [128, 128])
    m0c = IN("m0c", [128, 512])
    e127 = IN("e127", [128, 1])
    bsel = IN("bsel", [128, 2])
    selT = IN("selT", [8, 8])
    amask = IN("amask", [8, 1])
    posisel = IN("posisel", [16, 2048])
    pos8 = IN("pos8", [8, 1024])
    bm16 = IN("bm16", [8, 128])

    out = nc.dram_tensor("out", [D, TC], F32, kind="ExternalOutput").ap()
    dbg_out = {}
    if dbg:
        for n, s in [("d_mbg", [128, 192]), ("d_ga", [128, 32]), ("d_ysb", [128, 512]),
                     ("d_dm", [128, 512]), ("d_eD", [128, 512]),
                     ("d_mymod", [128, 96]), ("d_la", [128, 128]),
                     ("d_lnb", [128, 128]), ("d_cla", [128, 128]),
                     ("d_wfac", [128, 128]), ("d_mrow", [1, 1024]),
                     ("d_istd", [1, 1024]), ("d_scal", [128, 16]),
                     ("d_coefb", [128, 128]), ("d_mx", [128, 8]),
                     ("d_x1", [D, TC]), ("d_xq", [D, TC]),
                     ("d_h", [D, TC]), ("d_state", [128, D]),
                     ("d_td", [128, 128]), ("d_cd", [128, 128]),
                     ("d_f4", [8, 128]), ("d_s2", [8, 128])]:
            dbg_out[n] = nc.dram_tensor(n, s, F32, kind="ExternalOutput").ap()
        for n, s in [("d_q", [NH * 128, TC]), ("d_k", [NCH * 128, D]),
                     ("d_v", [NCH * 128, D]), ("d_wbig", [128, 2048]),
                     ("d_hbf", [128, 2048]), ("d_at", [128, 512])]:
            dbg_out[n] = nc.dram_tensor(n, s, BF16, kind="ExternalOutput").ap()

    cc_mod_in = nc.dram_tensor("cc_mod_in", [MOD_SZ], F32, kind="Internal")
    cc_mod_out = nc.dram_tensor("cc_mod_out", [NB * MOD_SZ], F32, kind="Internal")
    cc_wq_in = nc.dram_tensor("cc_wq_in", [WQIN_SZ], BF16, kind="Internal")
    cc_wq_out = nc.dram_tensor("cc_wq_out", [NB * WQIN_SZ], BF16, kind="Internal")
    cc_st_in = nc.dram_tensor("cc_st_in", [ST_SZ], F32, kind="Internal")
    cc_st_out = nc.dram_tensor("cc_st_out", [NB * ST_SZ], F32, kind="Internal")
    q_sp = nc.dram_tensor("q_sp", [NH, 128, TC], BF16, kind="Internal")
    k_sp = nc.dram_tensor("k_sp", [NCH, 128, D], BF16, kind="Internal")
    v_sp = nc.dram_tensor("v_sp", [NCH, 128, D], BF16, kind="Internal")
    x_sp = nc.dram_tensor("x_sp", [D, TC], F32, kind="Internal")

    es = contextlib.ExitStack()
    SB = lambda n, s, d: es.enter_context(nc.sbuf_tensor(n, s, d))
    PS = lambda n, s, d: es.enter_context(nc.psum_tensor(n, s, d))
    SEM = lambda n: es.enter_context(nc.semaphore(n))

    id_f = SB("id_f", [128, 128], F32)
    id_bf = SB("id_bf", [128, 128], BF16)
    onef = SB("onef", [128, 128], F32)
    onebf_c = SB("onebf_c", [128, 1], BF16)
    onebf_r = SB("onebf_r", [1, 512], BF16)
    cEPS5 = SB("cEPS5", [128, 1], F32)
    cEPS6 = SB("cEPS6", [128, 1], F32)
    cONE = SB("cONE", [128, 1], F32)
    cEPS24 = SB("cEPS24", [128, 1], F32)
    SCAL = SB("SCAL", [128, 16], F32)
    mymod = SB("mymod", [128, 96], F32)
    camod = SB("camod", [128, 16], F32)
    cbmod = SB("cbmod", [128, 16], F32)
    ga_t = SB("ga_t", [128, 16], F32)
    gb_t = SB("gb_t", [128, 16], F32)
    gtmp = SB("gtmp", [128, 32], F32)
    rowA = SB("rowA", [1, 1024], F32)
    rowB = SB("rowB", [1, 1024], F32)
    stage0 = SB("stage0", [128, 1024], F32)
    stage1 = SB("stage1", [128, 1024], F32)
    stage2 = SB("stage2", [128, 1024], F32)
    bst = [SB("bst0", [128, 512], BF16), SB("bst1", [128, 512], BF16)]
    wsum = SB("wsum", [128, 52], F32)
    wsmall = SB("wsmall", [128, 8], F32)
    scalrow = SB("scalrow", [1, 16], F32)
    pos8_f = SB("pos8_f", [8, 1024], F32)
    substacks = []
    es_a = contextlib.ExitStack()
    substacks.append(es_a)
    SBA = lambda n, s, d: es_a.enter_context(nc.sbuf_tensor(n, s, d))
    triu_f = SBA("triu_f", [128, 128], F32)
    m0_f = SBA("m0_f", [128, 512], F32)
    e127_f = SBA("e127_f", [128, 1], F32)
    bsel_f = SBA("bsel_f", [128, 2], F32)
    selT_f = SBA("selT_f", [8, 8], F32)
    amask_f = SBA("amask_f", [8, 1], F32)
    zab = SBA("zab", [128, 256], F32)
    la_all = SBA("la_all", [128, 128], F32)
    lnb_all = SBA("lnb_all", [128, 128], F32)
    albe = SBA("albe", [128, 256], F32)
    cla_all = SBA("cla_all", [128, 128], F32)
    cd_all = SBA("cd_all", [128, 128], F32)
    nclb = SBA("nclb", [128, 128], F32)
    claT_s = SBA("claT_s", [16, 1024], F32)
    nclbT_s = SBA("nclbT_s", [16, 1024], F32)
    wfac_all = SBA("wfac_all", [128, 128], F32)
    td_b = SBA("td_b", [128, 128], F32)
    cend_row = SBA("cend_row", [1, 128], F32)
    g_pub = SBA("g_pub", [1, 16], F32)
    gsb = SBA("gsb", [8, 16], F32)
    coef_sb = SBA("coef_sb", [8, 16], F32)
    coefB = SBA("coefB", [128, 128], F32)
    mb_g = SBA("mb_g", [128, 192], F32)
    psel_f = SBA("psel_f", [16, 2048], F32)
    bm16_f = SBA("bm16_f", [8, 128], F32)
    cmsk = SBA("cmsk", [8, 128], F32)
    ctrep = SBA("ctrep", [16, 512], F32)
    dr1 = [SBA("dr10", [16, 512], F32), SBA("dr11", [16, 512], F32)]

    pb = [PS("pb%d" % i, [128, 512], F32) for i in range(8)]

    qS = SEM("qS"); qG = SEM("qG"); qC = SEM("qC")
    cS = SEM("cS"); tS = SEM("tS"); vS = SEM("vS"); sS = SEM("sS"); gS = SEM("gS")
    C = {"qS": 0, "qG": 0, "qC": 0, "cS": 0, "tS": 0, "vS": 0, "sS": 0, "gS": 0}
    SMS = {"qS": qS, "qG": qG, "qC": qC, "cS": cS, "tS": tS, "vS": vS, "sS": sS,
           "gS": gS}

    def inc(name, n=1):
        C[name] += n
        return C[name]

    bankfree = {}

    def _bail():
        for _st in reversed(substacks):
            _st.close()
        @block.sync
        def _(y):
            y.wait_ge(qC, C["qC"])
            y.wait_ge(qS, C["qS"])
            y.wait_ge(qG, C["qG"])
            y.wait_ge(vS, C["vS"])
            y.wait_ge(sS, C["sS"])
            y.wait_ge(tS, C["tS"])

    def bwait(eng, *banks):
        for i in banks:
            if i in bankfree:
                sn, cnt = bankfree[i]
                eng.wait_ge(SMS[sn], cnt)

    with es, nc.Block() as _block:
        class _SafeBlock:
            def __getattr__(self, eng):
                dec = getattr(_block, eng)
                def wrap(fn):
                    def body(e, fn=fn):
                        if SAFE:
                            for nm in ("qS", "qG", "qC", "cS", "tS", "vS", "sS", "gS"):
                                if C[nm]:
                                    e.wait_ge(SMS[nm], C[nm])
                        return fn(e)
                    return dec(body)
                return wrap
        block = _SafeBlock()
        @block.sync
        def _(y):
            y.dma_start(out=id_f[:], in_=ident).then_inc(qS, 16); inc("qS", 16)
            y.dma_start(out=onef[:], in_=ones_full).then_inc(qS, 16); inc("qS", 16)
            y.dma_start(out=triu_f[:], in_=triu).then_inc(qS, 16); inc("qS", 16)
            y.dma_start(out=m0_f[:], in_=m0c).then_inc(qS, 16); inc("qS", 16)
            y.dma_start(out=e127_f[:], in_=e127).then_inc(qS, 16); inc("qS", 16)
            y.dma_start(out=bsel_f[:], in_=bsel).then_inc(qS, 16); inc("qS", 16)
            y.dma_start(out=selT_f[:], in_=selT).then_inc(qS, 16); inc("qS", 16)
            y.dma_start(out=amask_f[:], in_=amask).then_inc(qS, 16); inc("qS", 16)
            y.dma_start(out=psel_f[:], in_=posisel).then_inc(qS, 16); inc("qS", 16)
            y.dma_start(out=bm16_f[:], in_=bm16).then_inc(qS, 16); inc("qS", 16)
            y.dma_start(out=pos8_f[:], in_=pos8).then_inc(qS, 16); inc("qS", 16)
            y.dma_start(out=stage0[:, 0:32], in_=cT32).then_inc(qS, 16); inc("qS", 16)
        CONSTS_S = C["qS"]

        @block.gpsimd
        def _(g):
            g.dma_start(out=id_bf[:], in_=ident).then_inc(qG, 16); inc("qG", 16)
            g.memset(onebf_c[:], 1.0)
            g.memset(onebf_r[:], 1.0)
            g.memset(cEPS5[:], 1e-5)
            g.memset(cEPS6[:], 1e-6)
            g.memset(cONE[:], 1.0)
            g.memset(cEPS24[:], 1e-24)
            g.drain().then_inc(gS, 1); inc("gS", 1)
        CONSTS_G = C["qG"]
        CONSTS_GS = C["gS"]

        scT = stage1[:, 0:64].bitcast(BF16)
        @block.scalar
        def _(s):
            s.wait_ge(qS, CONSTS_S)
            s.wait_ge(qG, CONSTS_G)
            s.wait_ge(gS, CONSTS_GS)
            s.activation(out=scT[:, 0:32], in_=stage0[:, 0:32], func=AF.Silu)
            s.drain().then_inc(sS, 1); inc("sS", 1)
        SILU_C = C["sS"]

        if STOP_AFTER == "consts":
            _bail()
            return nc, dbg_out

        es_p0b = contextlib.ExitStack()
        substacks.append(es_p0b)
        SBPB = lambda n, s, d: es_p0b.enter_context(nc.sbuf_tensor(n, s, d))
        modst = SBPB("modst", [128, 24], F32)
        wqstg = [SBPB("wqstg0", [128, 1024], BF16), SBPB("wqstg1", [128, 1024], BF16)]
        es_p0 = contextlib.ExitStack()
        substacks.append(es_p0)
        SBP = lambda n, s, d: es_p0.enter_context(nc.sbuf_tensor(n, s, d))
        wm = [SBP("wm0", [128, 1536], BF16), SBP("wm1", [128, 1536], BF16)]
        wm_ld = []
        wm_used = []
        mod_ev = []
        for j in range(NDC):
            @block.gpsimd
            def _(g, j=j):
                if j >= 2:
                    g.wait_ge(tS, wm_used[j - 2])
                g.dma_start(out=wm[j % 2][:],
                            in_=wmod_sl[j * 128:(j + 1) * 128, :]).then_inc(qG, 16)
            inc("qG", 16)
            wm_ld.append(C["qG"])

            bk_i = j % 2
            @block.tensor
            def _(t, j=j, bk_i=bk_i):
                if j == 0:
                    t.wait_ge(sS, SILU_C)
                bwait(t, bk_i)
                t.wait_ge(qG, wm_ld[j])
                for ql in range(12):
                    t.matmul(pb[bk_i][:, ql * 2:ql * 2 + 2],
                             wm[j % 2][:, ql * 128:(ql + 1) * 128],
                             scT[:, 2 * j:2 * j + 2], start=True, stop=True)
                t.drain().then_inc(tS, 1)
            inc("tS", 1)
            wm_used.append(C["tS"])

            @block.vector
            def _(v, j=j, bk_i=bk_i):
                v.wait_ge(tS, wm_used[j])
                if j == 0:
                    v.tensor_copy(modst[:], pb[bk_i][:, 0:24])
                else:
                    v.drain()
                    v.tensor_tensor(out=modst[:], in0=modst[:], in1=pb[bk_i][:, 0:24],
                                    op=ALU.add)
                v.drain().then_inc(vS, 1)
            inc("vS", 1)
            mod_ev.append(C["vS"])
            bankfree[bk_i] = ("vS", C["vS"])
        MOD_EVAC = C["vS"]

        if STOP_AFTER == "mod":
            _bail()
            return nc, dbg_out

        wsl = [SBP("wsl0", [128, 1024], F32), SBP("wsl1", [128, 1024], F32)]
        absdump = SBP("absdump", [128, 1024], F32)
        wtiles = []
        for j in range(NDC):
            wtiles.append(wgT_sl[j * 128:(j + 1) * 128, :])
        for j in range(NDC):
            wtiles.append(wuT_sl[j * 128:(j + 1) * 128, :])
        for it in range(8):
            wtiles.append(wdT_sl[it * 128:(it + 1) * 128, 0:1024])
            wtiles.append(wdT_sl[it * 128:(it + 1) * 128, 1024:2048])
        wt_ld = []
        wt_used = []
        for i in range(48):
            @block.sync
            def _(y, i=i):
                if i >= 2:
                    y.wait_ge(sS, wt_used[i - 2])
                y.dma_start(out=wsl[i % 2][:], in_=wtiles[i]).then_inc(qS, 16)
            inc("qS", 16)
            wt_ld.append(C["qS"])

            @block.scalar
            def _(s, i=i):
                s.wait_ge(qS, wt_ld[i])
                s.activation(out=absdump[:], in_=wsl[i % 2][:], func=AF.Abs,
                             accum_out=wsum[:, i:i + 1]).then_inc(sS, 1)
            inc("sS", 1)
            wt_used.append(C["sS"])
        ABS_DONE = C["sS"]

        @block.vector
        def _(v):
            v.wait_ge(sS, ABS_DONE)
            v.tensor_reduce(out=wsmall[:, 0:1], in_=wsum[:, 0:16], axis=AX, op=ALU.add)
            v.tensor_reduce(out=wsmall[:, 1:2], in_=wsum[:, 16:32], axis=AX, op=ALU.add)
            v.tensor_reduce(out=wsmall[:, 2:3], in_=wsum[:, 32:48], axis=AX, op=ALU.add)
            v.drain().then_inc(vS, 1)
        inc("vS", 1)
        WRED = C["vS"]

        @block.tensor
        def _(t):
            t.wait_ge(vS, WRED)
            bwait(t, 1)
            t.matmul(pb[1][0:1, 0:3], onef[:, 0:1], wsmall[:, 0:3],
                     start=True, stop=True).then_inc(tS, 1)
        inc("tS", 1)
        WRED_MM = C["tS"]

        @block.vector
        def _(v):
            v.wait_ge(tS, WRED_MM)
            v.tensor_copy(scalrow[:, 0:3], pb[1][0:1, 0:3]).then_inc(vS, 1)
        inc("vS", 1)
        WROW = C["vS"]
        bankfree[1] = ("vS", WROW)

        if STOP_AFTER == "absw":
            _bail()
            return nc, dbg_out

        @block.gpsimd
        def _(g):
            g.wait_ge(vS, max(MOD_EVAC, WROW))
            g.dma_start(out=cc_mod_in.ap()[0:3072].rearrange("(p c) -> p c", p=128),
                        in_=modst[:]).then_inc(qG, 16)
            inc("qG", 16)
            g.dma_start(out=cc_mod_in.ap()[3072:3088].rearrange("(o c) -> o c", o=1),
                        in_=scalrow[:, 0:16]).then_inc(qG, 16)
            inc("qG", 16)
            g.wait_ge(qG, C["qG"])
            g.collective_compute(
                "AllGather", ALU.bypass, replica_groups=[list(range(NB))],
                ins=[cc_mod_in.ap().opt()], outs=[cc_mod_out.ap().opt()],
            ).then_inc(cS, 1)
        inc("cS", 1)
        AG1 = C["cS"]

        @block.sync
        def _(y):
            y.wait_ge(cS, AG1)
            y.dma_start(out=gsb[:],
                        in_=cc_mod_out.ap().rearrange("(r x) -> r x", r=NB)[:, 3072:3088]
                        ).then_inc(qS, 16)
            inc("qS", 16)
            y.dma_start(out=mb_g[:].rearrange("p (r c) -> p r c", r=NB),
                        in_=cc_mod_out.ap().rearrange("(r x) -> r x", r=NB)[:, 0:3072]
                        .rearrange("r (p c) -> p r c", p=128)).then_inc(qS, 16)
            inc("qS", 16)
        AG1_LD = C["qS"]

        @block.tensor
        def _(t):
            t.wait_ge(qS, AG1_LD)
            bwait(t, 2)
            t.matmul(pb[2][0:1, 0:16], onef[0:8, 0:1], gsb[:, 0:16],
                     start=True, stop=True).then_inc(tS, 1)
        inc("tS", 1)
        GS_MM = C["tS"]

        @block.vector
        def _(v):
            v.wait_ge(tS, GS_MM)
            v.tensor_scalar(out=scalrow[:, 0:3], in0=pb[2][0:1, 0:3],
                            scalar1=1.0 / (8192.0 * 2048.0), scalar2=1e-5,
                            op0=ALU.mult, op1=ALU.add)
            v.drain()
            v.reciprocal(scalrow[:, 3:6], scalrow[:, 0:3])
            v.drain().then_inc(vS, 1)
        inc("vS", 1)
        SCROW = C["vS"]
        bankfree[2] = ("vS", SCROW)

        @block.tensor
        def _(t):
            t.wait_ge(vS, SCROW)
            bwait(t, 3)
            t.matmul(pb[3][:, 0:16], onef[0:1, 0:128], scalrow[:, 0:16],
                     start=True, stop=True).then_inc(tS, 1)
        inc("tS", 1)
        SC_MM = C["tS"]

        @block.vector
        def _(v):
            v.wait_ge(tS, SC_MM)
            v.tensor_copy(SCAL[:], pb[3][:, 0:16]).then_inc(vS, 1)
            inc("vS", 1)
            v.wait_ge(qS, AG1_LD)
            mbv = mb_g[:].rearrange("p (r q b) -> p r q b", r=NB, b=2)
            v.tensor_scalar(out=mymod[:].rearrange("p (r q) -> p r q", r=NB),
                            in0=mbv[:, :, :, 0], scalar1=bsel_f[:, 0:1],
                            scalar2=0.0, op0=ALU.mult, op1=ALU.add)
            v.drain()
            v.scalar_tensor_tensor(out=mymod[:].rearrange("p (r q) -> p r q", r=NB),
                                   in0=mbv[:, :, :, 1], scalar=bsel_f[:, 1:2],
                                   in1=mymod[:].rearrange("p (r q) -> p r q", r=NB),
                                   op0=ALU.mult, op1=ALU.add)
            v.drain()
            v.tensor_scalar(out=camod[:], in0=mymod[:, 16:32], scalar1=1.0, scalar2=0.0,
                            op0=ALU.add, op1=ALU.add)
            v.tensor_scalar(out=cbmod[:], in0=mymod[:, 64:80], scalar1=1.0, scalar2=0.0,
                            op0=ALU.add, op1=ALU.add)
            v.drain().then_inc(vS, 1)
        inc("vS", 1)
        MYMOD = C["vS"]
        bankfree[3] = ("vS", MYMOD)

        @block.vector
        def _(v):
            v.wait_ge(vS, MYMOD)
            for (gt, col) in ((ga_t, 32), (gb_t, 80)):
                off = 0 if col == 32 else 16
                v.tensor_scalar_mul(gtmp[:, off:off + 16], mymod[:, col:col + 16], 1e30)
                v.drain()
                v.tensor_scalar(out=gtmp[:, off:off + 16], in0=gtmp[:, off:off + 16],
                                scalar1=1.0, scalar2=-1.0, op0=ALU.min, op1=ALU.max)
                v.drain()
                v.tensor_tensor(out=gt[:], in0=mymod[:, col:col + 16],
                                in1=gtmp[:, off:off + 16], op=ALU.mult)
                v.drain()
                v.tensor_scalar(out=gt[:], in0=gt[:], scalar1=0.25, scalar2=0.0,
                                op0=ALU.max, op1=ALU.add)
                v.drain()
                v.tensor_tensor(out=gt[:], in0=gt[:], in1=gtmp[:, off:off + 16],
                                op=ALU.mult)
            v.drain().then_inc(vS, 1)
        inc("vS", 1)
        GAGB = C["vS"]

        if STOP_AFTER == "ag1c":
            _bail()
            return nc, dbg_out

        wq_dst = []
        for j in range(NDC):
            wq_dst.append(("gu", j * 131072))
        for j in range(NDC):
            wq_dst.append(("gu", WG_SZ + j * 131072))
        for it in range(8):
            wq_dst.append(("wd", (it, 0)))
            wq_dst.append(("wd", (it, 1)))
        q_ld = []
        q_sc = []
        q_vc = []
        q_st = []
        for i in range(48):
            invcol = 3 + (0 if i < 16 else (1 if i < 32 else 2))

            @block.sync
            def _(y, i=i):
                if i >= 2:
                    y.wait_ge(sS, q_sc[i - 2])
                if i == 0:
                    y.wait_ge(vS, SCROW)
                y.dma_start(out=wsl[i % 2][:], in_=wtiles[i]).then_inc(qS, 16)
            inc("qS", 16)
            q_ld.append(C["qS"])

            @block.scalar
            def _(s, i=i, invcol=invcol):
                s.wait_ge(qS, q_ld[i])
                if i == 0:
                    s.wait_ge(vS, SCROW + 1)
                if i >= 2:
                    s.wait_ge(vS, q_vc[i - 2])
                s.activation(out=stage2[:], in_=wsl[i % 2][:], func=AF.Copy,
                             scale=SCAL[:, invcol:invcol + 1], bias=MAGIC
                             ).then_inc(sS, 1)
            inc("sS", 1)
            q_sc.append(C["sS"])

            @block.vector
            def _(v, i=i):
                v.wait_ge(sS, q_sc[i])
                if i >= 2:
                    v.wait_ge(qG, q_st[i - 2])
                v.tensor_scalar(out=stage2[:], in0=stage2[:], scalar1=MAGIC + 1.0,
                                scalar2=MAGIC - 1.0, op0=ALU.min, op1=ALU.max)
                v.tensor_scalar(out=wqstg[i % 2][:], in0=stage2[:], scalar1=MAGIC,
                                scalar2=0.0, op0=ALU.subtract, op1=ALU.add
                                ).then_inc(vS, 1)
            inc("vS", 1)
            q_vc.append(C["vS"])

            kind, off = wq_dst[i]

            @block.gpsimd
            def _(g, i=i, kind=kind, off=off):
                g.wait_ge(vS, q_vc[i])
                if kind == "gu":
                    g.dma_start(out=cc_wq_in.ap()[off:off + 131072]
                                .rearrange("(p c) -> p c", p=128),
                                in_=wqstg[i % 2][:]).then_inc(qG, 16)
                else:
                    it, half = off
                    w0 = 2 * WG_SZ + half * 8 * 131072
                    dst = (cc_wq_in.ap()[w0:w0 + 8 * 131072]
                           .rearrange("(hc i h) -> hc i h", hc=8, i=1024)
                           [:, it * 128:(it + 1) * 128, :]
                           .rearrange("hc p h -> p hc h"))
                    g.dma_start(out=dst,
                                in_=wqstg[i % 2][:].rearrange("p (hc h) -> p hc h", hc=8)
                                ).then_inc(qG, 16)
            inc("qG", 16)
            q_st.append(C["qG"])
        QPUB = C["qG"]

        @block.gpsimd
        def _(g):
            g.wait_ge(qG, QPUB)
            g.collective_compute(
                "AllGather", ALU.bypass, replica_groups=[list(range(NB))],
                ins=[cc_wq_in.ap().opt()], outs=[cc_wq_out.ap().opt()],
            ).then_inc(cS, 1)
        inc("cS", 1)
        AG2 = C["cS"]

        if STOP_AFTER == "quant":
            _bail()
            return nc, dbg_out

        es_p0.close()
        es_h = contextlib.ExitStack()
        substacks.append(es_h)
        h_bf = es_h.enter_context(nc.sbuf_tensor("h_bf", [128, NDC * 1024], BF16))
        es_x = contextlib.ExitStack()
        substacks.append(es_x)
        x_f = es_x.enter_context(nc.sbuf_tensor("x_f", [128, NDC * 1024], F32))
        x_bf = es_x.enter_context(nc.sbuf_tensor("x_bf", [128, NDC * 1024], BF16))
        x2b = [es_x.enter_context(nc.sbuf_tensor("x2b%d" % i, [128, 1024], BF16))
               for i in range(2)]
        rowC = es_x.enter_context(nc.sbuf_tensor("rowC", [1, 1024], F32))

        @block.sync
        def _(y, F=(C["tS"], C["vS"], C["sS"])):
            y.wait_ge(tS, F[0])
            y.wait_ge(vS, F[1])
            y.wait_ge(sS, F[2])
            y.dma_start(out=x_f[:].rearrange("p (j t) -> p j t", j=NDC),
                        in_=xT.rearrange("(j p) t -> p j t", p=128)).then_inc(qS, 16)
        inc("qS", 16)
        XLD = C["qS"]

        @block.gpsimd
        def _(g, F=(C["tS"], C["vS"], C["sS"])):
            g.wait_ge(tS, F[0])
            g.wait_ge(vS, F[1])
            g.wait_ge(sS, F[2])
            g.dma_start(out=x_bf[:].rearrange("p (j t) -> p j t", j=NDC),
                        in_=xT.rearrange("(j p) t -> p j t", p=128)).then_inc(qG, 16)
        inc("qG", 16)
        XBLD = C["qG"]

        sq_done = []
        st_done = []
        for j in range(NDC):
            @block.scalar
            def _(s, j=j):
                if j == 0:
                    s.wait_ge(qS, XLD)
                if j >= 2:
                    s.wait_ge(tS, st_done[j - 2])
                s.activation(out=x2b[j % 2][:], in_=x_f[:, j * 1024:(j + 1) * 1024],
                             func=AF.Square).then_inc(sS, 1)
            inc("sS", 1)
            sq_done.append(C["sS"])

            @block.tensor
            def _(t, j=j):
                if j == 0:
                    t.wait_ge(qG, XBLD)
                    bwait(t, 4, 5, 6, 7)
                t.wait_ge(sS, sq_done[j])
                for tt in range(2):
                    t.matmul(pb[4 + tt][0:1, :], onebf_c[:, 0:1],
                             x_bf[:, j * 1024 + tt * 512: j * 1024 + (tt + 1) * 512],
                             start=(j == 0), stop=(j == NDC - 1))
                    t.matmul(pb[6 + tt][0:1, :], onebf_c[:, 0:1],
                             x2b[j % 2][:, tt * 512:(tt + 1) * 512],
                             start=(j == 0), stop=(j == NDC - 1))
                t.drain().then_inc(tS, 1)
            inc("tS", 1)
            st_done.append(C["tS"])
        STATS_MM = C["tS"]

        @block.vector
        def _(v):
            v.wait_ge(tS, STATS_MM)
            v.tensor_scalar(out=rowA[0:1, 0:512], in0=pb[4][0:1, :], scalar1=1.0 / 2048.0,
                            scalar2=0.0, op0=ALU.mult, op1=ALU.add)
            v.tensor_scalar(out=rowA[0:1, 512:1024], in0=pb[5][0:1, :], scalar1=1.0 / 2048.0,
                            scalar2=0.0, op0=ALU.mult, op1=ALU.add)
            v.tensor_scalar(out=rowB[0:1, 0:512], in0=pb[6][0:1, :], scalar1=1.0 / 2048.0,
                            scalar2=0.0, op0=ALU.mult, op1=ALU.add)
            v.tensor_scalar(out=rowB[0:1, 512:1024], in0=pb[7][0:1, :], scalar1=1.0 / 2048.0,
                            scalar2=0.0, op0=ALU.mult, op1=ALU.add)
            v.drain()
            v.tensor_tensor(out=rowC[:], in0=rowA[:], in1=rowA[:], op=ALU.mult)
            v.drain()
            v.tensor_tensor(out=rowB[:], in0=rowB[:], in1=rowC[:], op=ALU.subtract)
            v.drain().then_inc(vS, 1)
        inc("vS", 1)
        VAR1 = C["vS"]
        for b_ in (4, 5, 6, 7):
            bankfree[b_] = ("vS", VAR1)

        @block.scalar
        def _(s):
            s.wait_ge(vS, VAR1)
            s.activation(out=rowC[:], in_=rowB[:], func=AF.Ln, bias=cEPS5[0:1, 0:1])
            s.activation(out=rowB[:], in_=rowC[:], func=AF.Exp, scale=-0.5)
            s.drain().then_inc(sS, 1)
        inc("sS", 1)
        ISTD1 = C["sS"]

        if dbg:
            @block.scalar
            def _(s):
                s.wait_ge(vS, MYMOD)
                s.wait_ge(sS, ISTD1)
                s.drain()
                s.dma_start(out=dbg_out["d_mbg"], in_=mb_g[:]).then_inc(qC, 16)
                s.dma_start(out=dbg_out["d_mrow"], in_=rowA[:]).then_inc(qC, 16)
                s.dma_start(out=dbg_out["d_istd"], in_=rowB[:]).then_inc(qC, 16)
                s.dma_start(out=dbg_out["d_mymod"], in_=mymod[:]).then_inc(qC, 16)
                s.dma_start(out=dbg_out["d_scal"], in_=SCAL[:]).then_inc(qC, 16)
            inc("qC", 80)

        @block.tensor
        def _(t):
            t.wait_ge(sS, ISTD1)
            bwait(t, 0, 1, 2, 3)
            for tt in range(2):
                t.matmul(pb[0 + tt][:, :], onef[0:1, 0:128],
                         rowA[0:1, tt * 512:(tt + 1) * 512], start=True, stop=True)
                t.matmul(pb[2 + tt][:, :], onef[0:1, 0:128],
                         rowB[0:1, tt * 512:(tt + 1) * 512], start=True, stop=True)
            t.drain().then_inc(tS, 1)
        inc("tS", 1)
        BC1 = C["tS"]

        @block.vector
        def _(v):
            v.wait_ge(tS, BC1)
            v.tensor_copy(stage0[:, 0:512], pb[0][:, :])
            v.tensor_copy(stage0[:, 512:1024], pb[1][:, :])
            v.tensor_copy(stage1[:, 0:512], pb[2][:, :])
            v.tensor_copy(stage1[:, 512:1024], pb[3][:, :])
            v.drain().then_inc(vS, 1)
        inc("vS", 1)
        MB1 = C["vS"]
        for b_ in (0, 1, 2, 3):
            bankfree[b_] = ("vS", MB1)

        for j in range(NDC):
            @block.vector
            def _(v, j=j):
                if j == 0:
                    v.wait_ge(vS, max(MB1, GAGB))
                cs = slice(j * 1024, (j + 1) * 1024)
                v.tensor_tensor(out=stage2[:], in0=x_f[:, cs], in1=stage0[:], op=ALU.subtract)
                v.tensor_tensor(out=stage2[:], in0=stage2[:], in1=stage1[:], op=ALU.mult)
                v.tensor_scalar(out=h_bf[:, cs], in0=stage2[:],
                                scalar1=camod[:, j:j + 1], scalar2=mymod[:, j:j + 1],
                                op0=ALU.mult, op1=ALU.add)
                v.drain().then_inc(vS, 1)
            inc("vS", 1)
        H_DONE = C["vS"]

        if dbg:
            @block.gpsimd
            def _(g):
                g.wait_ge(vS, H_DONE)
                g.dma_start(out=dbg_out["d_h"].rearrange("(j p) t -> p j t", p=128),
                            in_=h_bf[:].rearrange("p (j t) -> p j t", j=NDC)).then_inc(qG, 16)
            inc("qG", 16)

        @block.scalar
        def _(s):
            s.wait_ge(vS, H_DONE)
            s.dma_start(out=x_sp.ap().rearrange("(j p) t -> p j t", p=128),
                        in_=x_f[:].rearrange("p (j t) -> p j t", j=NDC)).then_inc(qC, 16)
        inc("qC", 16)
        XSPILL = C["qC"]
        es_x.close()

        if STOP_AFTER == "ln1":
            _bail()
            return nc, dbg_out

        es_w = contextlib.ExitStack()
        substacks.append(es_w)
        wbig = [es_w.enter_context(nc.sbuf_tensor("wbig%d" % i, [128, 8192], BF16))
                for i in range(2)]
        wab_t = es_w.enter_context(nc.sbuf_tensor("wab_t", [128, 512], BF16))
        bq_bf = es_w.enter_context(nc.sbuf_tensor("bq_bf", [1, D], BF16))
        bk_bf = es_w.enter_context(nc.sbuf_tensor("bk_bf", [1, D], BF16))
        bv_bf = es_w.enter_context(nc.sbuf_tensor("bv_bf", [1, D], BF16))
        bab_bf = es_w.enter_context(nc.sbuf_tensor("bab_bf", [1, 32], BF16))
        kn_s = es_w.enter_context(nc.sbuf_tensor("kn_s", [128, 8], F32))

        @block.gpsimd
        def _(g):
            g.wait_ge(vS, H_DONE)
            g.wait_ge(qC, XSPILL)
            g.wait_ge(tS, C["tS"])
            g.dma_start(out=bq_bf[:], in_=bq).then_inc(qG, 16)
            g.dma_start(out=bk_bf[:], in_=bk).then_inc(qG, 16)
            g.dma_start(out=bv_bf[:], in_=bv).then_inc(qG, 16)
            g.dma_start(out=bab_bf[:], in_=bab).then_inc(qG, 16)
            g.dma_start(out=wab_t[:].rearrange("p (j c) -> p j c", j=NDC),
                        in_=Wab.rearrange("(j p) c -> p j c", p=128)).then_inc(qG, 16)
        inc("qG", 80)
        BIAS_LD = C["qG"]

        wq_ld = []
        q_use = []
        q_stq = []
        for h in range(NH):
            @block.gpsimd
            def _(g, h=h):
                if h >= 2:
                    g.wait_ge(tS, q_use[h - 2])
                g.dma_start(out=wbig[h % 2][:, 0:2048].rearrange("p (j s) -> p j s", j=NDC),
                            in_=Wq.rearrange("(j p) s -> p j s", p=128)
                            [:, :, h * 128:(h + 1) * 128]).then_inc(qG, 16)
            inc("qG", 16)
            wq_ld.append(C["qG"])

            for tt in range(2):
                bk_i = (h * 2 + tt) % 2
                @block.tensor
                def _(t, h=h, tt=tt, bk_i=bk_i):
                    if tt == 0:
                        t.wait_ge(qG, wq_ld[h])
                        if h == 0:
                            t.wait_ge(vS, H_DONE)
                            t.wait_ge(qG, BIAS_LD)
                    bwait(t, bk_i)
                    for j in range(NDC):
                        t.matmul(pb[bk_i][:, :], wbig[h % 2][:, j * 128:(j + 1) * 128],
                                 h_bf[:, j * 1024 + tt * 512: j * 1024 + (tt + 1) * 512],
                                 start=(j == 0), stop=False)
                    t.matmul(pb[bk_i][:, :], bq_bf[0:1, h * 128:(h + 1) * 128],
                             onebf_r[0:1, 0:512], start=False, stop=True)
                    t.drain().then_inc(tS, 1)
                inc("tS", 1)
                mmq = C["tS"]
                if tt == 1:
                    q_use.append(mmq)

                @block.scalar
                def _(s, h=h, tt=tt, bk_i=bk_i, mmq=mmq):
                    s.wait_ge(tS, mmq)
                    idx = h * 2 + tt
                    if idx >= 2:
                        s.wait_ge(qC, q_stq[idx - 2])
                    s.activation(out=bst[bk_i][:], in_=pb[bk_i][:, :], func=AF.Copy
                                 ).then_inc(sS, 1)
                    s.drain()
                    s.dma_start(out=q_sp.ap()[h, :, tt * 512:(tt + 1) * 512],
                                in_=bst[bk_i][:]).then_inc(qC, 16)
                inc("sS", 1)
                inc("qC", 16)
                q_stq.append(C["qC"])
                bankfree[bk_i] = ("sS", C["sS"])
        Q_DONE = C["qC"]
        if dbg:
            @block.gpsimd
            def _(g):
                g.wait_ge(tS, q_use[-1])
                g.dma_start(out=dbg_out["d_wbig"], in_=wbig[15 % 2][:, 0:2048]).then_inc(qG, 16)
                g.dma_start(out=dbg_out["d_hbf"], in_=h_bf[:, 0:2048]).then_inc(qG, 16)
            inc("qG", 32)

        wk_ld = []
        k_use = []
        k_stq = []
        for sg in range(4):
            @block.gpsimd
            def _(g, sg=sg):
                if sg >= 2:
                    g.wait_ge(tS, k_use[sg - 2])
                else:
                    g.wait_ge(tS, q_use[-1])
                g.dma_start(out=wbig[sg % 2][:].rearrange("p (j s) -> p j s", j=NDC),
                            in_=Wk.rearrange("(j p) s -> p j s", p=128)
                            [:, :, sg * 512:(sg + 1) * 512]).then_inc(qG, 16)
            inc("qG", 16)
            wk_ld.append(C["qG"])

            for tc in range(NCH):
                bk_i = 2 + (sg * NCH + tc) % 2
                @block.tensor
                def _(t, sg=sg, tc=tc, bk_i=bk_i):
                    if tc == 0:
                        t.wait_ge(qG, wk_ld[sg])
                    bwait(t, bk_i)
                    for j in range(NDC):
                        t.matmul(pb[bk_i][:, :],
                                 h_bf[:, j * 1024 + tc * 128: j * 1024 + (tc + 1) * 128],
                                 wbig[sg % 2][:, j * 512:(j + 1) * 512],
                                 start=(j == 0), stop=False)
                    t.matmul(pb[bk_i][:, :], onebf_r[0:1, 0:128],
                             bk_bf[0:1, sg * 512:(sg + 1) * 512], start=False, stop=True)
                    t.drain().then_inc(tS, 1)
                inc("tS", 1)
                mmk = C["tS"]
                if tc == NCH - 1:
                    k_use.append(mmk)

                @block.scalar
                def _(s, bk_i=bk_i, mmk=mmk):
                    s.wait_ge(tS, mmk)
                    s.activation(out=stage2[:, 0:512], in_=pb[bk_i][:, :],
                                 func=AF.Square).then_inc(sS, 1)
                inc("sS", 1)
                sqk = C["sS"]

                @block.vector
                def _(v, sqk=sqk):
                    v.wait_ge(sS, sqk)
                    v.tensor_reduce(out=kn_s[:, 0:4],
                                    in_=stage2[:, 0:512].rearrange("p (h s) -> p h s", h=4),
                                    axis=AX, op=ALU.add)
                    v.drain().then_inc(vS, 1)
                inc("vS", 1)
                rdk = C["vS"]

                @block.scalar
                def _(s, rdk=rdk):
                    s.wait_ge(vS, rdk)
                    s.activation(out=kn_s[:, 4:8], in_=kn_s[:, 0:4], func=AF.Ln,
                                 bias=cEPS24[:, 0:1])
                    s.activation(out=kn_s[:, 0:4], in_=kn_s[:, 4:8], func=AF.Exp,
                                 scale=-0.5).then_inc(sS, 1)
                inc("sS", 1)
                lnk = C["sS"]

                @block.vector
                def _(v, sg=sg, tc=tc, bk_i=bk_i, lnk=lnk):
                    v.wait_ge(sS, lnk)
                    idx = sg * NCH + tc
                    if idx >= 2:
                        v.wait_ge(qC, k_stq[idx - 2])
                    else:
                        v.wait_ge(qC, Q_DONE)
                    for hh in range(4):
                        v.tensor_scalar(out=bst[bk_i - 2][:, hh * 128:(hh + 1) * 128],
                                        in0=pb[bk_i][:, hh * 128:(hh + 1) * 128],
                                        scalar1=kn_s[:, hh:hh + 1], scalar2=0.0,
                                        op0=ALU.mult, op1=ALU.add)
                    v.drain().then_inc(vS, 1)
                inc("vS", 1)
                bankfree[bk_i] = ("vS", C["vS"])
                evk = C["vS"]

                @block.scalar
                def _(s, sg=sg, tc=tc, bk_i=bk_i, evk=evk):
                    s.wait_ge(vS, evk)
                    s.dma_start(out=k_sp.ap()[tc, :, sg * 512:(sg + 1) * 512],
                                in_=bst[bk_i - 2][:]).then_inc(qC, 16)
                inc("qC", 16)
                k_stq.append(C["qC"])
        K_DONE = C["qC"]

        wv_ld = []
        v_use = []
        v_stq = []
        for sg in range(4):
            @block.gpsimd
            def _(g, sg=sg):
                if sg >= 2:
                    g.wait_ge(tS, v_use[sg - 2])
                else:
                    g.wait_ge(tS, k_use[-1])
                g.dma_start(out=wbig[sg % 2][:].rearrange("p (j s) -> p j s", j=NDC),
                            in_=Wv.rearrange("(j p) s -> p j s", p=128)
                            [:, :, sg * 512:(sg + 1) * 512]).then_inc(qG, 16)
            inc("qG", 16)
            wv_ld.append(C["qG"])

            for tc in range(NCH):
                bk_i = 4 + (sg * NCH + tc) % 2
                @block.tensor
                def _(t, sg=sg, tc=tc, bk_i=bk_i):
                    if tc == 0:
                        t.wait_ge(qG, wv_ld[sg])
                    bwait(t, bk_i)
                    for j in range(NDC):
                        t.matmul(pb[bk_i][:, :],
                                 h_bf[:, j * 1024 + tc * 128: j * 1024 + (tc + 1) * 128],
                                 wbig[sg % 2][:, j * 512:(j + 1) * 512],
                                 start=(j == 0), stop=False)
                    t.matmul(pb[bk_i][:, :], onebf_r[0:1, 0:128],
                             bv_bf[0:1, sg * 512:(sg + 1) * 512], start=False, stop=True)
                    t.drain().then_inc(tS, 1)
                inc("tS", 1)
                mmv = C["tS"]
                if tc == NCH - 1:
                    v_use.append(mmv)

                @block.scalar
                def _(s, sg=sg, tc=tc, bk_i=bk_i, mmv=mmv):
                    s.wait_ge(tS, mmv)
                    idx = sg * NCH + tc
                    if idx >= 2:
                        s.wait_ge(qC, v_stq[idx - 2])
                    else:
                        s.wait_ge(qC, K_DONE)
                    s.activation(out=bst[bk_i - 4][:], in_=pb[bk_i][:, :], func=AF.Copy
                                 ).then_inc(sS, 1)
                    s.drain()
                    s.dma_start(out=v_sp.ap()[tc, :, sg * 512:(sg + 1) * 512],
                                in_=bst[bk_i - 4][:]).then_inc(qC, 16)
                inc("sS", 1)
                inc("qC", 16)
                v_stq.append(C["qC"])
                bankfree[bk_i] = ("sS", C["sS"])
        V_DONE = C["qC"]

        ab_done = []
        for tc in range(NCH):
            bk_i = 6 + tc % 2
            @block.tensor
            def _(t, tc=tc, bk_i=bk_i):
                if tc == 0:
                    t.wait_ge(qG, BIAS_LD)
                bwait(t, bk_i)
                for j in range(NDC):
                    t.matmul(pb[bk_i][:, 0:32],
                             h_bf[:, j * 1024 + tc * 128: j * 1024 + (tc + 1) * 128],
                             wab_t[:, j * 32:(j + 1) * 32],
                             start=(j == 0), stop=False)
                t.matmul(pb[bk_i][:, 0:32], onebf_r[0:1, 0:128], bab_bf[0:1, 0:32],
                         start=False, stop=True)
                t.drain().then_inc(tS, 1)
            inc("tS", 1)
            mab = C["tS"]

            @block.vector
            def _(v, tc=tc, bk_i=bk_i, mab=mab):
                v.wait_ge(tS, mab)
                v.tensor_copy(zab[:, tc * 32:(tc + 1) * 32], pb[bk_i][:, 0:32]
                              ).then_inc(vS, 1)
            inc("vS", 1)
            ab_done.append(C["vS"])
            bankfree[bk_i] = ("vS", C["vS"])
        AB_DONE = C["vS"]

        if STOP_AFTER == "proj":
            _bail()
            return nc, dbg_out

        es_w.close()
        es_h.close()

        @block.vector
        def _(v):
            v.wait_ge(vS, AB_DONE)
            v.tensor_copy(albe[:, 0:128].rearrange("p (tc c) -> p tc c", tc=8),
                          zab[:].rearrange("p (tc c) -> p tc c", tc=8)[:, :, 0:16])
            v.tensor_copy(albe[:, 128:256].rearrange("p (tc c) -> p tc c", tc=8),
                          zab[:].rearrange("p (tc c) -> p tc c", tc=8)[:, :, 16:32])
            v.drain().then_inc(vS, 1)
        inc("vS", 1)
        AB_G = C["vS"]

        @block.scalar
        def _(s):
            s.wait_ge(vS, AB_G)
            s.activation(out=cd_all[:], in_=albe[:, 0:128], func=AF.Sigmoid)
            s.activation(out=la_all[:], in_=cd_all[:], func=AF.Ln, bias=cEPS6[:, 0:1])
            s.activation(out=cd_all[:], in_=albe[:, 128:256], func=AF.Abs)
            s.activation(out=nclb[:], in_=cd_all[:], func=AF.Exp, scale=-1.0)
            s.activation(out=cd_all[:], in_=nclb[:], func=AF.Ln, bias=cONE[:, 0:1])
            s.drain().then_inc(sS, 1)
        inc("sS", 1)
        SP_A = C["sS"]

        @block.vector
        def _(v):
            v.wait_ge(sS, SP_A)
            v.scalar_tensor_tensor(out=nclb[:], in0=albe[:, 128:256], scalar=0.0,
                                   in1=cd_all[:], op0=ALU.max, op1=ALU.add)
            v.drain().then_inc(vS, 1)
        inc("vS", 1)
        SP_B = C["vS"]

        @block.scalar
        def _(s):
            s.wait_ge(vS, SP_B)
            s.activation(out=lnb_all[:], in_=nclb[:], func=AF.Ln)
            s.drain().then_inc(sS, 1)
        inc("sS", 1)
        LNB_DONE = C["sS"]

        cla_ev = []
        for tc in range(NCH):
            bk_i = 6 + tc % 2
            @block.tensor
            def _(t, tc=tc, bk_i=bk_i):
                if tc == 0:
                    t.wait_ge(sS, LNB_DONE)
                bwait(t, bk_i)
                t.matmul(pb[bk_i][:, 0:16], triu_f[:], la_all[:, tc * 16:(tc + 1) * 16],
                         start=True, stop=True).then_inc(tS, 1)
            inc("tS", 1)
            mcla = C["tS"]

            @block.vector
            def _(v, tc=tc, bk_i=bk_i, mcla=mcla):
                v.wait_ge(tS, mcla)
                v.tensor_copy(cla_all[:, tc * 16:(tc + 1) * 16], pb[bk_i][:, 0:16]
                              ).then_inc(vS, 1)
            inc("vS", 1)
            cla_ev.append(C["vS"])
            bankfree[bk_i] = ("vS", C["vS"])
        CLA_DONE = C["vS"]

        @block.scalar
        def _(s):
            s.wait_ge(vS, CLA_DONE)
            s.activation(out=cd_all[:], in_=cla_all[:], func=AF.Exp)
            s.drain().then_inc(sS, 1)
        inc("sS", 1)
        CD_DONE = C["sS"]

        @block.vector
        def _(v):
            v.wait_ge(vS, CLA_DONE)
            v.tensor_tensor(out=nclb[:], in0=lnb_all[:], in1=cla_all[:], op=ALU.subtract)
            v.drain().then_inc(vS, 1)
        inc("vS", 1)
        NCLB_DONE = C["vS"]

        trev = []
        for tc in range(NCH):
            bk_i = 6 + tc % 2
            @block.tensor
            def _(t, tc=tc, bk_i=bk_i):
                if tc == 0:
                    t.wait_ge(vS, NCLB_DONE)
                bwait(t, bk_i)
                t.matmul(pb[bk_i][0:16, 0:128], cla_all[:, tc * 16:(tc + 1) * 16], id_f[:],
                         start=True, stop=True)
                t.matmul(pb[bk_i][0:16, 128:256], nclb[:, tc * 16:(tc + 1) * 16], id_f[:],
                         start=True, stop=True)
                t.drain().then_inc(tS, 1)
            inc("tS", 1)
            mtr = C["tS"]

            @block.vector
            def _(v, tc=tc, bk_i=bk_i, mtr=mtr):
                v.wait_ge(tS, mtr)
                v.tensor_copy(claT_s[:, tc * 128:(tc + 1) * 128], pb[bk_i][0:16, 0:128])
                v.tensor_copy(nclbT_s[:, tc * 128:(tc + 1) * 128], pb[bk_i][0:16, 128:256])
                v.drain().then_inc(vS, 1)
            inc("vS", 1)
            trev.append(C["vS"])
            bankfree[bk_i] = ("vS", C["vS"])
        TR_DONE = C["vS"]

        @block.tensor
        def _(t):
            bwait(t, 0)
            t.wait_ge(vS, CLA_DONE)
            t.matmul(pb[0][0:1, 0:128], e127_f[:, 0:1], cla_all[:], start=True, stop=True
                     ).then_inc(tS, 1)
        inc("tS", 1)
        CEND_MM = C["tS"]

        @block.vector
        def _(v):
            v.wait_ge(tS, CEND_MM)
            v.tensor_copy(cend_row[:], pb[0][0:1, 0:128]).then_inc(vS, 1)
            inc("vS", 1)
            v.drain()
            v.tensor_reduce(out=g_pub[:], in_=cend_row[:].rearrange("o (t h) -> o h t", t=8),
                            axis=AX, op=ALU.add)
            v.drain().then_inc(vS, 1)
        inc("vS", 1)
        GPUB = C["vS"]
        bankfree[0] = ("vS", GPUB)

        @block.scalar
        def _(s):
            s.wait_ge(vS, GPUB)
            s.activation(out=rowA[0:1, 0:128], in_=cend_row[:], func=AF.Exp)
            s.drain().then_inc(sS, 1)
        inc("sS", 1)
        ETD = C["sS"]

        @block.tensor
        def _(t):
            t.wait_ge(sS, ETD)
            bwait(t, 1, 2)
            t.matmul(pb[1][:, 0:128], onef[0:1, 0:128], rowA[0:1, 0:128],
                     start=True, stop=True)
            t.matmul(pb[2][:, 0:128], onef[0:1, 0:128], cend_row[0:1, 0:128],
                     start=True, stop=True)
            t.drain().then_inc(tS, 1)
        inc("tS", 1)
        TDB_MM = C["tS"]

        @block.vector
        def _(v):
            v.wait_ge(tS, TDB_MM)
            v.tensor_copy(td_b[:], pb[1][:, 0:128]).then_inc(vS, 1)
            inc("vS", 1)
            v.tensor_tensor(out=stage2[:, 0:128], in0=pb[2][:, 0:128], in1=nclb[:],
                            op=ALU.add)
            v.drain().then_inc(vS, 1)
        inc("vS", 1)
        WARG = C["vS"]
        bankfree[1] = ("vS", WARG)
        bankfree[2] = ("vS", WARG)

        @block.scalar
        def _(s):
            s.wait_ge(vS, WARG)
            s.activation(out=wfac_all[:], in_=stage2[:, 0:128], func=AF.Exp)
            s.drain().then_inc(sS, 1)
        inc("sS", 1)
        WFAC = C["sS"]

        if dbg:
            @block.scalar
            def _(s):
                s.wait_ge(vS, GAGB)
                s.drain()
                s.dma_start(out=dbg_out["d_ga"][:, 0:16], in_=ga_t[:]).then_inc(qC, 16)
                s.dma_start(out=dbg_out["d_ga"][:, 16:32], in_=gb_t[:]).then_inc(qC, 16)
            inc("qC", 32)

            @block.scalar
            def _(s):
                s.wait_ge(sS, WFAC)
                s.dma_start(out=dbg_out["d_la"], in_=la_all[:]).then_inc(qC, 16)
                s.dma_start(out=dbg_out["d_lnb"], in_=lnb_all[:]).then_inc(qC, 16)
                s.dma_start(out=dbg_out["d_cla"], in_=cla_all[:]).then_inc(qC, 16)
                s.dma_start(out=dbg_out["d_wfac"], in_=wfac_all[:]).then_inc(qC, 16)
                s.dma_start(out=dbg_out["d_td"], in_=td_b[:]).then_inc(qC, 16)
                s.dma_start(out=dbg_out["d_cd"], in_=cd_all[:]).then_inc(qC, 16)
            inc("qC", 96)

        if STOP_AFTER == "cla":
            _bail()
            return nc, dbg_out

        es_c = contextlib.ExitStack()
        substacks.append(es_c)
        SBC = lambda n, s, d: es_c.enter_context(nc.sbuf_tensor(n, s, d))
        kt = [SBC("kt0", [128, D], BF16), SBC("kt1", [128, D], BF16)]
        vt = [SBC("vt0", [128, D], BF16), SBC("vt1", [128, D], BF16)]
        qt = [SBC("qt0", [128, D], BF16), SBC("qt1", [128, D], BF16)]
        xt = [SBC("xt0", [128, D], F32), SBC("xt1", [128, D], F32)]
        kwb = SBC("kwb", [128, D], BF16)
        state_s = SBC("state_s", [128, D], F32)
        state_bf = SBC("state_bf", [128, D], BF16)
        kfeat = SBC("kfeat", [128, D], BF16)
        Lscr = [SBC("Lscr0", [128, D], F32), SBC("Lscr1", [128, D], F32)]
        dmb = [SBC("dmb0", [128, 512], F32), SBC("dmb1", [128, 512], F32)]
        eDb = [SBC("eDb0", [128, 512], F32), SBC("eDb1", [128, 512], F32)]
        ysb = [SBC("ysb0", [128, 512], F32), SBC("ysb1", [128, 512], F32)]
        atb = [SBC("atb0", [128, 512], BF16), SBC("atb1", [128, 512], BF16)]

        if STOP_AFTER == "clb":
            _bail()
            return nc, dbg_out

        p1_kld = []
        p1_vld = []
        p1_kw = []
        p1_up = []
        for tc in range(NCH):
            @block.sync
            def _(y, tc=tc, F=(C["tS"], C["vS"], C["sS"], C["qC"])):
                if tc == 0:
                    y.wait_ge(tS, F[0]); y.wait_ge(vS, F[1])
                    y.wait_ge(sS, F[2]); y.wait_ge(qC, F[3])
                if tc >= 2:
                    y.wait_ge(vS, p1_kw[tc - 2])
                y.dma_start(out=kt[tc % 2][:], in_=k_sp.ap()[tc]).then_inc(qS, 16)
                inc("qS", 16)
                p1_kld.append(C["qS"])
                if tc >= 2:
                    y.wait_ge(tS, p1_up[tc - 2][0])
                y.dma_start(out=vt[tc % 2][:], in_=v_sp.ap()[tc]).then_inc(qS, 16)
                inc("qS", 16)
                p1_vld.append(C["qS"])

            @block.vector
            def _(v, tc=tc):
                v.wait_ge(qS, p1_kld[tc])
                if tc == 0:
                    v.wait_ge(sS, WFAC)
                for h in range(NH):
                    v.tensor_scalar(out=kwb[:, h * 128:(h + 1) * 128],
                                    in0=kt[tc % 2][:, h * 128:(h + 1) * 128],
                                    scalar1=wfac_all[:, tc * 16 + h:tc * 16 + h + 1],
                                    scalar2=0.0, op0=ALU.mult, op1=ALU.add)
                v.drain().then_inc(vS, 1)
            inc("vS", 1)
            p1_kw.append(C["vS"])

            mm_marks = []
            for grp in range(4):
                bk_i = grp + 4 * (tc % 2)
                @block.tensor
                def _(t, tc=tc, grp=grp, bk_i=bk_i):
                    t.wait_ge(vS, p1_kw[tc])
                    t.wait_ge(qS, p1_vld[tc])
                    bwait(t, bk_i)
                    for hh in range(4):
                        h = grp * 4 + hh
                        t.matmul(pb[bk_i][:, hh * 128:(hh + 1) * 128],
                                 kwb[:, h * 128:(h + 1) * 128],
                                 vt[tc % 2][:, h * 128:(h + 1) * 128],
                                 start=True, stop=True)
                    t.drain().then_inc(tS, 1)
                inc("tS", 1)
                mm_marks.append(C["tS"])

                @block.vector
                def _(v, tc=tc, grp=grp, bk_i=bk_i, mm=mm_marks[grp]):
                    v.wait_ge(tS, mm)
                    for hh in range(4):
                        h = grp * 4 + hh
                        if tc == 0:
                            v.tensor_copy(state_s[:, h * 128:(h + 1) * 128],
                                          pb[bk_i][:, hh * 128:(hh + 1) * 128])
                        else:
                            v.scalar_tensor_tensor(
                                out=state_s[:, h * 128:(h + 1) * 128],
                                in0=state_s[:, h * 128:(h + 1) * 128],
                                scalar=td_b[:, tc * 16 + h:tc * 16 + h + 1],
                                in1=pb[bk_i][:, hh * 128:(hh + 1) * 128],
                                op0=ALU.mult, op1=ALU.add)
                    v.drain().then_inc(vS, 1)
                inc("vS", 1)
                bankfree[bk_i] = ("vS", C["vS"])
            p1_up.append((mm_marks[-1], C["vS"]))
        P1_DONE = C["vS"]

        if STOP_AFTER == "pass1":
            _bail()
            return nc, dbg_out

        @block.gpsimd
        def _(g):
            g.wait_ge(vS, P1_DONE)
            g.dma_start(out=cc_st_in.ap()[0:128 * D].rearrange("(p c) -> p c", p=128),
                        in_=state_s[:]).then_inc(qG, 16)
            inc("qG", 16)
            g.dma_start(out=cc_st_in.ap()[128 * D:128 * D + 16]
                        .rearrange("(o c) -> o c", o=1), in_=g_pub[:]).then_inc(qG, 16)
            inc("qG", 16)
            g.wait_ge(qG, C["qG"])
            g.collective_compute(
                "AllGather", ALU.bypass, replica_groups=[list(range(NB))],
                ins=[cc_st_in.ap().opt()], outs=[cc_st_out.ap().opt()],
            ).then_inc(cS, 1)
        inc("cS", 1)
        AG3 = C["cS"]

        if STOP_AFTER == "ag3":
            _bail()
            return nc, dbg_out

        @block.sync
        def _(y):
            y.wait_ge(cS, AG3)
            y.dma_start(out=gsb[:],
                        in_=cc_st_out.ap().rearrange("(r x) -> r x", r=NB)
                        [:, 128 * D:128 * D + 16]).then_inc(qS, 16)
        inc("qS", 16)
        GS3 = C["qS"]

        @block.tensor
        def _(t):
            t.wait_ge(qS, GS3)
            bwait(t, 6)
            t.matmul(pb[6][0:8, 0:16], selT_f[:], gsb[:], start=True, stop=True
                     ).then_inc(tS, 1)
        inc("tS", 1)
        SEL_MM = C["tS"]

        @block.scalar
        def _(s):
            s.wait_ge(tS, SEL_MM)
            s.activation(out=coef_sb[:], in_=pb[6][0:8, 0:16], func=AF.Exp
                         ).then_inc(sS, 1)
        inc("sS", 1)
        COEF_E = C["sS"]

        @block.vector
        def _(v):
            v.wait_ge(sS, COEF_E)
            v.tensor_scalar(out=coef_sb[:], in0=coef_sb[:], scalar1=amask_f[0:8, 0:1],
                            scalar2=0.0, op0=ALU.mult, op1=ALU.add)
            v.drain().then_inc(vS, 1)
        inc("vS", 1)
        COEF_M = C["vS"]
        bankfree[6] = ("vS", COEF_M)

        @block.vector
        def _(v):
            v.wait_ge(vS, COEF_M)
            for r in range(NB):
                v.tensor_copy(cmsk[:, r * 16:(r + 1) * 16], coef_sb[:, 0:16])
            v.drain()
            v.tensor_tensor(out=cmsk[:], in0=cmsk[:], in1=bm16_f[:], op=ALU.mult)
            v.drain().then_inc(vS, 1)
        inc("vS", 1)
        CMSK = C["vS"]

        @block.tensor
        def _(t):
            t.wait_ge(vS, CMSK)
            bwait(t, 7)
            t.matmul(pb[7][:, 0:128], onef[0:8, 0:128], cmsk[:], start=True, stop=True
                     ).then_inc(tS, 1)
        inc("tS", 1)
        CB_MM = C["tS"]

        @block.vector
        def _(v):
            v.wait_ge(tS, CB_MM)
            v.tensor_copy(coefB[:], pb[7][:, 0:128]).then_inc(vS, 1)
        inc("vS", 1)
        CB_EV = C["vS"]
        bankfree[7] = ("vS", CB_EV)

        if dbg:
            @block.scalar
            def _(s):
                s.wait_ge(vS, CB_EV)
                s.dma_start(out=dbg_out["d_coefb"], in_=coefB[:]).then_inc(qC, 16)
            inc("qC", 16)

        L_ld = []
        L_cb = []
        for r in range(NB):
            @block.sync
            def _(y, r=r):
                if r == 0:
                    y.wait_ge(cS, AG3)
                if r >= 2:
                    y.wait_ge(vS, L_cb[r - 2])
                y.dma_start(out=Lscr[r % 2][:],
                            in_=cc_st_out.ap()[r * ST_SZ:r * ST_SZ + 128 * D]
                            .rearrange("(p c) -> p c", p=128)).then_inc(qS, 16)
            inc("qS", 16)
            L_ld.append(C["qS"])

            @block.vector
            def _(v, r=r):
                v.wait_ge(qS, L_ld[r])
                if r == 0:
                    v.wait_ge(vS, CB_EV)
                for h in range(NH):
                    if r == 0:
                        v.tensor_scalar(out=state_s[:, h * 128:(h + 1) * 128],
                                        in0=Lscr[0][:, h * 128:(h + 1) * 128],
                                        scalar1=coefB[:, h:h + 1], scalar2=0.0,
                                        op0=ALU.mult, op1=ALU.add)
                    else:
                        v.scalar_tensor_tensor(
                            out=state_s[:, h * 128:(h + 1) * 128],
                            in0=Lscr[r % 2][:, h * 128:(h + 1) * 128],
                            scalar=coefB[:, r * 16 + h:r * 16 + h + 1],
                            in1=state_s[:, h * 128:(h + 1) * 128],
                            op0=ALU.mult, op1=ALU.add)
                v.drain().then_inc(vS, 1)
            inc("vS", 1)
            L_cb.append(C["vS"])
        SENTRY = C["vS"]

        if dbg:
            @block.scalar
            def _(s):
                s.wait_ge(vS, SENTRY)
                s.dma_start(out=dbg_out["d_state"], in_=state_s[:]).then_inc(qC, 16)
            inc("qC", 16)

            @block.gpsimd
            def _(g):
                g.wait_ge(qC, V_DONE)
                g.dma_start(out=dbg_out["d_q"].rearrange("(h s) t -> h s t", h=NH),
                            in_=q_sp.ap()).then_inc(qG, 16)
                g.dma_start(out=dbg_out["d_k"].rearrange("(c t) d -> c t d", c=NCH),
                            in_=k_sp.ap()).then_inc(qG, 16)
                g.dma_start(out=dbg_out["d_v"].rearrange("(c t) d -> c t d", c=NCH),
                            in_=v_sp.ap()).then_inc(qG, 16)
            inc("qG", 48)

        if STOP_AFTER == "sentry":
            _bail()
            return nc, dbg_out

        p2_kld = [None] * NCH
        p2_vld = [None] * NCH
        p2_qld = [None] * NCH
        p2_xld = [None] * NCH
        p2_kwm = [None] * NCH
        p2_ktr_use = [None] * NCH
        p2_mm_last = [None] * NCH
        p2_xst = [None] * NCH
        p2_yg_last = [None] * NCH
        dr1_prev = [None, None]
        for tc in range(NCH):
            @block.sync
            def _(y, tc=tc):
                if tc >= 2:
                    y.wait_ge(vS, p2_ktr_use[tc - 2])
                    y.wait_ge(tS, p2_mm_last[tc - 2])
                y.dma_start(out=kt[tc % 2][:], in_=k_sp.ap()[tc]).then_inc(qS, 16)
                inc("qS", 16)
                p2_kld[tc] = C["qS"]
                y.dma_start(out=vt[tc % 2][:], in_=v_sp.ap()[tc]).then_inc(qS, 16)
                inc("qS", 16)
                p2_vld[tc] = C["qS"]
                y.dma_start(out=qt[tc % 2][:].rearrange("p (h c) -> p h c", h=NH),
                            in_=q_sp.ap()[:, :, tc * 128:(tc + 1) * 128]
                            .rearrange("h s c -> s h c")).then_inc(qS, 16)
                inc("qS", 16)
                p2_qld[tc] = C["qS"]
                if tc >= 2:
                    y.wait_ge(qC, p2_xst[tc - 2])
                if tc == 0:
                    y.wait_ge(qC, XSPILL)
                y.dma_start(out=xt[tc % 2][:].rearrange("p (j c) -> p j c", j=NDC),
                            in_=x_sp.ap().rearrange("(j p) t -> p j t", p=128)
                            [:, :, tc * 128:(tc + 1) * 128]).then_inc(qS, 16)
                inc("qS", 16)
                p2_xld[tc] = C["qS"]

            @block.vector
            def _(v, tc=tc):
                if tc == 0:
                    v.wait_ge(vS, SENTRY)
                if tc >= 1 and p2_yg_last[tc - 1] is not None:
                    v.wait_ge(tS, p2_yg_last[tc - 1])
                v.tensor_copy(state_bf[:], state_s[:]).then_inc(vS, 1)
                inc("vS", 1)
                v.wait_ge(qS, p2_kld[tc])
                if tc >= 1 and p2_mm_last[tc - 1] is not None:
                    v.wait_ge(tS, p2_mm_last[tc - 1])
                for h in range(NH):
                    v.tensor_scalar(out=kwb[:, h * 128:(h + 1) * 128],
                                    in0=kt[tc % 2][:, h * 128:(h + 1) * 128],
                                    scalar1=wfac_all[:, tc * 16 + h:tc * 16 + h + 1],
                                    scalar2=0.0, op0=ALU.mult, op1=ALU.add)
                if dr1_prev[0] is not None:
                    v.wait_ge(tS, max(x for x in dr1_prev if x is not None))
                for rr in range(4):
                    v.tensor_copy(ctrep[:, rr * 128:(rr + 1) * 128],
                                  claT_s[:, tc * 128:(tc + 1) * 128])
                v.drain().then_inc(vS, 1)
            inc("vS", 1)
            p2_kwm[tc] = C["vS"]
            CAST_M = C["vS"] - 1

            for grp in range(4):
                pk = (tc * 4 + grp) % 2
                bA = pk * 4 + 0
                bB = pk * 4 + 1
                bC = pk * 4 + 2
                bD = pk * 4 + 3

                @block.tensor
                def _(t, tc=tc, grp=grp, bA=bA):
                    bwait(t, bA)
                    t.wait_ge(qS, p2_kld[tc])
                    for hh in range(4):
                        h = grp * 4 + hh
                        t.transpose(pb[bA][:].bitcast(BF16)[:, hh * 128:(hh + 1) * 128],
                                    kt[tc % 2][:, h * 128:(h + 1) * 128], id_bf[:])
                    t.drain().then_inc(tS, 1)
                inc("tS", 1)
                T1 = C["tS"]

                @block.vector
                def _(v, grp=grp, bA=bA, T1=T1):
                    v.wait_ge(tS, T1)
                    v.tensor_copy(kfeat[:, grp * 512:(grp + 1) * 512],
                                  pb[bA][:].bitcast(BF16)[:, 0:512]).then_inc(vS, 1)
                inc("vS", 1)
                V1 = C["vS"]

                @block.vector
                def _(v, tc=tc, grp=grp, pk=pk):
                    v.wait_ge(vS, p2_kwm[tc])
                    if dr1_prev[pk] is not None:
                        v.wait_ge(tS, dr1_prev[pk])
                    v.tensor_tensor(out=dr1[pk][:], in0=ctrep[:],
                                    in1=psel_f[0:16, grp * 512:(grp + 1) * 512]
                                    .rearrange("k (h c) -> k h c", h=4)
                                    if False else
                                    psel_f[0:16, grp * 512:(grp + 1) * 512],
                                    op=ALU.mult)
                    v.drain().then_inc(vS, 1)
                inc("vS", 1)
                VR1 = C["vS"]

                @block.tensor
                def _(t, tc=tc, grp=grp, pk=pk, bB=bB, bC=bC, V1=V1, VR1=VR1):
                    bwait(t, bB)
                    t.wait_ge(vS, V1)
                    t.wait_ge(qS, p2_qld[tc])
                    for hh in range(4):
                        h = grp * 4 + hh
                        t.matmul(pb[bB][:, hh * 128:(hh + 1) * 128],
                                 kfeat[:, grp * 512 + hh * 128: grp * 512 + (hh + 1) * 128],
                                 qt[tc % 2][:, h * 128:(h + 1) * 128],
                                 start=True, stop=True)
                    t.drain().then_inc(tS, 1)
                    inc("tS", 1)
                    bwait(t, bC)
                    t.wait_ge(vS, VR1)
                    t.matmul(pb[bC][:, 0:512], onef[0:16, 0:128], dr1[pk][:],
                             start=True, stop=False)
                    t.matmul(pb[bC][:, 0:512], nclbT_s[0:16, tc * 128:(tc + 1) * 128],
                             psel_f[0:16, grp * 512:(grp + 1) * 512],
                             start=False, stop=True)
                    t.drain().then_inc(tS, 1)
                inc("tS", 1)
                T23 = C["tS"]
                dr1_prev[pk] = T23

                @block.vector
                def _(v, pk=pk, bC=bC, T23=T23):
                    v.wait_ge(tS, T23)
                    v.tensor_tensor(out=dmb[pk][:], in0=pb[bC][:, :], in1=m0_f[:],
                                    op=ALU.add).then_inc(vS, 1)
                inc("vS", 1)
                V2 = C["vS"]

                @block.scalar
                def _(s, pk=pk, V2=V2):
                    s.wait_ge(vS, V2)
                    s.activation(out=eDb[pk][:], in_=dmb[pk][:], func=AF.Exp
                                 ).then_inc(sS, 1)
                inc("sS", 1)
                S1 = C["sS"]

                @block.vector
                def _(v, pk=pk, bB=bB, S1=S1):
                    v.wait_ge(sS, S1)
                    v.tensor_tensor(out=atb[pk][:], in0=pb[bB][:, :], in1=eDb[pk][:],
                                    op=ALU.mult).then_inc(vS, 1)
                inc("vS", 1)
                V3 = C["vS"]
                if dbg and tc == 0 and grp == 0:
                    @block.gpsimd
                    def _(g, pk=pk, V3=V3):
                        g.wait_ge(vS, V3)
                        g.dma_start(out=dbg_out["d_dm"], in_=dmb[pk][:]).then_inc(qG, 16)
                        g.dma_start(out=dbg_out["d_eD"], in_=eDb[pk][:]).then_inc(qG, 16)
                        g.dma_start(out=dbg_out["d_at"], in_=atb[pk][:]).then_inc(qG, 16)
                    inc("qG", 48)

                @block.tensor
                def _(t, tc=tc, grp=grp, pk=pk, bA=bA, bB=bB, bC=bC, V3=V3, CAST_M=CAST_M):
                    t.wait_ge(vS, V3)
                    t.wait_ge(qS, p2_vld[tc])
                    for hh in range(4):
                        h = grp * 4 + hh
                        t.matmul(pb[bA][:, hh * 128:(hh + 1) * 128],
                                 atb[pk][:, hh * 128:(hh + 1) * 128],
                                 vt[tc % 2][:, h * 128:(h + 1) * 128],
                                 start=True, stop=True)
                    t.drain().then_inc(tS, 1)
                    inc("tS", 1)
                    t.wait_ge(vS, CAST_M)
                    for hh in range(4):
                        h = grp * 4 + hh
                        t.matmul(pb[bC][:, hh * 128:(hh + 1) * 128],
                                 qt[tc % 2][:, h * 128:(h + 1) * 128],
                                 state_bf[:, h * 128:(h + 1) * 128],
                                 start=True, stop=True)
                    t.drain().then_inc(tS, 1)
                    inc("tS", 1)
                    t.wait_ge(vS, p2_kwm[tc])
                    for hh in range(4):
                        h = grp * 4 + hh
                        t.matmul(pb[bB][:, hh * 128:(hh + 1) * 128],
                                 kwb[:, h * 128:(h + 1) * 128],
                                 vt[tc % 2][:, h * 128:(h + 1) * 128],
                                 start=True, stop=True)
                    t.drain().then_inc(tS, 1)
                inc("tS", 1)
                T456 = C["tS"]
                p2_yg_last[tc] = T456 - 1

                @block.vector
                def _(v, tc=tc, grp=grp, pk=pk, bA=bA, bC=bC, T456=T456):
                    v.wait_ge(tS, T456 - 1)
                    for hh in range(4):
                        h = grp * 4 + hh
                        v.tensor_scalar(out=ysb[pk][:, hh * 128:(hh + 1) * 128],
                                        in0=pb[bC][:, hh * 128:(hh + 1) * 128],
                                        scalar1=cd_all[:, tc * 16 + h:tc * 16 + h + 1],
                                        scalar2=0.0, op0=ALU.mult, op1=ALU.add)
                    for hh in range(4):
                        v.scalar_tensor_tensor(
                            out=ysb[pk][:, hh * 128:(hh + 1) * 128],
                            in0=pb[bA][:, hh * 128:(hh + 1) * 128], scalar=1.0,
                            in1=ysb[pk][:, hh * 128:(hh + 1) * 128],
                            op0=ALU.mult, op1=ALU.add)
                    v.drain().then_inc(vS, 1)
                inc("vS", 1)
                V4 = C["vS"]
                bankfree[bA] = ("vS", V4)
                bankfree[bC] = ("vS", V4)

                if dbg and tc == 0 and grp == 0:
                    @block.gpsimd
                    def _(g, pk=pk, V4=V4):
                        g.wait_ge(vS, V4)
                        g.dma_start(out=dbg_out["d_ysb"], in_=ysb[pk][:]).then_inc(qG, 16)
                    inc("qG", 16)

                @block.tensor
                def _(t, pk=pk, bD=bD, V4=V4):
                    bwait(t, bD)
                    t.wait_ge(vS, V4)
                    for hh in range(4):
                        t.transpose(pb[bD][:, hh * 128:(hh + 1) * 128],
                                    ysb[pk][:, hh * 128:(hh + 1) * 128], id_f[:])
                    t.drain().then_inc(tS, 1)
                inc("tS", 1)
                T7 = C["tS"]

                @block.vector
                def _(v, tc=tc, grp=grp, pk=pk, bB=bB, bD=bD, T7=T7):
                    v.wait_ge(tS, T7)
                    v.wait_ge(qS, p2_xld[tc])
                    for hh in range(4):
                        h = grp * 4 + hh
                        v.scalar_tensor_tensor(
                            out=xt[tc % 2][:, h * 128:(h + 1) * 128],
                            in0=pb[bD][:, hh * 128:(hh + 1) * 128],
                            scalar=ga_t[:, h:h + 1],
                            in1=xt[tc % 2][:, h * 128:(h + 1) * 128],
                            op0=ALU.mult, op1=ALU.add)
                    v.drain().then_inc(vS, 1)
                    inc("vS", 1)
                    for hh in range(4):
                        h = grp * 4 + hh
                        v.scalar_tensor_tensor(
                            out=state_s[:, h * 128:(h + 1) * 128],
                            in0=state_s[:, h * 128:(h + 1) * 128],
                            scalar=td_b[:, tc * 16 + h:tc * 16 + h + 1],
                            in1=pb[bB][:, hh * 128:(hh + 1) * 128],
                            op0=ALU.mult, op1=ALU.add)
                    v.drain().then_inc(vS, 1)
                inc("vS", 1)
                V56 = C["vS"]
                bankfree[bB] = ("vS", V56)
                bankfree[bD] = ("vS", V56 - 1)
                if grp == 3:
                    p2_ktr_use[tc] = p2_kwm[tc]
                    p2_mm_last[tc] = T456

            @block.scalar
            def _(s, tc=tc):
                s.wait_ge(vS, C["vS"] - 1)
                s.dma_start(out=x_sp.ap().rearrange("(j p) t -> p j t", p=128)
                            [:, :, tc * 128:(tc + 1) * 128],
                            in_=xt[tc % 2][:].rearrange("p (j c) -> p j c", j=NDC)
                            ).then_inc(qC, 16)
            inc("qC", 16)
            p2_xst[tc] = C["qC"]
        P2_DONE = C["qC"]
        es_c.close()
        es_p0b.close()
        es_a.close()

        if STOP_AFTER == "pass2":
            _bail()
            return nc, dbg_out

        es_q = contextlib.ExitStack()
        substacks.append(es_q)
        xq_bf = es_q.enter_context(nc.sbuf_tensor("xq_bf", [128, NDC * 1024], BF16))
        r8 = es_q.enter_context(nc.sbuf_tensor("r8", [8, 1024], F32))
        sb2_ball = es_q.enter_context(nc.sbuf_tensor("sb2_ball", [128, 1024], F32))
        fr4 = es_q.enter_context(nc.sbuf_tensor("fr4", [4, 512], F32))
        sr4 = es_q.enter_context(nc.sbuf_tensor("sr4", [4, 512], F32))
        es_l = contextlib.ExitStack()
        substacks.append(es_l)
        x1_f = es_l.enter_context(nc.sbuf_tensor("x1_f", [128, NDC * 1024], F32))
        x1_bf = es_l.enter_context(nc.sbuf_tensor("x1_bf", [128, NDC * 1024], BF16))
        x2c = [es_l.enter_context(nc.sbuf_tensor("x2c%d" % i, [128, 1024], BF16))
               for i in range(2)]
        rowC2 = es_l.enter_context(nc.sbuf_tensor("rowC2", [1, 1024], F32))
        i8rep = es_l.enter_context(nc.sbuf_tensor("i8rep", [8, 1024], F32))

        @block.sync
        def _(y, F=(C["tS"], C["vS"], C["sS"])):
            y.wait_ge(qC, P2_DONE)
            y.wait_ge(tS, F[0]); y.wait_ge(vS, F[1]); y.wait_ge(sS, F[2])
            y.dma_start(out=x1_f[:].rearrange("p (j t) -> p j t", j=NDC),
                        in_=x_sp.ap().rearrange("(j p) t -> p j t", p=128)
                        ).then_inc(qS, 16)
        inc("qS", 16)
        X1LD = C["qS"]

        @block.gpsimd
        def _(g):
            g.wait_ge(qC, P2_DONE)
            g.dma_start(out=x1_bf[:].rearrange("p (j t) -> p j t", j=NDC),
                        in_=x_sp.ap().rearrange("(j p) t -> p j t", p=128)
                        ).then_inc(qG, 16)
        inc("qG", 16)
        X1BLD = C["qG"]

        sq2 = []
        st2 = []
        for j in range(NDC):
            @block.scalar
            def _(s, j=j):
                if j == 0:
                    s.wait_ge(qS, X1LD)
                if j >= 2:
                    s.wait_ge(tS, st2[j - 2])
                s.activation(out=x2c[j % 2][:], in_=x1_f[:, j * 1024:(j + 1) * 1024],
                             func=AF.Square).then_inc(sS, 1)
            inc("sS", 1)
            sq2.append(C["sS"])

            @block.tensor
            def _(t, j=j):
                if j == 0:
                    t.wait_ge(qG, X1BLD)
                    bwait(t, 4, 5, 6, 7)
                t.wait_ge(sS, sq2[j])
                for tt in range(2):
                    t.matmul(pb[4 + tt][0:1, :], onebf_c[:, 0:1],
                             x1_bf[:, j * 1024 + tt * 512: j * 1024 + (tt + 1) * 512],
                             start=(j == 0), stop=(j == NDC - 1))
                    t.matmul(pb[6 + tt][0:1, :], onebf_c[:, 0:1],
                             x2c[j % 2][:, tt * 512:(tt + 1) * 512],
                             start=(j == 0), stop=(j == NDC - 1))
                t.drain().then_inc(tS, 1)
            inc("tS", 1)
            st2.append(C["tS"])
        STATS2 = C["tS"]

        @block.vector
        def _(v):
            v.wait_ge(tS, STATS2)
            v.tensor_scalar(out=rowA[0:1, 0:512], in0=pb[4][0:1, :], scalar1=1.0 / 2048.0,
                            scalar2=0.0, op0=ALU.mult, op1=ALU.add)
            v.tensor_scalar(out=rowA[0:1, 512:1024], in0=pb[5][0:1, :], scalar1=1.0 / 2048.0,
                            scalar2=0.0, op0=ALU.mult, op1=ALU.add)
            v.tensor_scalar(out=rowB[0:1, 0:512], in0=pb[6][0:1, :], scalar1=1.0 / 2048.0,
                            scalar2=0.0, op0=ALU.mult, op1=ALU.add)
            v.tensor_scalar(out=rowB[0:1, 512:1024], in0=pb[7][0:1, :], scalar1=1.0 / 2048.0,
                            scalar2=0.0, op0=ALU.mult, op1=ALU.add)
            v.drain()
            v.tensor_tensor(out=rowC2[:], in0=rowA[:], in1=rowA[:], op=ALU.mult)
            v.drain()
            v.tensor_tensor(out=rowB[:], in0=rowB[:], in1=rowC2[:], op=ALU.subtract)
            v.drain().then_inc(vS, 1)
        inc("vS", 1)
        VAR2 = C["vS"]
        for b_ in (4, 5, 6, 7):
            bankfree[b_] = ("vS", VAR2)

        @block.scalar
        def _(s):
            s.wait_ge(vS, VAR2)
            s.activation(out=rowC2[:], in_=rowB[:], func=AF.Ln, bias=cEPS5[0:1, 0:1])
            s.activation(out=rowB[:], in_=rowC2[:], func=AF.Exp, scale=-0.5)
            s.drain().then_inc(sS, 1)
        inc("sS", 1)
        ISTD2 = C["sS"]

        @block.tensor
        def _(t):
            t.wait_ge(sS, ISTD2)
            bwait(t, 0, 1, 2, 3)
            for tt in range(2):
                t.matmul(pb[0 + tt][:, :], onef[0:1, 0:128],
                         rowA[0:1, tt * 512:(tt + 1) * 512], start=True, stop=True)
                t.matmul(pb[2 + tt][:, :], onef[0:1, 0:128],
                         rowB[0:1, tt * 512:(tt + 1) * 512], start=True, stop=True)
            t.drain().then_inc(tS, 1)
        inc("tS", 1)
        BC2 = C["tS"]

        @block.vector
        def _(v):
            v.wait_ge(tS, BC2)
            v.tensor_copy(stage0[:, 0:512], pb[0][:, :])
            v.tensor_copy(stage0[:, 512:1024], pb[1][:, :])
            v.tensor_copy(stage1[:, 0:512], pb[2][:, :])
            v.tensor_copy(stage1[:, 512:1024], pb[3][:, :])
            v.drain().then_inc(vS, 1)
        inc("vS", 1)
        MB2 = C["vS"]
        for b_ in (0, 1, 2, 3):
            bankfree[b_] = ("vS", MB2)

        for j in range(NDC):
            @block.vector
            def _(v, j=j):
                if j == 0:
                    v.wait_ge(vS, MB2)
                cs = slice(j * 1024, (j + 1) * 1024)
                v.tensor_tensor(out=stage2[:], in0=x1_f[:, cs], in1=stage0[:],
                                op=ALU.subtract)
                v.tensor_tensor(out=stage2[:], in0=stage2[:], in1=stage1[:], op=ALU.mult)
                v.tensor_scalar(out=x1_f[:, cs], in0=stage2[:],
                                scalar1=cbmod[:, j:j + 1], scalar2=mymod[:, 48 + j:49 + j],
                                op0=ALU.mult, op1=ALU.add)
                v.drain().then_inc(vS, 1)
            inc("vS", 1)
        H2_DONE = C["vS"]

        mxacc = []
        for tsub in range(NCH):
            for jg in range(4):
                bk_i = jg % 2
                @block.tensor
                def _(t, tsub=tsub, jg=jg, bk_i=bk_i):
                    if tsub == 0 and jg == 0:
                        t.wait_ge(vS, H2_DONE)
                    bwait(t, bk_i)
                    for i in range(4):
                        j = jg * 4 + i
                        t.transpose(pb[bk_i][:, i * 128:(i + 1) * 128],
                                    x1_f[:, j * 1024 + tsub * 128: j * 1024 + (tsub + 1) * 128],
                                    id_f[:])
                    t.drain().then_inc(tS, 1)
                inc("tS", 1)
                mtr2 = C["tS"]

                @block.vector
                def _(v, tsub=tsub, jg=jg, bk_i=bk_i, mtr2=mtr2):
                    v.wait_ge(tS, mtr2)
                    v.tensor_reduce(out=gtmp[:, 8:9], in_=pb[bk_i][:, :], axis=AX,
                                    op=ALU.max, apply_absolute_value=True)
                    v.drain()
                    if jg == 0:
                        v.tensor_copy(gtmp[:, tsub:tsub + 1], gtmp[:, 8:9])
                    else:
                        v.tensor_tensor(out=gtmp[:, tsub:tsub + 1],
                                        in0=gtmp[:, tsub:tsub + 1], in1=gtmp[:, 8:9],
                                        op=ALU.max)
                    v.drain().then_inc(vS, 1)
                inc("vS", 1)
                bankfree[bk_i] = ("vS", C["vS"])
        MX_DONE = C["vS"]

        if dbg:
            @block.scalar
            def _(s):
                s.wait_ge(vS, MX_DONE)
                s.dma_start(out=dbg_out["d_mx"], in_=gtmp[:, 0:8]).then_inc(qC, 16)
            inc("qC", 16)

        @block.vector
        def _(v):
            v.wait_ge(vS, MX_DONE)
            v.tensor_scalar(out=wsmall[:, 0:8], in0=gtmp[:, 0:8], scalar1=1.0 / 127.0,
                            scalar2=1e-5, op0=ALU.mult, op1=ALU.add)
            v.drain().then_inc(vS, 1)
        inc("vS", 1)
        S2COL = C["vS"]

        @block.tensor
        def _(t):
            t.wait_ge(vS, S2COL)
            bwait(t, 2)
            t.matmul(pb[2][0:8, 0:128], wsmall[:, 0:8], id_f[:], start=True, stop=True
                     ).then_inc(tS, 1)
        inc("tS", 1)
        S2T_MM = C["tS"]

        @block.vector
        def _(v):
            v.wait_ge(tS, S2T_MM)
            v.tensor_copy(r8[:, 0:128], pb[2][0:8, 0:128]).then_inc(vS, 1)
            inc("vS", 1)
            v.drain()
            v.reciprocal(r8[:, 128:256], r8[:, 0:128])
            v.drain().then_inc(vS, 1)
        inc("vS", 1)
        INV2 = C["vS"]
        bankfree[2] = ("vS", INV2)

        if dbg:
            @block.scalar
            def _(s):
                s.wait_ge(vS, INV2)
                s.dma_start(out=dbg_out["d_s2"], in_=r8[0:8, 0:128]).then_inc(qC, 16)
            inc("qC", 16)

        @block.vector
        def _(v):
            v.wait_ge(vS, INV2)
            for rr in range(8):
                v.tensor_copy(i8rep[:, rr * 128:(rr + 1) * 128], r8[:, 128:256])
            v.drain()
            v.tensor_tensor(out=i8rep[:], in0=i8rep[:], in1=pos8_f[:], op=ALU.mult)
            v.drain().then_inc(vS, 1)
        inc("vS", 1)
        I8R = C["vS"]

        @block.tensor
        def _(t):
            t.wait_ge(vS, I8R)
            bwait(t, 0, 1)
            t.matmul(pb[0][:, :], onef[0:8, 0:128], i8rep[:, 0:512], start=True, stop=True)
            t.matmul(pb[1][:, :], onef[0:8, 0:128], i8rep[:, 512:1024], start=True, stop=True)
            t.drain().then_inc(tS, 1)
        inc("tS", 1)
        I2B_MM = C["tS"]

        @block.vector
        def _(v):
            v.wait_ge(tS, I2B_MM)
            v.tensor_copy(stage0[:, 0:512], pb[0][:, :])
            v.tensor_copy(stage0[:, 512:1024], pb[1][:, :])
            for rr in range(8):
                v.tensor_copy(i8rep[:, rr * 128:(rr + 1) * 128], r8[:, 0:128])
            v.drain()
            v.tensor_tensor(out=i8rep[:], in0=i8rep[:], in1=pos8_f[:], op=ALU.mult)
            v.drain().then_inc(vS, 1)
        inc("vS", 1)
        I2B = C["vS"]
        bankfree[0] = ("vS", I2B)
        bankfree[1] = ("vS", I2B)

        @block.tensor
        def _(t):
            t.wait_ge(vS, I2B)
            bwait(t, 2, 3)
            t.matmul(pb[2][:, :], onef[0:8, 0:128], i8rep[:, 0:512], start=True, stop=True)
            t.matmul(pb[3][:, :], onef[0:8, 0:128], i8rep[:, 512:1024], start=True, stop=True)
            t.drain().then_inc(tS, 1)
        inc("tS", 1)
        S2B_MM = C["tS"]

        @block.vector
        def _(v):
            v.wait_ge(tS, S2B_MM)
            v.tensor_copy(sb2_ball[:, 0:512], pb[2][:, :])
            v.tensor_copy(sb2_ball[:, 512:1024], pb[3][:, :])
            v.drain().then_inc(vS, 1)
        inc("vS", 1)
        S2BALL = C["vS"]
        bankfree[2] = ("vS", S2BALL)
        bankfree[3] = ("vS", S2BALL)

        for j in range(NDC):
            @block.vector
            def _(v, j=j):
                if j == 0:
                    v.wait_ge(vS, I2B)
                cs = slice(j * 1024, (j + 1) * 1024)
                v.tensor_tensor(out=stage2[:], in0=x1_f[:, cs], in1=stage0[:], op=ALU.mult)
                v.tensor_scalar(out=xq_bf[:, cs], in0=stage2[:], scalar1=MAGIC,
                                scalar2=MAGIC, op0=ALU.add, op1=ALU.subtract)
                v.drain().then_inc(vS, 1)
            inc("vS", 1)
        XQ_DONE = C["vS"]

        if dbg:
            @block.gpsimd
            def _(g):
                g.wait_ge(qC, P2_DONE)
                g.dma_start(out=dbg_out["d_x1"], in_=x_sp.ap()).then_inc(qG, 16)
            inc("qG", 16)

            @block.gpsimd
            def _(g):
                g.wait_ge(vS, XQ_DONE)
                g.dma_start(out=dbg_out["d_xq"].rearrange("(j p) t -> p j t", p=128),
                            in_=xq_bf[:].rearrange("p (j t) -> p j t", j=NDC)).then_inc(qG, 16)
            inc("qG", 16)
        es_l.close()

        if STOP_AFTER == "ln2":
            _bail()
            return nc, dbg_out

        es_f1 = contextlib.ExitStack()
        substacks.append(es_f1)
        gu0 = es_f1.enter_context(nc.sbuf_tensor("gu0", [128, 64 * 512], FP16))
        guq = gu0[:].bitcast(BF16)

        for th in range(2):
            es_f2 = contextlib.ExitStack()
            substacks.append(es_f2)
            wgb = [[es_f2.enter_context(nc.sbuf_tensor("wgb%d%d_%d" % (i, mtx, th),
                                                       [128, 2048], BF16))
                    for mtx in range(2)] for i in range(2)]
            g2buf = es_f2.enter_context(nc.sbuf_tensor("g2buf_%d" % th, [128, 512], F32))

            @block.vector
            def _(v, th=th):
                if th == 0:
                    v.wait_ge(vS, XQ_DONE)
                v.tensor_scalar(out=stage1[:, 0:512],
                                in0=sb2_ball[:, th * 512:(th + 1) * 512],
                                scalar1=SCAL[:, 0:1], scalar2=0.0,
                                op0=ALU.mult, op1=ALU.add)
                v.tensor_scalar(out=stage0[:, 0:512],
                                in0=sb2_ball[:, th * 512:(th + 1) * 512],
                                scalar1=SCAL[:, 1:2], scalar2=0.0,
                                op0=ALU.mult, op1=ALU.add)
                v.drain().then_inc(vS, 1)
            inc("vS", 1)
            S2GB = C["vS"]

            @block.gpsimd
            def _(g, th=th):
                if th == 0:
                    g.wait_ge(cS, AG2)
                g.sem_inc(gS, 1)
            inc("gS", 1)
            AG2W = C["gS"]

            wg_ld = [None] * 64
            chain_m = [None] * 64
            for ic in range(64):
                r = ic // 8
                cb = (ic % 8) * 128

                @block.sync
                def _(y, ic=ic, r=r, cb=cb, th=th,
                      F=(C["tS"], C["vS"], C["sS"], C["qC"])):
                    if ic == 0:
                        y.wait_ge(tS, F[0]); y.wait_ge(vS, F[1])
                        y.wait_ge(sS, F[2]); y.wait_ge(qC, F[3])
                        y.wait_ge(cS, AG2)
                    if ic >= 2 and chain_m[ic - 2] is not None:
                        y.wait_ge(tS, chain_m[ic - 2][0])
                    for mtx in range(2):
                        base = r * WQIN_SZ + mtx * WG_SZ
                        src = (cc_wq_out.ap()[base:base + WG_SZ]
                               .rearrange("(j p i) -> p j i", j=NDC, p=128)
                               [:, :, cb:cb + 128])
                        y.dma_start(out=wgb[ic % 2][mtx][:]
                                    .rearrange("p (j c) -> p j c", j=NDC),
                                    in_=src).then_inc(qS, 16)
                        inc("qS", 16)
                wg_ld[ic] = C["qS"]

                bset = (ic % 2) * 2 + 4
                @block.tensor
                def _(t, ic=ic, th=th, bset=bset):
                    t.wait_ge(qS, wg_ld[ic])
                    if ic == 0:
                        t.wait_ge(vS, S2GB)
                        t.wait_ge(gS, AG2W)
                    bwait(t, bset, bset + 1)
                    for mtx in range(2):
                        for j in range(NDC):
                            t.matmul(pb[bset + mtx][:, :],
                                     wgb[ic % 2][mtx][:, j * 128:(j + 1) * 128],
                                     xq_bf[:, j * 1024 + th * 512: j * 1024 + (th + 1) * 512],
                                     start=(j == 0), stop=(j == NDC - 1))
                    t.drain().then_inc(tS, 1)
                inc("tS", 1)
                GU_MM = C["tS"]

                @block.vector
                def _(v, ic=ic, bset=bset, GU_MM=GU_MM):
                    v.wait_ge(tS, GU_MM)
                    v.tensor_tensor(out=stage2[:, 0:512], in0=pb[bset][:, :],
                                    in1=stage1[:, 0:512], op=ALU.mult).then_inc(vS, 1)
                inc("vS", 1)
                G1 = C["vS"]

                @block.scalar
                def _(s, G1=G1):
                    s.wait_ge(vS, G1)
                    s.activation(out=g2buf[:], in_=stage2[:, 0:512], func=AF.Silu
                                 ).then_inc(sS, 1)
                inc("sS", 1)
                SIL = C["sS"]

                @block.vector
                def _(v, ic=ic, bset=bset, SIL=SIL):
                    v.wait_ge(sS, SIL)
                    v.tensor_tensor(out=stage2[:, 512:1024], in0=pb[bset + 1][:, :],
                                    in1=stage0[:, 0:512], op=ALU.mult)
                    v.tensor_tensor(out=gu0[:, ic * 512:(ic + 1) * 512], in0=g2buf[:],
                                    in1=stage2[:, 512:1024], op=ALU.mult).then_inc(vS, 1)
                inc("vS", 1)
                GU0 = C["vS"]
                bankfree[bset] = ("vS", GU0)
                bankfree[bset + 1] = ("vS", GU0)

                @block.scalar
                def _(s, ic=ic, GU0=GU0):
                    s.wait_ge(vS, GU0)
                    s.activation(out=g2buf[:], in_=gu0[:, ic * 512:(ic + 1) * 512],
                                 func=AF.Abs).then_inc(sS, 1)
                inc("sS", 1)
                ABSM = C["sS"]

                @block.vector
                def _(v, ic=ic, ABSM=ABSM):
                    v.wait_ge(sS, ABSM)
                    if ic == 0:
                        v.tensor_copy(stage1[:, 512:1024], g2buf[:])
                    else:
                        v.tensor_tensor(out=stage1[:, 512:1024], in0=stage1[:, 512:1024],
                                        in1=g2buf[:], op=ALU.max)
                    v.drain().then_inc(vS, 1)
                inc("vS", 1)
                chain_m[ic] = (GU_MM, C["vS"])
            GUP_DONE = C["vS"]

            mx_ev = []
            for tb in range(4):
                bk_i = tb % 2
                @block.tensor
                def _(t, tb=tb, bk_i=bk_i):
                    if tb == 0:
                        t.wait_ge(vS, GUP_DONE)
                    bwait(t, bk_i)
                    t.transpose(pb[bk_i][:, 0:128],
                                stage1[:, 512 + tb * 128: 512 + (tb + 1) * 128], id_f[:]
                                ).then_inc(tS, 1)
                inc("tS", 1)
                mtb = C["tS"]

                @block.vector
                def _(v, tb=tb, bk_i=bk_i, mtb=mtb):
                    v.wait_ge(tS, mtb)
                    v.tensor_reduce(out=wsmall[:, 8 + tb:9 + tb] if False else gtmp[:, 16 + tb:17 + tb],
                                    in_=pb[bk_i][:, 0:128], axis=AX, op=ALU.max)
                    v.drain().then_inc(vS, 1)
                inc("vS", 1)
                mx_ev.append(C["vS"])
                bankfree[bk_i] = ("vS", C["vS"])
            MX4 = C["vS"]

            @block.vector
            def _(v, th=th):
                v.wait_ge(vS, MX4)
                v.tensor_scalar(out=gtmp[:, 24:28], in0=gtmp[:, 16:20],
                                scalar1=1.0 / 127.0, scalar2=1e-5,
                                op0=ALU.mult, op1=ALU.add)
                v.drain()
                v.reciprocal(gtmp[:, 16:20], gtmp[:, 24:28])
                v.drain().then_inc(vS, 1)
            inc("vS", 1)
            F4COL = C["vS"]

            @block.tensor
            def _(t):
                t.wait_ge(vS, F4COL)
                bwait(t, 2)
                t.matmul(pb[2][0:4, 0:128], gtmp[:, 16:20], id_f[:], start=True, stop=True)
                t.matmul(pb[2][0:4, 128:256], gtmp[:, 24:28], id_f[:], start=True, stop=True)
                t.drain().then_inc(tS, 1)
            inc("tS", 1)
            F4T_MM = C["tS"]

            @block.vector
            def _(v, F4T_MM=F4T_MM):
                v.wait_ge(tS, F4T_MM)
                v.tensor_copy(r8[0:4, 768:896], pb[2][0:4, 0:128])
                v.tensor_copy(r8[0:4, 512:640], pb[2][0:4, 128:256])
                v.drain().then_inc(vS, 1)
            inc("vS", 1)
            F4ROWS = C["vS"]
            bankfree[2] = ("vS", F4ROWS)

            if dbg and th == 0:
                @block.scalar
                def _(s):
                    s.wait_ge(vS, F4ROWS)
                    s.dma_start(out=dbg_out["d_f4"][0:4, :], in_=r8[0:4, 768:896]
                                ).then_inc(qC, 16)
                inc("qC", 16)

            @block.vector
            def _(v):
                v.wait_ge(vS, F4ROWS)
                for rr in range(4):
                    v.tensor_copy(fr4[:, rr * 128:(rr + 1) * 128], r8[0:4, 768:896])
                    v.tensor_copy(sr4[:, rr * 128:(rr + 1) * 128], r8[0:4, 512:640])
                v.drain()
                v.tensor_tensor(out=fr4[:], in0=fr4[:], in1=pos8_f[0:4, 0:512], op=ALU.mult)
                v.tensor_tensor(out=sr4[:], in0=sr4[:], in1=pos8_f[0:4, 0:512], op=ALU.mult)
                v.drain().then_inc(vS, 1)
            inc("vS", 1)
            FR4M = C["vS"]

            @block.tensor
            def _(t, FR4M=FR4M):
                t.wait_ge(vS, FR4M)
                bwait(t, 0, 1)
                t.matmul(pb[0][:, :], onef[0:4, 0:128], fr4[:], start=True, stop=True)
                t.matmul(pb[1][:, :], onef[0:4, 0:128], sr4[:], start=True, stop=True)
                t.drain().then_inc(tS, 1)
            inc("tS", 1)
            F4B_MM = C["tS"]

            @block.vector
            def _(v, F4B_MM=F4B_MM):
                v.wait_ge(tS, F4B_MM)
                v.tensor_copy(stage0[:, 0:512], pb[0][:, :])
                v.tensor_scalar(out=stage1[:, 0:512], in0=pb[1][:, :],
                                scalar1=SCAL[:, 2:3], scalar2=0.0,
                                op0=ALU.mult, op1=ALU.add)
                v.drain().then_inc(vS, 1)
            inc("vS", 1)
            F4B = C["vS"]
            bankfree[0] = ("vS", F4B)
            bankfree[1] = ("vS", F4B)

            rnd_m = []
            for ic in range(64):
                @block.vector
                def _(v, ic=ic):
                    if ic == 0:
                        v.wait_ge(vS, F4B)
                    cs = slice(ic * 512, (ic + 1) * 512)
                    v.tensor_tensor(out=stage2[:, 0:512], in0=gu0[:, cs],
                                    in1=stage0[:, 0:512], op=ALU.mult)
                    v.tensor_scalar(out=guq[:, cs], in0=stage2[:, 0:512], scalar1=MAGIC,
                                    scalar2=MAGIC, op0=ALU.add, op1=ALU.subtract)
                    v.drain().then_inc(vS, 1)
                inc("vS", 1)
                rnd_m.append(C["vS"])
            RND_DONE = C["vS"]

            es_f2.close()

            es_f3 = contextlib.ExitStack()
            substacks.append(es_f3)
            wdb = [es_f3.enter_context(nc.sbuf_tensor("wdb%d_%d" % (i, th), [128, 8192], BF16))
                   for i in range(2)]
            xj = [es_f3.enter_context(nc.sbuf_tensor("xj%d_%d" % (i, th), [128, 512], F32))
                  for i in range(2)]
            outst = [es_f3.enter_context(nc.sbuf_tensor("outst%d_%d" % (i, th), [128, 512], F32))
                     for i in range(2)]

            wd_ld = [None] * NDC
            dn_ev = [None] * NDC
            dn_mm = [None] * NDC
            dn_st = [None] * NDC
            for j in range(NDC):
                @block.sync
                def _(y, j=j, th=th, F=(C["tS"], C["vS"], C["sS"])):
                    if j == 0:
                        y.wait_ge(tS, F[0]); y.wait_ge(vS, F[1]); y.wait_ge(sS, F[2])
                    if j >= 2:
                        y.wait_ge(tS, dn_mm[j - 2])
                    for r in range(NB):
                        base = r * WQIN_SZ + 2 * WG_SZ + j * 131072
                        y.dma_start(
                            out=wdb[j % 2][:, r * 1024:(r + 1) * 1024]
                            .rearrange("p (il c) -> p il c", il=8),
                            in_=cc_wq_out.ap()[base:base + 131072]
                            .rearrange("(il p c) -> p il c", il=8, p=128)
                            ).then_inc(qS, 16)
                        inc("qS", 16)
                    wd_ld[j] = C["qS"]
                    if j >= 2:
                        y.wait_ge(vS, dn_ev[j - 2])
                    y.dma_start(out=xj[j % 2][:],
                                in_=x_sp.ap()[j * 128:(j + 1) * 128,
                                              th * 512:(th + 1) * 512]).then_inc(qS, 16)
                    inc("qS", 16)

                XJLD = C["qS"]
                bk_i = 2 + j % 2

                @block.tensor
                def _(t, j=j, th=th, bk_i=bk_i):
                    t.wait_ge(qS, wd_ld[j])
                    if j == 0:
                        t.wait_ge(vS, RND_DONE)
                    bwait(t, bk_i)
                    for ic in range(64):
                        t.matmul(pb[bk_i][:, :], wdb[j % 2][:, ic * 128:(ic + 1) * 128],
                                 guq[:, ic * 512:(ic + 1) * 512],
                                 start=(ic == 0), stop=(ic == 63))
                    t.drain().then_inc(tS, 1)
                inc("tS", 1)
                dn_mm[j] = C["tS"]

                @block.vector
                def _(v, j=j, bk_i=bk_i, XJLD=XJLD):
                    v.wait_ge(tS, dn_mm[j])
                    v.wait_ge(qS, XJLD)
                    if j >= 2:
                        v.wait_ge(qC, dn_st[j - 2])
                    v.tensor_tensor(out=outst[j % 2][:], in0=pb[bk_i][:, :],
                                    in1=stage1[:, 0:512], op=ALU.mult)
                    v.scalar_tensor_tensor(out=outst[j % 2][:], in0=outst[j % 2][:],
                                           scalar=gb_t[:, j:j + 1], in1=xj[j % 2][:],
                                           op0=ALU.mult, op1=ALU.add)
                    v.drain().then_inc(vS, 1)
                inc("vS", 1)
                dn_ev[j] = C["vS"]
                bankfree[bk_i] = ("vS", dn_ev[j])

                @block.scalar
                def _(s, j=j, th=th):
                    s.wait_ge(vS, dn_ev[j])
                    s.dma_start(out=out[j * 128:(j + 1) * 128, th * 512:(th + 1) * 512],
                                in_=outst[j % 2][:]).then_inc(qC, 16)
                inc("qC", 16)
                dn_st[j] = C["qC"]
            es_f3.close()
        es_f1.close()
        es_q.close()

        @block.sync
        def _(y):
            y.wait_ge(qC, C["qC"])
            y.wait_ge(qS, C["qS"])
            y.wait_ge(qG, C["qG"])

    return nc, dbg_out


_NC_CACHE = {}


def _get_nc(dbg=False):
    if dbg not in _NC_CACHE:
        _NC_CACHE[dbg] = build(dbg)
    return _NC_CACHE[dbg]


def _host_inputs(L0_attn_x, L0_attn_Wq, L0_attn_bq, L0_attn_Wk, L0_attn_bk,
                 L0_attn_Wv, L0_attn_bv, L0_attn_Wbeta, L0_attn_bbeta,
                 L0_attn_Walpha, L0_attn_balpha, L0_ffn_w_gate, L0_ffn_w_up,
                 L0_ffn_w_down, L0_c, L0_wmod):
    f32 = lambda a: np.ascontiguousarray(np.asarray(a, dtype=np.float32))
    x = f32(L0_attn_x)
    Wq = f32(L0_attn_Wq); Wk = f32(L0_attn_Wk); Wv = f32(L0_attn_Wv)
    bq = f32(L0_attn_bq).reshape(1, D)
    bk = f32(L0_attn_bk).reshape(1, D)
    bv = f32(L0_attn_bv).reshape(1, D)
    Wab = f32(np.concatenate([np.asarray(L0_attn_Walpha), np.asarray(L0_attn_Wbeta)], axis=1))
    bab = f32(np.concatenate([np.asarray(L0_attn_balpha).ravel(),
                              np.asarray(L0_attn_bbeta).ravel()])).reshape(1, 32)
    wg = f32(L0_ffn_w_gate); wu = f32(L0_ffn_w_up); wd = f32(L0_ffn_w_down)
    c = f32(L0_c); wmod = f32(L0_wmod)

    ident = np.eye(128, dtype=np.float32)
    ones_full = np.ones((128, 128), dtype=np.float32)
    triu_np = np.triu(np.ones((128, 128), dtype=np.float32))
    ee = np.arange(128)
    m0_1 = np.where(ee[None, :] >= ee[:, None], 0.0, -1e30).astype(np.float32)
    m0c = np.ascontiguousarray(np.tile(m0_1, (1, 4)))
    e127_np = np.zeros((128, 1), dtype=np.float32); e127_np[127, 0] = 1.0
    kk = np.arange(16)
    posisel_np = (kk[:, None] == (np.arange(2048) // 128)[None, :]).astype(np.float32)
    k8 = np.arange(8)
    pos8_np = (k8[:, None] == (np.arange(1024) // 128)[None, :]).astype(np.float32)
    bm16_np = (k8[:, None] == (np.arange(128) // 16)[None, :]).astype(np.float32)
    cT32 = np.ascontiguousarray(
        c.T.reshape(16, 128, 2).transpose(1, 0, 2).reshape(128, 32))

    in_maps = []
    for m in range(NB):
        b = m // 4
        t0 = (m % 4) * TC
        grp0 = (m // 4) * 4
        bsel_np = np.zeros((128, 2), dtype=np.float32)
        bsel_np[:, 0] = 1.0 if b == 0 else 0.0
        bsel_np[:, 1] = 1.0 if b == 1 else 0.0
        selT_np = np.zeros((8, 8), dtype=np.float32)
        amask_np = np.zeros((8, 1), dtype=np.float32)
        for r in range(8):
            if grp0 <= r < m:
                amask_np[r, 0] = 1.0
                for rp in range(8):
                    if r < rp < m and grp0 <= rp:
                        selT_np[rp, r] = 1.0
        d = dict(
            xT=np.ascontiguousarray(x[b, t0:t0 + TC, :].T),
            Wq=Wq, Wk=Wk, Wv=Wv, bq=bq, bk=bk, bv=bv, Wab=Wab, bab=bab,
            wmod_sl=np.ascontiguousarray(wmod[:, m * 1536:(m + 1) * 1536]),
            cT32=cT32,
            wgT_sl=np.ascontiguousarray(wg[m * ISL:(m + 1) * ISL, :].T),
            wuT_sl=np.ascontiguousarray(wu[m * ISL:(m + 1) * ISL, :].T),
            wdT_sl=np.ascontiguousarray(wd[:, m * ISL:(m + 1) * ISL].T),
            ident=ident, ones_full=ones_full, triu=triu_np, m0c=m0c,
            e127=e127_np, bsel=bsel_np, selT=selT_np, amask=amask_np,
            posisel=posisel_np, pos8=pos8_np, bm16=bm16_np,
        )
        in_maps.append(d)
    return in_maps


def kernel(**inputs):
    in_maps = _host_inputs(**inputs)
    nc, _ = _get_nc(DEBUG)
    res = run_bass_kernel_spmd(nc, in_maps, core_ids=list(range(NB)))
    full = np.empty((2, 4096, D), dtype=np.float32)
    for m in range(NB):
        b = m // 4
        t0 = (m % 4) * TC
        full[b, t0:t0 + TC, :] = res.results[m]["out"].T
    if DEBUG:
        kernel.last_results = res.results
    return full
